# version 22
# baseline (speedup 1.0000x reference)
import numpy as np
import ml_dtypes
from contextlib import ExitStack

import concourse.bass as bass
import concourse.mybir as mybir
import concourse.bass_utils as _bu
from concourse.bass_utils import run_bass_kernel_spmd

_orig_run_command = _bu.run_command


def _run_command_ldwopt(cmd, *a, **kw):
    if isinstance(cmd, list):
        cmd = ["--enable-ldw-opt=true" if c == "--enable-ldw-opt=false" else c
               for c in cmd]
    return _orig_run_command(cmd, *a, **kw)


_bu.run_command = _run_command_ldwopt

F32 = mybir.dt.float32
BF16 = mybir.dt.bfloat16
I32 = mybir.dt.int32
ALU = mybir.AluOpType
AF = mybir.ActivationFunctionType
AX = mybir.AxisListType

NCORES = 8
CFG_FULL = dict(B=4, T=2048, D=1024, H=8, dh=256, L=64, I=3072)


class Ctr:
    def __init__(self):
        self.v = 0

    def inc(self, n=1):
        self.v += n
        return self.v


class Chan:

    def __init__(self, sem):
        self.sem = sem
        self.c = Ctr()

    def dma(self, q, out, in_):
        q.dma_start(out=out, in_=in_).then_inc(self.sem, 16)
        self.c.inc(16)
        return self.c.v

    @property
    def v(self):
        return self.c.v

    def wait(self, eng, v=None):
        eng.wait_ge(self.sem, self.c.v if v is None else v)


class MK:
    def __init__(self, cfg):
        self.cfg = dict(cfg)
        self.nc = bass.Bass("TRN2", target_bir_lowering=False, debug=False)
        self.bank = {}
        self.stg_store = {}

    def build(self):
        c = self.cfg
        nc = self.nc
        B, T, D, H, dh, L, I = (c["B"], c["T"], c["D"], c["H"], c["dh"],
                                c["L"], c["I"])
        TOK = B * T
        HD = H * dh
        self.TOK, self.HD = TOK, HD

        din = lambda n, s, dt=F32: nc.dram_tensor(n, s, dt, kind="ExternalInput").ap()
        self.x = din("x", [TOK, D])
        self.Wq = din("Wq", [HD, D])
        self.Wkv = din("Wkv", [L, D])
        self.Wku = din("Wku", [HD, L])
        self.Wvu = din("Wvu", [HD, L])
        self.Wo = din("Wo", [D, HD])
        self.Wg = din("Wg", [I, D])
        self.Wu = din("Wu", [I, D])
        self.Wd = din("Wd", [D, I])
        self.idf = din("idf", [128, 128])
        self.idb = din("idb", [128, 128], BF16)
        self.out = nc.dram_tensor("out", [TOK, D], F32, kind="ExternalOutput").ap()

        dbg = getattr(MK, "DEBUG_DUMP", ())

        def dsc(n, s, dt=F32):
            if n in dbg:
                return nc.dram_tensor(n, s, dt, kind="ExternalOutput").ap()
            return nc.dram_tensor(n, s, dt).ap()
        self.xnT_hi = dsc("xnT_hi", [D, TOK], BF16)
        self.xnT_lo = dsc("xnT_lo", [D, TOK], BF16)
        self.xqT_hi = dsc("xqT_hi", [D, TOK], BF16)
        self.xqT_lo = dsc("xqT_lo", [D, TOK], BF16)
        self.q_hi = dsc("q_hi", [B, HD, T], BF16)
        self.q_lo = dsc("q_lo", [B, HD, T], BF16)
        self.k_hi = dsc("k_hi", [B, HD, T], BF16)
        self.k_lo = dsc("k_lo", [B, HD, T], BF16)
        self.v_hi = dsc("v_hi", [B, T, HD], BF16)
        self.v_lo = dsc("v_lo", [B, T, HD], BF16)
        self.oT_hi = dsc("oT_hi", [B, HD, T], BF16)
        self.oT_lo = dsc("oT_lo", [B, HD, T], BF16)
        self.upT = dsc("upT", [I, TOK])
        self.gateT = dsc("gateT", [I, TOK])
        self.ffnT = dsc("ffnT", [D, TOK])
        self.attT = dsc("attT", [D, TOK])
        self.dbg_raw = {}
        for nm in ("up_raw", "gate_raw"):
            if nm in dbg:
                self.dbg_raw[nm] = dsc(nm, [I, TOK])
        self.cc_in = [dsc(f"cc_in{i}", [1, 2]) for i in range(4)]
        self.cc_out = [dsc(f"cc_out{i}", [NCORES, 2]) for i in range(4)]

        self._ctx = ExitStack()
        e = self._ctx.enter_context
        sb = lambda n, s, dt: e(nc.sbuf_tensor(n, s, dt))
        self.s_idf = sb("s_idf", [128, 128], F32)
        self.s_on16 = sb("s_on16", [128, 128], BF16)
        self.s_idb = sb("s_idb", [128, 128], BF16)
        self.s_ones = sb("s_ones", [1, 128], F32)
        self.s_sc = sb("s_sc", [128, 16], F32)
        self.s_st = sb("s_st", [128, 256], F32)
        self.s_cc = sb("s_cc", [1, 32], F32)
        self.s_sr = sb("s_sr", [1, 16], F32)
        self.s_vec = sb("s_vec", [128, 16], F32)
        self.a16 = sb("a16", [128, 78 * 1024], BF16)
        self.a32 = sb("a32", [128, 9216], F32)
        self.ai = sb("ai", [128, 3072], I32)
        self.ps_a = e(nc.psum_tensor("ps_a", [128, 2048], F32))
        self.ps_b = e(nc.psum_tensor("ps_b", [128, 1024], BF16))
        self.ps_c = e(nc.psum_tensor("ps_c", [128, 1024], BF16))
        self.ps_d = e(nc.psum_tensor("ps_d", [128, 512], F32))
        self.ps_e = e(nc.psum_tensor("ps_e", [128, 512], F32))

        self.sems = {}
        for name in ["cc", "ev", "es", "et"]:
            self.sems[name] = (e(nc.semaphore(name)), Ctr())
        self.chans = {}
        self._enter = e
        self.blk = e(nc.Block())

        self.emit_all()
        self._ctx.close()
        return nc

    def S(self, n):
        return self.sems[n][0]

    def C(self, n):
        return self.sems[n][1]

    def chan(self, name):
        if name not in self.chans:
            self.chans[name] = Chan(self._enter(self.nc.semaphore("ch_" + name)))
        return self.chans[name]

    def bank_wait(self, eng, *keys):
        for k in keys:
            if k in self.bank:
                sem_name, val = self.bank[k]
                eng.wait_ge(self.S(sem_name), val)

    def bank_free(self, key, sem_name):
        self.bank[key] = (sem_name, self.C(sem_name).v)

    def qbar(self, q):
        for n in ["ev", "es", "et"]:
            q.wait_ge(self.S(n), self.C(n).v)
        for name, ch in list(self.chans.items()):
            if name.startswith("st_"):
                ch.wait(q)

    SL = dict(s_x=0, r_x=1, s_up=2, r_up=3, s_g=4, r_g=5, b_g=6,
              s_gs=7, r_gs=8, s_o=9, r_o=10)
    MX = dict(tmp=0, xn=1, up=2, gate=3, gs=4, o=5)

    def emit_all(self):
        c = self.cfg
        B = c["B"]
        self.emit_preamble()
        self.emit_ln()
        self.emit_scale_round(0, [("xn", "s_x", "r_x", 32767.0)])
        self.emit_prep_lhsT(self.Wkv, 69632, dequant=False, tag="kvd")
        self.emit_prep_lhsT(self.Wku, 74752, dequant=False, tag="kup")
        self.emit_prep_lhsT(self.Wvu, 70656, dequant=False, tag="vup")
        self.emit_prep_lhsT(self.Wq, 0, dequant=False, tag="wq")
        for s in range(B):
            self.emit_proj_q(s)
        for s in range(B):
            self.emit_proj_kv(s)
        p1_et = self.C("et").v
        self.emit_prep_lhsT(self.Wu, 0, dequant=True, tag="wu")
        self.emit_xq(start_et=p1_et)
        self.emit_ffn_mm(self.upT, "up", "wu", first_et=p1_et)
        up_et = self.C("et").v
        self.emit_prep_lhsT(self.Wg, 0, dequant=True, tag="wg")
        self.emit_ffn_mm(self.gateT, "gate", "wg", first_et=up_et)
        self.emit_scale_round(1, [("up", "s_up", "r_up", 32767.0),
                                  ("gate", "s_g", "r_g", 32767.0)])
        for s in range(B):
            for h in range(c["H"]):
                self.emit_flash(s, h)
        self.emit_prep_lhsT(self.Wo, 0, dequant=False, tag="wo")
        nu = B * (c["T"] // 512)
        nc_t = self._ffn_c_state()["nTile"]
        nd_t = self._ffn_d_state()["nTile"]
        for j in range(nu // 2):
            self.emit_combine_mm_range(1)
            self.emit_ffn_c_range(-(-nc_t // (nu // 2)))
        self.emit_ffn_c_range(10 ** 9)
        self.emit_scale_round(2, [("gs", "s_gs", "r_gs", 32767.0)])
        for j in range(nu - nu // 2):
            self.emit_combine_mm_range(1)
            self.emit_ffn_d_range(-(-nd_t // (nu - nu // 2)))
        self.emit_ffn_d_range(10 ** 9)
        self.emit_scale_round(3, [("o", "s_o", "r_o", 32767.0)])
        self.emit_prep_lhsT(self.Wd, 0, dequant=True, tag="wd")
        self.emit_ffn_down()

        chans = list(self.chans.values())

        @self.blk.gpsimd
        def _(g):
            for ch in chans:
                ch.wait(g)

    def emit_preamble(self):
        blk = self.blk
        ev, c_ev = self.S("ev"), self.C("ev")
        ch = self.chan("ld_pre")

        @blk.sync
        def _(q):
            ch.dma(q, self.s_idf[:], self.idf)
            ch.dma(q, self.s_idb[:], self.idb)

        @blk.vector
        def _(v):
            ch.wait(v)
            v.memset(self.s_ones[:], 1.0)
            v.memset(self.s_st[:], -1e30)
            v.memset(self.s_cc[:], 0.0)
            v.memset(self.s_sr[:], 0.0)
            v.memset(self.s_vec[:], 0.0)
            v.drain()
            v.memset(self.s_sc[:, 11:12], 1e-5)
            v.memset(self.s_sc[:, 12:13], 0.0)
            v.memset(self.s_sc[:, 13:14], -3.0)
            v.memset(self.s_on16[:], 1.0)
            v.drain().then_inc(ev, 1)
        c_ev.inc()

    def emit_ln(self):
        blk = self.blk
        D = self.cfg["D"]
        TOK = self.TOK
        nDc = D // 128
        nTt = TOK // 128
        ST = 512
        nST = TOK // ST
        a16, a32 = self.a16, self.a32
        ev, c_ev = self.S("ev"), self.C("ev")
        es, c_es = self.S("es"), self.C("es")
        et, c_et = self.S("et"), self.C("et")
        x_t = self.x.rearrange("(n p) d -> n p d", p=128)
        MXC = self.MX
        chx = [self.chan("ld_x0"), self.chan("ld_x1")]
        chst = self.chan("st_p0")

        free_at = {}
        split_free = {}
        store_done = []
        pre_ev = c_ev.v

        @blk.sync
        def _(q):
            q.wait_ge(ev, pre_ev)
            for i in range(min(2, nTt)):
                chx[i % 2].dma(q, a32[:, (i % 2) * 1024:(i % 2) * 1024 + 1024],
                               x_t[i])

        for sti in range(nST):
            for j in range(ST // 128):
                ti = sti * (ST // 128) + j
                xbuf = a32[:, (ti % 2) * 1024:(ti % 2) * 1024 + 1024]
                d = a32[:, 2048:3072]
                xn = a32[:, 3072:4096]
                sp_h = a16[:, 16384 + (ti % 2) * 1024:16384 + (ti % 2) * 1024 + 1024]
                sp_l = a16[:, 18432 + (ti % 2) * 1024:18432 + (ti % 2) * 1024 + 1024]
                ldv = chx[ti % 2].v

                @blk.vector
                def _(v):
                    chx[ti % 2].wait(v, ldv)
                    v.reduce_sum(self.s_vec[:, 0:1], xbuf, axis=AX.X)
                    v.drain()
                    v.tensor_scalar(self.s_vec[:, 1:2], self.s_vec[:, 0:1],
                                    1.0 / D, None, ALU.mult)
                    v.drain()
                    v.tensor_scalar(d, xbuf, self.s_vec[:, 1:2], None, ALU.subtract)
                    v.drain()
                    v.tensor_tensor(xn, d, d, ALU.mult)
                    v.drain()
                    v.reduce_sum(self.s_vec[:, 2:3], xn, axis=AX.X)
                    v.drain().then_inc(ev, 1)
                c_ev.inc()
                free_at[ti] = c_ev.v

                if ti + 2 < nTt:
                    evv, nt = free_at[ti], ti + 2

                    @blk.sync
                    def _(q):
                        q.wait_ge(ev, evv)
                        chx[nt % 2].dma(q, a32[:, (nt % 2) * 1024:
                                               (nt % 2) * 1024 + 1024], x_t[nt])

                t_ev = c_ev.v

                @blk.scalar
                def _(s):
                    s.wait_ge(ev, t_ev)
                    s.activation(self.s_vec[:, 3:4], self.s_vec[:, 2:3], AF.Ln,
                                 bias=self.s_sc[:, 11:12], scale=1.0 / D)
                    s.drain()
                    s.activation(self.s_vec[:, 4:5], self.s_vec[:, 3:4], AF.Exp,
                                 bias=self.s_sc[:, 12:13], scale=-0.5)
                    s.drain().then_inc(es, 1)
                c_es.inc()

                t_es = c_es.v
                etv = split_free.get(ti - 2, 0)

                @blk.vector
                def _(v):
                    v.wait_ge(es, t_es)
                    v.wait_ge(et, etv)
                    v.tensor_scalar(xn, d, self.s_vec[:, 4:5], None, ALU.mult)
                    v.drain()
                    v.reduce_max(self.s_st[:, MXC["tmp"]:MXC["tmp"] + 1], xn,
                                 axis=AX.X, apply_absolute_value=True)
                    v.drain()
                    v.tensor_tensor(self.s_st[:, MXC["xn"]:MXC["xn"] + 1],
                                    self.s_st[:, MXC["xn"]:MXC["xn"] + 1],
                                    self.s_st[:, MXC["tmp"]:MXC["tmp"] + 1],
                                    ALU.max)
                    v.drain()
                    v.tensor_copy(sp_h, xn)
                    v.drain()
                    v.tensor_tensor(sp_l, xn, sp_h, ALU.subtract)
                    v.drain().then_inc(ev, 1)
                c_ev.inc()

                tv = c_ev.v

                @blk.tensor
                def _(t):
                    t.wait_ge(ev, tv)
                    self.bank_wait(t, "b", "c")
                    for cdx in range(nDc):
                        t.transpose(self.ps_b[:, cdx * 128:cdx * 128 + 128],
                                    sp_h[:, cdx * 128:cdx * 128 + 128],
                                    self.s_idb[:])
                    for cdx in range(nDc):
                        t.transpose(self.ps_c[:, cdx * 128:cdx * 128 + 128],
                                    sp_l[:, cdx * 128:cdx * 128 + 128],
                                    self.s_idb[:])
                    t.drain().then_inc(et, 1)
                c_et.inc()
                split_free[ti] = c_et.v

                t_et = c_et.v
                stv = store_done[sti - 1] if sti >= 1 else 0
                jj = j

                @blk.vector
                def _(v):
                    v.wait_ge(et, t_et)
                    if jj == 0:
                        chst.wait(v, stv)
                    dst_h = a16[:, 0:8 * ST].rearrange(
                        "p (c t) -> p c t", t=ST)[:, :, jj * 128:jj * 128 + 128]
                    dst_l = a16[:, 4096:4096 + 8 * ST].rearrange(
                        "p (c t) -> p c t", t=ST)[:, :, jj * 128:jj * 128 + 128]
                    v.tensor_copy(dst_h,
                                  self.ps_b[:].rearrange("p (c t) -> p c t", t=128))
                    v.drain()
                    v.tensor_copy(dst_l,
                                  self.ps_c[:].rearrange("p (c t) -> p c t", t=128))
                    v.drain().then_inc(ev, 1)
                c_ev.inc()
                self.bank_free("b", "ev")
                self.bank_free("c", "ev")

            t_ev2 = c_ev.v
            sti_ = sti

            @blk.gpsimd
            def _(g):
                g.wait_ge(ev, t_ev2)
                src_h = a16[:, 0:8 * ST].rearrange("p (c t) -> p c t", t=ST)
                src_l = a16[:, 4096:4096 + 8 * ST].rearrange("p (c t) -> p c t", t=ST)
                dst_h = self.xnT_hi.rearrange("(c p) n -> p c n", p=128)[
                    :, :, sti_ * ST:sti_ * ST + ST]
                dst_l = self.xnT_lo.rearrange("(c p) n -> p c n", p=128)[
                    :, :, sti_ * ST:sti_ * ST + ST]
                chst.dma(g, dst_h, src_h)
                chst.dma(g, dst_l, src_l)
            store_done.append(chst.v)

    def emit_scale_round(self, round_i, items):
        blk = self.blk
        cc, c_cc = self.S("cc"), self.C("cc")
        ev, c_ev = self.S("ev"), self.C("ev")
        et, c_et = self.S("et"), self.C("et")
        MXC, SL = self.MX, self.SL
        n = len(items)
        chcc = self.chan("st_cc")

        tv = c_ev.v

        sb = list(getattr(self, "_sbank", []))

        @blk.tensor
        def _(t):
            t.wait_ge(ev, tv)
            self.bank_wait(t, "a")
            for v_ in sb:
                t.wait_ge(self.S("es"), v_)
            for i, (mx, _, _, _) in enumerate(items):
                t.transpose(self.ps_a[0:1, i * 128:(i + 1) * 128],
                            self.s_st[:, MXC[mx]:MXC[mx] + 1], self.s_idf[:])
            t.drain().then_inc(et, 1)
        c_et.inc()

        tev = c_et.v

        @blk.vector
        def _(v):
            v.wait_ge(et, tev)
            for i in range(n):
                v.reduce_max(self.s_cc[0:1, i:i + 1],
                             self.ps_a[0:1, i * 128:(i + 1) * 128], axis=AX.X)
                v.drain()
            v.drain().then_inc(ev, 1)
        c_ev.inc()
        self.bank_free("a", "ev")
        self.psa_guard = c_ev.v
        ccv = c_ev.v

        @blk.gpsimd
        def _(g):
            g.wait_ge(ev, ccv)
            chcc.dma(g, self.cc_in[round_i][0:1, 0:2], self.s_cc[0:1, 0:2])
            chcc.wait(g)
            g.collective_compute(
                "AllGather", ALU.bypass,
                replica_groups=[list(range(NCORES))],
                ins=[self.cc_in[round_i].opt()],
                outs=[self.cc_out[round_i].opt()],
            ).then_inc(cc, 1)
            c_cc.inc()
            g.wait_ge(cc, c_cc.v)
            chcc.dma(g, self.s_cc[0:1, 8:8 + 2 * NCORES],
                     self.cc_out[round_i].flatten().unsqueeze(0))
        gatherv = chcc.v

        @blk.vector
        def _(v):
            chcc.wait(v, gatherv)
            for i, (mx, s_slot, r_slot, qmax) in enumerate(items):
                gathered = self.s_cc[0:1, 8:8 + 2 * NCORES].rearrange(
                    "p (c k) -> p c k", k=2)[:, :, i:i + 1].transpose([0, 2, 1])
                v.reduce_max(self.s_sr[0:1, 2 * i:2 * i + 1], gathered, axis=AX.X)
                v.drain()
                v.tensor_scalar(self.s_sr[0:1, 2 * i:2 * i + 1],
                                self.s_sr[0:1, 2 * i:2 * i + 1],
                                1.0 / qmax, 1e-12, ALU.mult, ALU.max)
                v.drain()
                v.reciprocal(self.s_sr[0:1, 2 * i + 1:2 * i + 2],
                             self.s_sr[0:1, 2 * i:2 * i + 1])
                v.drain()
            v.drain().then_inc(ev, 1)
        c_ev.inc()
        bv = c_ev.v

        @blk.tensor
        def _(t):
            t.wait_ge(ev, bv)
            self.bank_wait(t, "d")
            t.matmul(self.ps_d[:, 0:2 * n], self.s_ones[:],
                     self.s_sr[0:1, 0:2 * n], start=True, stop=True)
            t.drain().then_inc(et, 1)
        c_et.inc()
        mv = c_et.v

        @blk.vector
        def _(v):
            v.wait_ge(et, mv)
            for i, (mx, s_slot, r_slot, qmax) in enumerate(items):
                v.tensor_copy(self.s_sc[:, SL[s_slot]:SL[s_slot] + 1],
                              self.ps_d[:, 2 * i:2 * i + 1])
                v.drain()
                v.tensor_copy(self.s_sc[:, SL[r_slot]:SL[r_slot] + 1],
                              self.ps_d[:, 2 * i + 1:2 * i + 2])
                v.drain()
                if s_slot == "s_g":
                    v.tensor_scalar(self.s_sc[:, SL["b_g"]:SL["b_g"] + 1],
                                    self.s_sc[:, SL["s_g"]:SL["s_g"] + 1],
                                    -32768.0, None, ALU.mult)
                    v.drain()
            v.drain().then_inc(ev, 1)
        c_ev.inc()
        self.bank_free("d", "ev")
        self.scale_ready = c_ev.v

    def emit_prep_lhsT(self, W, off, dequant, tag):
        blk = self.blk
        rows, cols = W.shape
        rp = min(128, rows)
        cw = min(128, cols)
        nMt = (rows + 127) // 128
        nCc = (cols + cw - 1) // cw
        half = nMt * nCc * rp
        setattr(self, "OFF_" + tag, (off, half, rows, cols))
        ev, c_ev = self.S("ev"), self.C("ev")
        es, c_es = self.S("es"), self.C("es")
        et, c_et = self.S("et"), self.C("et")
        a16, a32, ai = self.a16, self.a32, self.ai
        SPL_HI, SPL_LO = 49152, 52224
        w_t = W.rearrange("(n p) d -> n p d", p=128) if rows >= 128 else None
        chw = [self.chan("ld_w0"), self.chan("ld_w1")]
        load_done = {}
        split_done = {}
        tr_done = {}

        @blk.sync
        def _(q):
            self.qbar(q)
            for i in range(min(2, nMt)):
                src = w_t[i] if rows >= 128 else W
                load_done[i] = chw[i % 2].dma(
                    q, a32[0:rp, (i % 2) * cols:(i % 2) * cols + cols], src)

        for mt in range(nMt):
            wb = a32[0:rp, (mt % 2) * cols:(mt % 2) * cols + cols]
            hi = a16[0:rp, SPL_HI:SPL_HI + cols]
            lo = a16[0:rp, SPL_LO:SPL_LO + cols]

            qv = load_done[mt]
            etv = tr_done.get(mt - 1, 0)
            mtx = mt

            @blk.vector
            def _(v):
                chw[mtx % 2].wait(v, qv)
                v.wait_ge(et, etv)
                if dequant:
                    nblk = cols // 32
                    bm = self.s_st[:, 64:64 + nblk]
                    rcp = self.s_st[:, 160:160 + nblk]
                    w3 = wb.rearrange("p (b k) -> p b k", k=32)
                    v.reduce_max(bm[0:rp, :], w3, axis=AX.X,
                                 apply_absolute_value=True)
                    v.drain()
                    v.tensor_scalar(bm[0:rp, :], bm[0:rp, :], 1.0 / 7.0, 1e-12,
                                    ALU.mult, ALU.max)
                    v.drain()
                    v.reciprocal(rcp[0:rp, :], bm[0:rp, :])
                    v.drain()
                    tmp = a32[0:rp, 6144:6144 + cols]
                    v.tensor_tensor(tmp.rearrange("p (b k) -> p b k", k=32), w3,
                                    rcp[0:rp, :].unsqueeze(2).broadcast_to(
                                        [rp, nblk, 32]), ALU.mult)
                    v.drain()
                    ki = ai[0:rp, 0:cols]
                    v.tensor_scalar(ki, tmp, 1.0, 8.0, ALU.mult, ALU.add)
                    v.drain()
                    v.tensor_scalar(tmp, ki, -8.0, None, ALU.add)
                    v.drain()
                    v.tensor_tensor(tmp.rearrange("p (b k) -> p b k", k=32),
                                    tmp.rearrange("p (b k) -> p b k", k=32),
                                    bm[0:rp, :].unsqueeze(2).broadcast_to(
                                        [rp, nblk, 32]), ALU.mult)
                    v.drain()
                    src = tmp
                else:
                    src = wb
                v.tensor_copy(hi, src)
                v.drain()
                v.tensor_tensor(lo, src, hi, ALU.subtract)
                v.drain().then_inc(ev, 1)
            c_ev.inc()
            split_done[mt] = c_ev.v

            if mt + 2 < nMt:
                evv, nt = split_done[mt], mt + 2

                @blk.sync
                def _(q):
                    q.wait_ge(ev, evv)
                    load_done[nt] = chw[nt % 2].dma(
                        q, a32[0:rp, (nt % 2) * cols:(nt % 2) * cols + cols],
                        w_t[nt])

            per_bank = 1024 // rp
            for hl in range(2):
                ps = self.ps_b if hl == 0 else self.ps_c
                bank_key = "b" if hl == 0 else "c"
                srcbuf = hi if hl == 0 else lo
                for bs in range(0, nCc, per_bank):
                    nb = min(per_bank, nCc - bs)
                    sv = split_done[mt]
                    bsc, nbc, psx, srcx = bs, nb, ps, srcbuf

                    @blk.tensor
                    def _(t):
                        t.wait_ge(ev, sv)
                        self.bank_wait(t, bank_key)
                        for k in range(nbc):
                            t.transpose(psx[0:cw, k * rp:k * rp + rp],
                                        srcx[:, (bsc + k) * cw:(bsc + k) * cw + cw],
                                        self.s_idb[0:rp, 0:rp])
                        t.drain().then_inc(et, 1)
                    c_et.inc()
                    tev = c_et.v
                    mtc, hlc = mt, hl

                    @blk.scalar
                    def _(s):
                        s.wait_ge(et, tev)
                        dst = a16[0:cw,
                                  off + hlc * half + (mtc * nCc + bsc) * rp:
                                  off + hlc * half + (mtc * nCc + bsc) * rp +
                                  nbc * rp]
                        s.copy(dst, psx[0:cw, 0:nbc * rp])
                        s.drain().then_inc(es, 1)
                    c_es.inc()
                    self.bank_free(bank_key, "es")
            tr_done[mt] = c_et.v
        setattr(self, "PREP_DONE_" + tag, c_es.v)

    def lhsT(self, tag, mt, cc_, hl):
        off, half, rows, cols = getattr(self, "OFF_" + tag)
        rp = min(128, rows)
        cw = min(128, cols)
        nCc = (cols + cw - 1) // cw
        base = off + hl * half + (mt * nCc + cc_) * rp
        return self.a16[0:cw, base:base + rp]

    def emit_proj_q(self, s):
        blk = self.blk
        c = self.cfg
        T, D, H, dh, L = c["T"], c["D"], c["H"], c["dh"], c["L"]
        nDc = D // 128
        nTg = T // 512
        a16, a32 = self.a16, self.a32
        ev, c_ev = self.S("ev"), self.C("ev")
        et, c_et = self.S("et"), self.C("et")

        XN_HI = 32768
        XN_LO = 32768 + nDc * T
        KV_HI = 65536
        KV_LO = 65536 + T
        STG_Q = (65536, 66560)
        STG_KV = (0, 1024)
        chxn = self.chan("ld_xn")

        @blk.sync
        def _(q):
            self.qbar(q)
            for src, dst in [(self.xnT_hi, XN_HI), (self.xnT_lo, XN_LO)]:
                chxn.dma(q, a16[:, dst:dst + nDc * T].rearrange(
                    "p (c t) -> p c t", t=T),
                    src.rearrange("(c p) n -> p c n", p=128)[
                        :, :, s * T:(s + 1) * T])
        xn_ready = chxn.v

        xnt = lambda hl, dc, t0, w: a16[:, (XN_HI if hl == 0 else XN_LO) +
                                        dc * T + t0:
                                        (XN_HI if hl == 0 else XN_LO) +
                                        dc * T + t0 + w]

        for h in range(H):
            for dc2 in range(dh // 128):
                for tg in range(nTg):
                    hh, dd, tt = h, dc2, tg

                    psx, bk = ((self.ps_d, "d") if self._ess_i % 2 == 0
                               else (self.ps_e, "e"))

                    @blk.tensor
                    def _(t):
                        chxn.wait(t, xn_ready)
                        self.bank_wait(t, bk)
                        mi = hh * (dh // 128) + dd
                        first = True
                        for dc in range(nDc):
                            for (sp, mp) in ((0, 0), (0, 1), (1, 0)):
                                t.matmul(psx[:, 0:512],
                                         self.lhsT("wq", mi, dc, sp),
                                         xnt(mp, dc, tt * 512, 512),
                                         start=first,
                                         stop=(dc == nDc - 1 and
                                               (sp, mp) == (1, 0)))
                                first = False
                        t.drain().then_inc(et, 1)
                    c_et.inc()
                    self._evac_split_store(
                        bank_key=bk, ps=psx, stg=STG_Q, n=512,
                        dst_hi=self.q_hi[s, h * dh + dc2 * 128:
                                         h * dh + dc2 * 128 + 128,
                                         tg * 512:tg * 512 + 512],
                        dst_lo=self.q_lo[s, h * dh + dc2 * 128:
                                         h * dh + dc2 * 128 + 128,
                                         tg * 512:tg * 512 + 512])


    def emit_proj_kv(self, s):
        blk = self.blk
        c = self.cfg
        T, D, H, dh, L = c["T"], c["D"], c["H"], c["dh"], c["L"]
        nDc = D // 128
        nTg = T // 512
        a16, a32 = self.a16, self.a32
        ev, c_ev = self.S("ev"), self.C("ev")
        et, c_et = self.S("et"), self.C("et")

        XN_HI = 32768
        XN_LO = 32768 + nDc * T
        KV_HI = 65536
        KV_LO = 65536 + T
        STG_Q = (65536, 66560)
        STG_KV = (0, 1024)
        chxn = self.chan("ld_xn")

        @blk.sync
        def _(q):
            self.qbar(q)
            for src, dst in [(self.xnT_hi, XN_HI), (self.xnT_lo, XN_LO)]:
                chxn.dma(q, a16[:, dst:dst + nDc * T].rearrange(
                    "p (c t) -> p c t", t=T),
                    src.rearrange("(c p) n -> p c n", p=128)[
                        :, :, s * T:(s + 1) * T])
        xn_ready = chxn.v

        xnt = lambda hl, dc, t0, w: a16[:, (XN_HI if hl == 0 else XN_LO) +
                                        dc * T + t0:
                                        (XN_HI if hl == 0 else XN_LO) +
                                        dc * T + t0 + w]

        for tg in range(nTg):
            tgc = tg

            @blk.tensor
            def _(t):
                chxn.wait(t, xn_ready)
                self.bank_wait(t, "d")
                first = True
                for dc in range(nDc):
                    for (sp, mp) in ((0, 0), (0, 1), (1, 0)):
                        t.matmul(self.ps_d[0:64, 0:512],
                                 self.lhsT("kvd", 0, dc, sp),
                                 xnt(mp, dc, tgc * 512, 512),
                                 start=first, stop=(dc == nDc - 1 and
                                                    (sp, mp) == (1, 0)))
                        first = False
                t.drain().then_inc(et, 1)
            c_et.inc()
            mmv = c_et.v

            qs0 = self.chan("st_ess0").v
            qs1 = self.chan("st_ess1").v

            @blk.vector
            def _(v):
                v.wait_ge(et, mmv)
                self.chan("st_ess0").wait(v, qs0)
                self.chan("st_ess1").wait(v, qs1)
                kvf = a32[0:64, 0:512]
                v.tensor_copy(kvf, self.ps_d[0:64, 0:512])
                v.drain()
                v.tensor_copy(a16[0:64, KV_HI + tgc * 512:KV_HI + tgc * 512 + 512],
                              kvf)
                v.drain()
                v.tensor_tensor(a16[0:64, KV_LO + tgc * 512:KV_LO + tgc * 512 + 512],
                                kvf,
                                a16[0:64, KV_HI + tgc * 512:KV_HI + tgc * 512 + 512],
                                ALU.subtract)
                v.drain().then_inc(ev, 1)
            c_ev.inc()
            self.bank_free("d", "ev")
        kv_ready = c_ev.v

        kvt = lambda hl, t0, w: a16[0:64, (KV_HI if hl == 0 else KV_LO) +
                                    t0:(KV_HI if hl == 0 else KV_LO) + t0 + w]

        for h in range(H):
            for dc2 in range(dh // 128):
                for tg in range(nTg):
                    hh, dd, tt = h, dc2, tg

                    psx, bk = ((self.ps_d, "d") if self._ess_i % 2 == 0
                               else (self.ps_e, "e"))

                    @blk.tensor
                    def _(t):
                        t.wait_ge(ev, kv_ready)
                        self.bank_wait(t, bk)
                        mi = hh * (dh // 128) + dd
                        first = True
                        for (sp, mp) in ((0, 0), (0, 1), (1, 0)):
                            t.matmul(psx[:, 0:512],
                                     self.lhsT("kup", mi, 0, sp),
                                     kvt(mp, tt * 512, 512),
                                     start=first, stop=((sp, mp) == (1, 0)))
                            first = False
                        t.drain().then_inc(et, 1)
                    c_et.inc()
                    self._evac_split_store(
                        bank_key=bk, ps=psx, stg=STG_KV, n=512,
                        dst_hi=self.k_hi[s, h * dh + dc2 * 128:
                                         h * dh + dc2 * 128 + 128,
                                         tg * 512:tg * 512 + 512],
                        dst_lo=self.k_lo[s, h * dh + dc2 * 128:
                                         h * dh + dc2 * 128 + 128,
                                         tg * 512:tg * 512 + 512])

        nTc = T // 128
        voff, vhalf, _, _ = getattr(self, "OFF_vup")
        for tc in range(nTc):
            for hg in range(H * dh // 512):
                tcc, hgc = tc, hg

                psx, bk = ((self.ps_d, "d") if self._ess_i % 2 == 0
                           else (self.ps_e, "e"))

                @blk.tensor
                def _(t):
                    t.wait_ge(ev, kv_ready)
                    self.bank_wait(t, bk)
                    first = True
                    for (sp, mp) in ((0, 0), (0, 1), (1, 0)):
                        rhs = a16[0:64, voff + mp * vhalf + hgc * 512:
                                  voff + mp * vhalf + hgc * 512 + 512]
                        t.matmul(psx[:, 0:512],
                                 kvt(sp, tcc * 128, 128), rhs,
                                 start=first, stop=((sp, mp) == (1, 0)))
                        first = False
                    t.drain().then_inc(et, 1)
                c_et.inc()
                self._evac_split_store(
                    bank_key=bk, ps=psx, stg=STG_KV, n=512,
                    dst_hi=self.v_hi[s, tc * 128:tc * 128 + 128,
                                     hg * 512:hg * 512 + 512],
                    dst_lo=self.v_lo[s, tc * 128:tc * 128 + 128,
                                     hg * 512:hg * 512 + 512])


    _ess_i = 0

    def _evac_split_store(self, bank_key, stg, n, dst_hi, dst_lo, ps=None):
        blk = self.blk
        a16 = self.a16
        ev, c_ev = self.S("ev"), self.C("ev")
        slot = self._ess_i % 2
        self._ess_i += 1
        ch = self.chan(f"st_ess{slot}")
        ps = self.ps_d if ps is None else ps
        base = stg[slot]
        hi = a16[:, base:base + n]
        lo = a16[:, base + 512:base + 512 + n]
        tv = self.C("et").v
        stv = ch.v

        @blk.vector
        def _(v):
            v.wait_ge(self.S("et"), tv)
            ch.wait(v, stv)
            v.tensor_copy(hi, ps[:, 0:n])
            v.drain()
            v.tensor_tensor(lo, ps[:, 0:n], hi, ALU.subtract)
            v.drain().then_inc(ev, 1)
        c_ev.inc()
        self.bank_free(bank_key, "ev")
        sv = c_ev.v

        @blk.gpsimd
        def _(g):
            g.wait_ge(ev, sv)
            ch.dma(g, dst_hi, hi)
            ch.dma(g, dst_lo, lo)

    def emit_xq(self, start_et):
        blk = self.blk
        D, TOK = self.cfg["D"], self.TOK
        a16, a32, ai = self.a16, self.a32, self.ai
        ev, c_ev = self.S("ev"), self.C("ev")
        SL = self.SL
        FW = min(1024, TOK)
        nDc = D // 128
        nTile = nDc * (TOK // FW)
        _t4 = lambda ap: ap.rearrange("(c p) (n f) -> c n p f", p=128, f=FW)
        hi_t4, lo_t4 = _t4(self.xnT_hi), _t4(self.xnT_lo)
        qhi_t4, qlo_t4 = _t4(self.xqT_hi), _t4(self.xqT_lo)
        hi_t = lambda i: hi_t4[i % nDc, i // nDc]
        lo_t = lambda i: lo_t4[i % nDc, i // nDc]
        qhi_t = lambda i: qhi_t4[i % nDc, i // nDc]
        qlo_t = lambda i: qlo_t4[i % nDc, i // nDc]
        self.xq_f = {}
        et = self.S("et")
        IN_HI, IN_LO, ST_HI, ST_LO = 71680, 73728, 75776, 77824
        chi = [self.chan("ld_xq0"), self.chan("ld_xq1")]
        cho = [self.chan("st_xq0"), self.chan("st_xq1")]
        load_done = {}
        proc_done = {}
        scl = self.scale_ready

        @blk.sync
        def _(q):
            q.wait_ge(et, start_et)
            q.wait_ge(ev, scl)
            for i in range(min(2, nTile)):
                chi[i % 2].dma(q, a16[:, IN_HI + (i % 2) * FW:
                                      IN_HI + (i % 2) * FW + FW], hi_t(i))
                load_done[i] = chi[i % 2].dma(
                    q, a16[:, IN_LO + (i % 2) * FW:IN_LO + (i % 2) * FW + FW],
                    lo_t(i))

        for i in range(nTile):
            sl = i % 2
            qv = load_done[i]
            stv = cho[sl].v
            slx = sl

            @blk.vector
            def _(v):
                chi[slx].wait(v, qv)
                cho[slx].wait(v, stv)
                xh = a16[:, IN_HI + slx * FW:IN_HI + slx * FW + FW]
                xl = a16[:, IN_LO + slx * FW:IN_LO + slx * FW + FW]
                xf = a32[:, 4096:4096 + FW]
                qf = a32[:, 5120:5120 + FW]
                ki = ai[:, 1024:1024 + FW]
                sh = a16[:, ST_HI + slx * FW:ST_HI + slx * FW + FW]
                slo = a16[:, ST_LO + slx * FW:ST_LO + slx * FW + FW]
                v.tensor_tensor(xf, xh, xl, ALU.add)
                v.drain()
                v.tensor_scalar(ki, xf, self.s_sc[:, SL["r_x"]:SL["r_x"] + 1],
                                32768.0, ALU.mult, ALU.add)
                v.drain()
                v.tensor_scalar(sh, ki, -32768.0,
                                self.s_sc[:, SL["s_x"]:SL["s_x"] + 1],
                                ALU.add, ALU.mult)
                v.drain()
                v.tensor_scalar(qf, ki, -32768.0,
                                self.s_sc[:, SL["s_x"]:SL["s_x"] + 1],
                                ALU.add, ALU.mult)
                v.drain()
                v.tensor_tensor(slo, qf, sh, ALU.subtract)
                v.drain().then_inc(ev, 1)
            c_ev.inc()
            proc_done[i] = c_ev.v

            if i + 2 < nTile:
                evv, nt = proc_done[i], i + 2

                @blk.sync
                def _(q):
                    q.wait_ge(ev, evv)
                    chi[nt % 2].dma(q, a16[:, IN_HI + (nt % 2) * FW:
                                           IN_HI + (nt % 2) * FW + FW],
                                    hi_t(nt))
                    load_done[nt] = chi[nt % 2].dma(
                        q, a16[:, IN_LO + (nt % 2) * FW:IN_LO + (nt % 2) * FW + FW],
                        lo_t(nt))

            pv = proc_done[i]
            ii = i

            @blk.gpsimd
            def _(g):
                g.wait_ge(ev, pv)
                cho[slx].dma(g, qhi_t(ii),
                             a16[:, ST_HI + slx * FW:ST_HI + slx * FW + FW])
                cho[slx].dma(g, qlo_t(ii),
                             a16[:, ST_LO + slx * FW:ST_LO + slx * FW + FW])
            if i % nDc == nDc - 1:
                self.xq_f[i // nDc] = (cho[0].v, cho[1].v)

    def emit_ffn_mm(self, dstT, mx_name, tag, first_et):
        blk = self.blk
        c = self.cfg
        D, I, TOK = c["D"], c["I"], self.TOK
        nDc, nIc, nTg = D // 128, I // 128, TOK // 512
        a16, a32 = self.a16, self.a32
        ev, c_ev = self.S("ev"), self.C("ev")
        es, c_es = self.S("es"), self.C("es")
        et, c_et = self.S("et"), self.C("et")
        MXC = self.MX
        XQ = 55296
        qhi = self.xqT_hi.rearrange("(c p) n -> p c n", p=128)
        qlo = self.xqT_lo.rearrange("(c p) n -> p c n", p=128)
        chq = [self.chan("ld_fq0"), self.chan("ld_fq1")]
        chs = [self.chan("st_fs0"), self.chan("st_fs1")]
        load_done = {}
        tg_mm = {}
        prep_es = getattr(self, "PREP_DONE_" + tag)
        chox = [self.chan("st_xq0"), self.chan("st_xq1")]
        xq_f = self.xq_f

        @blk.sync
        def _(q):
            q.wait_ge(et, first_et)
            q.wait_ge(es, prep_es)
            chox[0].wait(q, xq_f[0][0])
            chox[1].wait(q, xq_f[0][1])
            for i in range(min(2, nTg)):
                sl = i % 2
                chq[sl].dma(q, a16[:, XQ + sl * 8192:XQ + sl * 8192 + 4096]
                            .rearrange("p (c t) -> p c t", t=512),
                            qhi[:, :, i * 512:i * 512 + 512])
                load_done[i] = chq[sl].dma(
                    q, a16[:, XQ + sl * 8192 + 4096:XQ + sl * 8192 + 8192]
                    .rearrange("p (c t) -> p c t", t=512),
                    qlo[:, :, i * 512:i * 512 + 512])

        for tg in range(nTg):
            sl = tg % 2
            for ic in range(nIc):
                bank_key = "d" if ic % 2 == 0 else "e"
                ps = self.ps_d if ic % 2 == 0 else self.ps_e
                qv = load_done[tg]
                slx, icx, tgx, psx = sl, ic, tg, ps

                @blk.tensor
                def _(t):
                    chq[slx].wait(t, qv)
                    self.bank_wait(t, bank_key)
                    first = True
                    for dc in range(nDc):
                        for (sp, mp) in ((0, 0), (0, 1), (1, 0)):
                            rhs = a16[:, XQ + slx * 8192 + mp * 4096 + dc * 512:
                                      XQ + slx * 8192 + mp * 4096 + dc * 512 + 512]
                            t.matmul(psx[:, 0:512],
                                     self.lhsT(tag, icx, dc, sp), rhs,
                                     start=first,
                                     stop=(dc == nDc - 1 and (sp, mp) == (1, 0)))
                            first = False
                    t.drain().then_inc(et, 1)
                c_et.inc()
                mmv = c_et.v

                stg = a32[:, (ic % 2) * 512:(ic % 2) * 512 + 512]
                stv = chs[ic % 2].v
                stgx = stg

                @blk.vector
                def _(v):
                    v.wait_ge(et, mmv)
                    chs[icx % 2].wait(v, stv)
                    v.tensor_copy(stgx, psx[:, 0:512])
                    v.drain()
                    v.reduce_max(self.s_st[:, MXC["tmp"]:MXC["tmp"] + 1], stgx,
                                 axis=AX.X, apply_absolute_value=True)
                    v.drain()
                    v.tensor_tensor(self.s_st[:, MXC[mx_name]:MXC[mx_name] + 1],
                                    self.s_st[:, MXC[mx_name]:MXC[mx_name] + 1],
                                    self.s_st[:, MXC["tmp"]:MXC["tmp"] + 1],
                                    ALU.max)
                    v.drain().then_inc(ev, 1)
                c_ev.inc()
                self.bank_free(bank_key, "ev")
                dv = c_ev.v

                dbg_t = self.dbg_raw.get(mx_name + "_raw")

                @blk.gpsimd
                def _(g):
                    g.wait_ge(ev, dv)
                    chs[icx % 2].dma(g, dstT[icx * 128:icx * 128 + 128,
                                             tgx * 512:tgx * 512 + 512], stgx)
                    if dbg_t is not None:
                        chs[icx % 2].dma(g, dbg_t[icx * 128:icx * 128 + 128,
                                                  tgx * 512:tgx * 512 + 512],
                                         stgx)
            tg_mm[tg] = c_et.v

            if tg + 2 < nTg:
                etv, nt = tg_mm[tg], tg + 2
                xv0, xv1 = xq_f[(tg + 2) // 2]

                @blk.sync
                def _(q):
                    q.wait_ge(et, etv)
                    chox[0].wait(q, xv0)
                    chox[1].wait(q, xv1)
                    sl2 = nt % 2
                    chq[sl2].dma(q, a16[:, XQ + sl2 * 8192:XQ + sl2 * 8192 + 4096]
                                 .rearrange("p (c t) -> p c t", t=512),
                                 qhi[:, :, nt * 512:nt * 512 + 512])
                    load_done[nt] = chq[sl2].dma(
                        q, a16[:, XQ + sl2 * 8192 + 4096:XQ + sl2 * 8192 + 8192]
                        .rearrange("p (c t) -> p c t", t=512),
                        qlo[:, :, nt * 512:nt * 512 + 512])

    def emit_flash(self, s, h):
        blk = self.blk
        c = self.cfg
        T, dh, H = c["T"], c["dh"], c["H"]
        nKc = T // 128
        nQg = T // 512
        ndc = dh // 128
        a16, a32 = self.a16, self.a32
        ev, c_ev = self.S("ev"), self.C("ev")
        es, c_es = self.S("es"), self.C("es")
        et, c_et = self.S("et"), self.C("et")
        ET_HI, ET_LO, OTS = 24576, 32768, 40960
        head_idx = s * H + h
        par = head_idx % 2
        BASE = [0, 45056][par]
        K_, V_, Q_ = BASE, BASE + 8192, BASE + 16384
        chf = [self.chan("ld_fl0"), self.chan("ld_fl1")]
        cho = [self.chan("st_fl0"), self.chan("st_fl1")]
        guard = getattr(self, "psa_guard", 0)
        if not hasattr(self, "_fl"):
            ev0 = self.C("et").v
            self._fl = {0: dict(loads=0, lastpv=ev0, issued=False),
                        1: dict(loads=0, lastpv=ev0, issued=False)}
            self._et_free = 0
            self._sbank = [0, 0, 0, 0]

        def issue_loads(si, hi_, parity):
            base = [0, 45056][parity]
            ch = chf[parity]
            lastpv = self._fl[parity]["lastpv"]

            @blk.sync
            def _(q):
                if si == 0 and hi_ == 0:
                    self.qbar(q)
                else:
                    q.wait_ge(et, lastpv)
                for hl, src in ((0, self.k_hi), (1, self.k_lo)):
                    ch.dma(q, a16[:, base + hl * 4096:base + hl * 4096 + ndc * T]
                           .rearrange("p (c t) -> p c t", t=T),
                           src.rearrange("b (c p) t -> b p c t", p=128)[
                               si, :, ndc * hi_:ndc * hi_ + ndc, :])
                for hl, src in ((0, self.v_hi), (1, self.v_lo)):
                    ch.dma(q, a16[:, base + 8192 + hl * 4096:
                                  base + 8192 + hl * 4096 + nKc * dh]
                           .rearrange("p (c t) -> p c t", t=dh),
                           src.rearrange("b (c p) m -> b p c m", p=128)[
                               si, :, :, hi_ * dh:(hi_ + 1) * dh])
                for hl, src in ((0, self.q_hi), (1, self.q_lo)):
                    ch.dma(q, a16[:, base + 16384 + hl * 4096:
                                  base + 16384 + hl * 4096 + ndc * T]
                           .rearrange("p (c t) -> p c t", t=T),
                           src.rearrange("b (c p) t -> b p c t", p=128)[
                               si, :, ndc * hi_:ndc * hi_ + ndc, :])
            self._fl[parity]["loads"] = ch.v
            self._fl[parity]["issued"] = True

        if not self._fl[par]["issued"]:
            issue_loads(s, h, par)
        loads = self._fl[par]["loads"]
        self._fl[par]["issued"] = False
        if head_idx + 1 < c["B"] * H:
            nh = head_idx + 1
            issue_loads(nh // H, nh % H, nh % 2)

        kT = lambda hl, dc, t0, w: a16[:, K_ + hl * 4096 + dc * T + t0:
                                       K_ + hl * 4096 + dc * T + t0 + w]
        vT = lambda hl, kc, d0, w: a16[:, V_ + hl * 4096 + kc * dh + d0:
                                       V_ + hl * 4096 + kc * dh + d0 + w]
        qT = lambda hl, dc, t0, w: a16[:, Q_ + hl * 4096 + dc * T + t0:
                                       Q_ + hl * 4096 + dc * T + t0 + w]
        eT = lambda hl, kc: a16[:, (ET_HI if hl == 0 else ET_LO) + kc * 512:
                                (ET_HI if hl == 0 else ET_LO) + kc * 512 + 512]

        first_of_head = True
        for qg in range(nQg):
            split_done = {}
            sum_emitted = 0
            et_free = self._et_free

            def emit_sum(kc):
                svv = split_done[kc]
                kcx = kc

                @blk.tensor
                def _(t):
                    t.wait_ge(ev, svv)
                    if kcx == 0:
                        self.bank_wait(t, "d")
                    t.matmul(self.ps_d[:, 0:512], self.s_on16[:], eT(0, kcx),
                             start=(kcx == 0), stop=False)
                    mm = t.matmul(self.ps_d[:, 0:512], self.s_on16[:],
                                  eT(1, kcx), start=False,
                                  stop=(kcx == nKc - 1))
                    if kcx == nKc - 1:
                        mm.then_inc(et, 1)
                if kcx == nKc - 1:
                    c_et.inc()

            for kc in range(nKc):
                bank = kc % 4
                qgx, kcx, bk = qg, kc, bank
                freev = self._sbank[bank]
                foh = first_of_head
                first_of_head = False

                @blk.tensor
                def _(t):
                    if foh:
                        chf[par].wait(t, loads)
                        t.wait_ge(ev, guard)
                        self.bank_wait(t, "a")
                    t.wait_ge(es, freev)
                    first = True
                    for dc in range(ndc):
                        for (sp, mp) in ((0, 0), (0, 1), (1, 0)):
                            mm = t.matmul(
                                self.ps_a[:, bk * 512:bk * 512 + 512],
                                kT(sp, dc, kcx * 128, 128),
                                qT(mp, dc, qgx * 512, 512),
                                start=first,
                                stop=(dc == ndc - 1 and (sp, mp) == (1, 0)))
                            first = False
                    mm.then_inc(et, 1)
                c_et.inc()
                qkv = c_et.v

                @blk.scalar
                def _(s_):
                    s_.wait_ge(et, qkv)
                    s_.activation(a32[:, bk * 512:bk * 512 + 512],
                                  self.ps_a[:, bk * 512:bk * 512 + 512],
                                  AF.Exp, bias=self.s_sc[:, 13:14],
                                  scale=1.0 / 16.0)
                    s_.drain().then_inc(es, 1)
                c_es.inc()
                self._sbank[bank] = c_es.v
                expv = c_es.v

                @blk.vector
                def _(v):
                    v.wait_ge(es, expv)
                    v.wait_ge(et, et_free)
                    v.tensor_copy(eT(0, kcx), a32[:, bk * 512:bk * 512 + 512])
                    v.drain()
                    v.tensor_tensor(eT(1, kcx),
                                    a32[:, bk * 512:bk * 512 + 512],
                                    eT(0, kcx), ALU.subtract)
                    v.drain().then_inc(ev, 1)
                c_ev.inc()
                split_done[kc] = c_ev.v

                if kc >= 1:
                    emit_sum(kc - 1)
            emit_sum(nKc - 1)
            sum_et = c_et.v

            @blk.vector
            def _(v):
                v.wait_ge(et, sum_et)
                v.reciprocal(a32[:, 2048:2560], self.ps_d[:, 0:512])
                v.drain().then_inc(ev, 1)
            c_ev.inc()
            self.bank_free("d", "ev")
            rcv = c_ev.v

            for dc in range(ndc):
                ps = self.ps_e if dc == 0 else self.ps_d
                bk2 = "e" if dc == 0 else "d"
                psx, dcx = ps, dc
                svv = split_done[nKc - 1]

                @blk.tensor
                def _(t):
                    t.wait_ge(ev, svv)
                    self.bank_wait(t, bk2)
                    first = True
                    for kc in range(nKc):
                        for (sp, mp) in ((0, 0), (0, 1), (1, 0)):
                            mm = t.matmul(psx[:, 0:512],
                                          vT(sp, kc, dcx * 128, 128),
                                          eT(mp, kc),
                                          start=first,
                                          stop=(kc == nKc - 1 and
                                                (sp, mp) == (1, 0)))
                            first = False
                    mm.then_inc(et, 1)
                c_et.inc()
                pvv = c_et.v
                if dc == ndc - 1:
                    self._fl[par]["lastpv"] = c_et.v
                    self._et_free = c_et.v
                slot = qg % 2
                stv = cho[slot].v
                OTB = OTS + slot * 2048

                @blk.vector
                def _(v):
                    v.wait_ge(et, pvv)
                    v.wait_ge(ev, rcv)
                    cho[slot].wait(v, stv)
                    rm = a32[:, 2048:2560]
                    t1 = a32[:, 2560:3072]
                    hi = a16[:, OTB + dcx * 512:OTB + dcx * 512 + 512]
                    lo = a16[:, OTB + 1024 + dcx * 512:
                             OTB + 1024 + dcx * 512 + 512]
                    v.tensor_tensor(hi, psx[:, 0:512], rm, ALU.mult)
                    v.drain()
                    v.tensor_tensor(t1, psx[:, 0:512], rm, ALU.mult)
                    v.drain()
                    v.tensor_tensor(lo, t1, hi, ALU.subtract)
                    v.drain().then_inc(ev, 1)
                c_ev.inc()
                self.bank_free(bk2, "ev")

            ov = c_ev.v
            sx, hx, qgx2, slot2 = s, h, qg, qg % 2
            OTB2 = OTS + slot2 * 2048

            @blk.gpsimd
            def _(g):
                g.wait_ge(ev, ov)
                for hl, dst in ((0, self.oT_hi), (1, self.oT_lo)):
                    cho[slot2].dma(
                        g, dst.rearrange("b (c p) t -> b p c t", p=128)[
                            sx, :, ndc * hx:ndc * hx + ndc,
                            qgx2 * 512:qgx2 * 512 + 512],
                        a16[:, OTB2 + hl * 1024:OTB2 + hl * 1024 + 1024]
                        .rearrange("p (c t) -> p c t", t=512))

    def _ffn_c_state(self):
        if not hasattr(self, "_p4"):
            TOK = self.TOK
            FW = min(1024, TOK)
            self._p4 = dict(i=0, a=0, FW=FW,
                            nTile=(self.cfg["I"] // 128) * (TOK // FW),
                            load_done={}, proc={}, store={}, kv={}, sig={},
                            started=False)
        return self._p4

    def emit_ffn_c_range(self, n):
        st = self._ffn_c_state()
        blk = self.blk
        a32, ai = self.a32, self.ai
        ev, c_ev = self.S("ev"), self.C("ev")
        es, c_es = self.S("es"), self.C("es")
        SL, MXC = self.SL, self.MX
        FW = st["FW"]
        nTile = st["nTile"]
        g_t4 = self.gateT.rearrange("(c p) (n f) -> c n p f", p=128, f=FW)
        nF = self.TOK // FW
        g_t = lambda i: g_t4[i // nF, i % nF]
        chi = [self.chan(f"ld_c{i}") for i in range(3)]
        cho = [self.chan("st_c0"), self.chan("st_c1")]
        load_done, proc_done, store_done = st["load_done"], st["proc"], st["store"]
        kv_done, sig_done = st["kv"], st["sig"]

        if not st["started"]:
            st["started"] = True
            scl = self.scale_ready
            wo_done = getattr(self, "PREP_DONE_wo", 0)

            @blk.gpsimd
            def _(q):
                q.wait_ge(ev, scl)
                q.wait_ge(self.S("es"), wo_done)
                for i in range(min(3, nTile)):
                    load_done[i] = chi[i % 3].dma(
                        q, a32[:, 1024 + (i % 3) * 1024:1024 + (i % 3) * 1024 + FW],
                        g_t(i))

        def emit_A(i):
            gbuf = a32[:, 1024 + (i % 3) * 1024:1024 + (i % 3) * 1024 + FW]
            ki = ai[:, (i % 3) * 1024:(i % 3) * 1024 + FW]
            qv = load_done[i]
            gbx, kix, ix = gbuf, ki, i

            @blk.vector
            def _(v):
                chi[ix % 3].wait(v, qv)
                v.tensor_scalar(kix, gbx, self.s_sc[:, SL["r_g"]:SL["r_g"] + 1],
                                32768.0, ALU.mult, ALU.add)
                v.drain().then_inc(ev, 1)
            c_ev.inc()
            kv_done[i] = c_ev.v
            sig = a32[:, 5120 + (i % 2) * 1024:5120 + (i % 2) * 1024 + FW]
            sfree = proc_done.get(i - 2, 0)
            stfree = store_done.get(i - 2, 0)
            kvv, sgx, slx = kv_done[i], sig, i % 2

            @blk.scalar
            def _(s_):
                s_.wait_ge(ev, kvv)
                s_.wait_ge(ev, sfree)
                cho[slx].wait(s_, stfree)
                s_.activation(sgx, kix, AF.Sigmoid,
                              bias=self.s_sc[:, SL["b_g"]:SL["b_g"] + 1],
                              scale=self.s_sc[:, SL["s_g"]:SL["s_g"] + 1])
                s_.drain().then_inc(es, 1)
            c_es.inc()
            sig_done[i] = c_es.v

            if i + 3 < nTile:
                evv, nt = kv_done[i], i + 3

                @blk.gpsimd
                def _(q):
                    q.wait_ge(ev, evv)
                    load_done[nt] = chi[nt % 3].dma(
                        q, a32[:, 1024 + (nt % 3) * 1024:
                               1024 + (nt % 3) * 1024 + FW], g_t(nt))

        def emit_B(i):
            sl = i % 2
            sig = a32[:, 5120 + sl * 1024:5120 + sl * 1024 + FW]
            sigq = a32[:, 7168:7168 + FW]
            gq = a32[:, 8192:8192 + FW]
            ki = ai[:, (i % 3) * 1024:(i % 3) * 1024 + FW]
            sv = sig_done[i]
            stv = store_done.get(i - 2, 0)
            gsx = sig
            sgx, sqx, gqx, kix, slx, ix = sig, sigq, gq, ki, sl, i

            @blk.vector
            def _(v):
                v.wait_ge(es, sv)
                cho[slx].wait(v, stv)
                v.tensor_scalar(gqx, kix, -32768.0,
                                self.s_sc[:, SL["s_g"]:SL["s_g"] + 1],
                                ALU.add, ALU.mult)
                v.drain()
                v.tensor_scalar(kix, sgx, 65536.0, 0.0, ALU.mult, ALU.add)
                v.drain()
                v.tensor_scalar(sqx, kix, 65535.0, 2.0 ** -16, ALU.min, ALU.mult)
                v.drain()
                v.tensor_tensor(gsx, gqx, sqx, ALU.mult)
                v.drain()
                v.reduce_max(self.s_st[:, MXC["tmp"]:MXC["tmp"] + 1], gsx,
                             axis=AX.X, apply_absolute_value=True)
                v.drain()
                v.tensor_tensor(self.s_st[:, MXC["gs"]:MXC["gs"] + 1],
                                self.s_st[:, MXC["gs"]:MXC["gs"] + 1],
                                self.s_st[:, MXC["tmp"]:MXC["tmp"] + 1], ALU.max)
                v.drain().then_inc(ev, 1)
            c_ev.inc()
            proc_done[i] = c_ev.v
            pv = proc_done[i]

            @blk.gpsimd
            def _(g):
                g.wait_ge(ev, pv)
                cho[slx].dma(g, g_t(ix), gsx)
            store_done[i] = cho[slx].v

        i0 = st["i"]
        i1 = min(i0 + n, nTile)
        st["i"] = i1
        for b in range(i0, i1):
            while st["a"] < min(b + 2, nTile):
                emit_A(st["a"])
                st["a"] += 1
            emit_B(b)

    def _ffn_d_state(self):
        if not hasattr(self, "_p5"):
            TOK = self.TOK
            FW = min(512, TOK)
            self._p5 = dict(i=0, FW=FW,
                            nTile=(self.cfg["I"] // 128) * (TOK // FW),
                            load_done={}, proc={}, store={}, started=False)
        return self._p5

    def emit_ffn_d_range(self, n):
        st = self._ffn_d_state()
        blk = self.blk
        a32, ai = self.a32, self.ai
        ev, c_ev = self.S("ev"), self.C("ev")
        SL, MXC = self.SL, self.MX
        FW = st["FW"]
        nTile = st["nTile"]
        nF = self.TOK // FW
        gs_t4 = self.gateT.rearrange("(c p) (n f) -> c n p f", p=128, f=FW)
        up_t4 = self.upT.rearrange("(c p) (n f) -> c n p f", p=128, f=FW)
        gs_t = lambda i: gs_t4[i // nF, i % nF]
        up_t = lambda i: up_t4[i // nF, i % nF]
        chi = [self.chan(f"ld_d{i}") for i in range(4)]
        cho = [self.chan("st_d0"), self.chan("st_d1")]
        load_done, proc_done, store_done = st["load_done"], st["proc"], st["store"]

        if not st["started"]:
            st["started"] = True
            scl = self.scale_ready
            pc0 = self.chan("st_c0").v
            pc1 = self.chan("st_c1").v

            @blk.gpsimd
            def _(q):
                q.wait_ge(ev, scl)
                self.chan("st_c0").wait(q, pc0)
                self.chan("st_c1").wait(q, pc1)
                for i in range(min(4, nTile)):
                    s4 = i % 4
                    chi[s4].dma(q, a32[:, 1024 + s4 * 512:1024 + s4 * 512 + FW],
                                gs_t(i))
                    load_done[i] = chi[s4].dma(
                        q, a32[:, 3072 + s4 * 512:3072 + s4 * 512 + FW],
                        up_t(i))

        i0 = st["i"]
        i1 = min(i0 + n, nTile)
        st["i"] = i1
        for i in range(i0, i1):
            sl = i % 2
            s4 = i % 4
            gsb = a32[:, 1024 + s4 * 512:1024 + s4 * 512 + FW]
            upb = a32[:, 3072 + s4 * 512:3072 + s4 * 512 + FW]
            g2 = a32[:, 5120:5120 + FW]
            uq = a32[:, 5632:5632 + FW]
            ob = a32[:, 6144 + sl * 512:6144 + sl * 512 + FW]
            ki = ai[:, 0:FW]
            ki2 = ai[:, 1024:1024 + FW]
            qv = load_done[i]
            stv = store_done.get(i - 2, 0)
            gx, ux, g2x, uqx, obx, slx, s4x = gsb, upb, g2, uq, ob, sl, i % 4

            @blk.vector
            def _(v):
                chi[s4x].wait(v, qv)
                cho[slx].wait(v, stv)
                v.tensor_scalar(ki, gx,
                                self.s_sc[:, SL["r_gs"]:SL["r_gs"] + 1],
                                32768.0, ALU.mult, ALU.add)
                v.drain()
                v.tensor_scalar(ki2, ux,
                                self.s_sc[:, SL["r_up"]:SL["r_up"] + 1],
                                32768.0, ALU.mult, ALU.add)
                v.drain()
                v.tensor_scalar(g2x, ki, -32768.0,
                                self.s_sc[:, SL["s_gs"]:SL["s_gs"] + 1],
                                ALU.add, ALU.mult)
                v.drain()
                v.tensor_scalar(uqx, ki2, -32768.0,
                                self.s_sc[:, SL["s_up"]:SL["s_up"] + 1],
                                ALU.add, ALU.mult)
                v.drain()
                v.tensor_tensor(obx, g2x, uqx, ALU.mult)
                v.drain()
                v.reduce_max(self.s_st[:, MXC["tmp"]:MXC["tmp"] + 1], obx,
                             axis=AX.X, apply_absolute_value=True)
                v.drain()
                v.tensor_tensor(self.s_st[:, MXC["o"]:MXC["o"] + 1],
                                self.s_st[:, MXC["o"]:MXC["o"] + 1],
                                self.s_st[:, MXC["tmp"]:MXC["tmp"] + 1], ALU.max)
                v.drain().then_inc(ev, 1)
            c_ev.inc()
            proc_done[i] = c_ev.v

            if i + 4 < nTile:
                evv, nt = proc_done[i], i + 4

                @blk.gpsimd
                def _(q):
                    q.wait_ge(ev, evv)
                    sl4 = nt % 4
                    chi[sl4].dma(q, a32[:, 1024 + sl4 * 512:
                                        1024 + sl4 * 512 + FW], gs_t(nt))
                    load_done[nt] = chi[sl4].dma(
                        q, a32[:, 3072 + sl4 * 512:3072 + sl4 * 512 + FW],
                        up_t(nt))

            pv = proc_done[i]
            ii = i

            @blk.gpsimd
            def _(g):
                g.wait_ge(ev, pv)
                cho[slx].dma(g, up_t(ii), obx)
            store_done[i] = cho[slx].v

    def emit_ffn_down(self):
        blk = self.blk
        c = self.cfg
        D, I, TOK = c["D"], c["I"], self.TOK
        nIc, nDc = I // 128, D // 128
        TW = 256 if TOK >= 512 else TOK
        nG = TOK // TW
        a16, a32, ai = self.a16, self.a32, self.ai
        ev, c_ev = self.S("ev"), self.C("ev")
        es, c_es = self.S("es"), self.C("es")
        et, c_et = self.S("et"), self.C("et")
        SL = self.SL
        o_t = self.upT.rearrange("(c p) n -> c p n", p=128)
        chi = [self.chan("ld_o0"), self.chan("ld_o1")]
        chs = [self.chan("st_dn0"), self.chan("st_dn1")]
        load_done = {}
        qdq_done = {}
        mm_done = {}
        prep_es = self.PREP_DONE_wd
        ch7 = self.chan("ld_fin")
        chx7 = self.chan("ld_px0")
        chout = [self.chan("st_out0"), self.chan("st_out1")]
        x_t7 = self.x.rearrange("(n p) d -> n p d", p=128)
        out_t7 = self.out.rearrange("(n p) d -> n p d", p=128)
        nJ = TW // 128
        f7 = dict(first=True, at0=self.chan("st_at0").v,
                  at1=self.chan("st_at1").v, tr={}, fin={}, ov={},
                  st={}, ld={}, xv={}, chsv={})
        self._fd_first = True

        def emit_lq(g, ics):
            gsl = g % 2
            OQ = 49152 + gsl * 12288
            mmv = mm_done.get(g - 2, 0)
            for ic in ics:
                sl = ic % 2
                ob = a32[:, sl * 512:sl * 512 + TW]
                evv = qdq_done.get((g, ic - 2),
                                   qdq_done.get((g - 1, nIc - 1), 0))
                obx, gx, icx, slx, OQx = ob, g, ic, sl, OQ
                first_l = self._fd_first
                self._fd_first = False

                @blk.sync
                def _(q):
                    if first_l:
                        self.qbar(q)
                        q.wait_ge(es, prep_es)
                    q.wait_ge(ev, evv)
                    load_done[(gx, icx)] = chi[slx].dma(
                        q, obx, o_t[icx, :, gx * TW:gx * TW + TW])

                qv = load_done[(g, ic)]
                mmvx = mmv

                @blk.vector
                def _(v):
                    chi[slx].wait(v, qv)
                    v.wait_ge(et, mmvx)
                    ki = ai[:, 0:TW]
                    qf = a32[:, 1024:1024 + TW]
                    hi = a16[:, OQx + icx * TW:OQx + icx * TW + TW]
                    lo = a16[:, OQx + nIc * TW + icx * TW:
                             OQx + nIc * TW + icx * TW + TW]
                    v.tensor_scalar(ki, obx,
                                    self.s_sc[:, SL["r_o"]:SL["r_o"] + 1],
                                    32768.0, ALU.mult, ALU.add)
                    v.drain()
                    v.tensor_scalar(hi, ki, -32768.0,
                                    self.s_sc[:, SL["s_o"]:SL["s_o"] + 1],
                                    ALU.add, ALU.mult)
                    v.drain()
                    v.tensor_scalar(qf, ki, -32768.0,
                                    self.s_sc[:, SL["s_o"]:SL["s_o"] + 1],
                                    ALU.add, ALU.mult)
                    v.drain()
                    v.tensor_tensor(lo, qf, hi, ALU.subtract)
                    v.drain().then_inc(ev, 1)
                c_ev.inc()
                qdq_done[(g, ic)] = c_ev.v

        def f7_loads(gc):
            off = gc * TW
            v0, v1 = f7["chsv"][gc]
            first_f = f7["first"]
            f7["first"] = False
            at0v, at1v = f7["at0"], f7["at1"]
            trl = f7["tr"].get(gc - 1, 0)
            stp = [f7["st"].get((gc - 1, j), 0) for j in range(2)]
            ovp = f7["ov"].get((gc - 1, nJ - 1), 0)

            @blk.sync
            def _(q):
                if first_f:
                    self.chan("st_at0").wait(q, at0v)
                    self.chan("st_at1").wait(q, at1v)
                chs[0].wait(q, v0)
                chs[1].wait(q, v1)
                q.wait_ge(et, trl)
                chout[0].wait(q, stp[0])
                chout[1].wait(q, stp[1])
                ch7.dma(q, a32[:, 3072:5120].rearrange("p (c t) -> p c t",
                                                       t=TW),
                        self.attT.rearrange("(c p) n -> p c n", p=128)[
                            :, :, off:off + TW])
                ch7.dma(q, a32[:, 5120:7168].rearrange("p (c t) -> p c t",
                                                       t=TW),
                        self.ffnT.rearrange("(c p) n -> p c n", p=128)[
                            :, :, off:off + TW])
            f7["ld"][gc] = ch7.v
            ovpx = ovp

            @blk.sync
            def _(q):
                q.wait_ge(ev, ovpx)
                for j in range(nJ):
                    chx7.dma(q, a32[:, 7168 + j * 1024:7168 + j * 1024 + 1024],
                             x_t7[off // 128 + j])
            f7["xv"][gc] = chx7.v

        def f7_fin(gc):
            ldv = f7["ld"][gc]

            @blk.vector
            def _(v):
                ch7.wait(v, ldv)
                v.tensor_tensor(a32[:, 3072:5120], a32[:, 3072:5120],
                                a32[:, 5120:7168], ALU.add)
                v.drain().then_inc(ev, 1)
            c_ev.inc()
            f7["fin"][gc] = c_ev.v

        def f7_tr(gc):
            fdx = f7["fin"][gc]
            pav = f7["ov"].get((gc - 1, nJ - 1), 0)

            @blk.tensor
            def _(t):
                t.wait_ge(ev, fdx)
                t.wait_ge(ev, pav)
                for j in range(nJ):
                    for dc in range(nDc):
                        t.transpose(
                            self.ps_a[:, j * 1024 + dc * 128:
                                      j * 1024 + dc * 128 + 128],
                            a32[:, 3072 + dc * TW + j * 128:
                                3072 + dc * TW + j * 128 + 128],
                            self.s_idf[:])
                t.drain().then_inc(et, 1)
            c_et.inc()
            f7["tr"][gc] = c_et.v

        def f7_ostg(gc, j):
            off = gc * TW
            trv = f7["tr"][gc]
            xv = f7["xv"][gc]
            stvp = f7["st"].get((gc - 1, j), 0)
            jx = j

            @blk.vector
            def _(v):
                v.wait_ge(et, trv)
                chx7.wait(v, xv)
                chout[jx].wait(v, stvp)
                v.tensor_tensor(a32[:, 5120 + jx * 1024:5120 + jx * 1024 + 1024],
                                self.ps_a[:, jx * 1024:jx * 1024 + 1024],
                                a32[:, 7168 + jx * 1024:7168 + jx * 1024 + 1024],
                                ALU.add)
                v.drain().then_inc(ev, 1)
            c_ev.inc()
            f7["ov"][(gc, j)] = c_ev.v
            ovv = c_ev.v
            tix = off // 128 + j

            @blk.gpsimd
            def _(q):
                q.wait_ge(ev, ovv)
                chout[jx].dma(q, out_t7[tix],
                              a32[:, 5120 + jx * 1024:5120 + jx * 1024 + 1024])
            f7["st"][(gc, j)] = chout[j].v

        emit_lq(0, range(nIc))
        for g in range(nG):
            gsl = g % 2
            OQ = 49152 + gsl * 12288
            if g >= 1:
                f7_loads(g - 1)
            ics_per_dc = nIc // nDc
            for dc in range(nDc):
                bank_key = "d" if dc % 2 == 0 else "e"
                ps = self.ps_d if dc % 2 == 0 else self.ps_e
                evv = qdq_done[(g, nIc - 1)]
                psx, dcx, gx, OQx = ps, dc, g, OQ

                @blk.tensor
                def _(t):
                    t.wait_ge(ev, evv)
                    self.bank_wait(t, bank_key)
                    first_mm = True
                    for ic in range(nIc):
                        for (sp, mp) in ((0, 0), (0, 1), (1, 0)):
                            rhs = a16[:, OQx + mp * nIc * TW + ic * TW:
                                      OQx + mp * nIc * TW + ic * TW + TW]
                            t.matmul(psx[:, 0:TW],
                                     self.lhsT("wd", dcx, ic, sp), rhs,
                                     start=first_mm,
                                     stop=(ic == nIc - 1 and (sp, mp) == (1, 0)))
                            first_mm = False
                    t.drain().then_inc(et, 1)
                c_et.inc()
                mm_done[g] = c_et.v
                mmv2 = c_et.v

                stg = a32[:, 2048 + (dc % 2) * 512:2048 + (dc % 2) * 512 + TW]
                stv = chs[dc % 2].v
                stgx = stg

                @blk.vector
                def _(v):
                    v.wait_ge(et, mmv2)
                    chs[dcx % 2].wait(v, stv)
                    v.tensor_copy(stgx, psx[:, 0:TW])
                    v.drain().then_inc(ev, 1)
                c_ev.inc()
                self.bank_free(bank_key, "ev")
                dv = c_ev.v

                @blk.gpsimd
                def _(g_):
                    g_.wait_ge(ev, dv)
                    chs[dcx % 2].dma(g_, self.ffnT[dcx * 128:dcx * 128 + 128,
                                                   gx * TW:gx * TW + TW],
                                     stgx)

                if g + 1 < nG:
                    emit_lq(g + 1, range(dc * ics_per_dc,
                                         (dc + 1) * ics_per_dc))
                if g >= 1:
                    if dc == 3:
                        f7_fin(g - 1)
                    elif dc == 4:
                        f7_tr(g - 1)
                    elif dc == 5:
                        f7_ostg(g - 1, 0)
                    elif dc == 6 and nJ > 1:
                        f7_ostg(g - 1, 1)
            f7["chsv"][g] = (chs[0].v, chs[1].v)

        gl = nG - 1
        f7_loads(gl)
        f7_fin(gl)
        f7_tr(gl)
        for j in range(nJ):
            f7_ostg(gl, j)

    def _p7a_state(self):
        if not hasattr(self, "_p7a"):
            self._p7a = dict(u=0, started=False)
        return self._p7a

    def emit_combine_mm_range(self, n):
        st = self._p7a_state()
        blk = self.blk
        c = self.cfg
        B, T = c["B"], c["T"]
        HD = self.HD
        nHc = HD // 128
        nDc = c["D"] // 128
        a16, a32 = self.a16, self.a32
        ev, c_ev = self.S("ev"), self.C("ev")
        et, c_et = self.S("et"), self.C("et")
        OG_HI, OG_LO = 32768, 40960
        chg = self.chan("ld_og")
        chs = [self.chan("st_at0"), self.chan("st_at1")]
        units = [(s, qgi) for s in range(B) for qgi in range(T // 512)]
        u0 = st["u"]
        u1 = min(u0 + n, len(units))
        st["u"] = u1

        for u in range(u0, u1):
            s, qgi = units[u]
            first_l = not st["started"]
            st["started"] = True
            mm_prev = st.get("mm_done", 0)
            sx, qx = s, qgi

            @blk.sync
            def _(q):
                if first_l:
                    self.qbar(q)
                    q.wait_ge(self.S("es"), self.PREP_DONE_wo)
                else:
                    q.wait_ge(et, mm_prev)
                for hl, src, dsto in ((0, self.oT_hi, OG_HI),
                                      (1, self.oT_lo, OG_LO)):
                    chg.dma(q, a16[:, dsto:dsto + nHc * 512].rearrange(
                        "p (c t) -> p c t", t=512),
                        src.rearrange("b (c p) t -> b p c t", p=128)[
                            sx, :, :, qx * 512:qx * 512 + 512])
            loads = chg.v

            for dc in range(nDc):
                bank_key = "d" if dc % 2 == 0 else "e"
                ps = self.ps_d if dc % 2 == 0 else self.ps_e
                psx, dcx = ps, dc

                @blk.tensor
                def _(t):
                    chg.wait(t, loads)
                    self.bank_wait(t, bank_key)
                    first_mm = True
                    for hc in range(nHc):
                        for (sp, mp) in ((0, 0), (0, 1), (1, 0)):
                            rhs = a16[:, (OG_HI if mp == 0 else OG_LO) +
                                      hc * 512:
                                      (OG_HI if mp == 0 else OG_LO) +
                                      hc * 512 + 512]
                            mm = t.matmul(psx[:, 0:512],
                                          self.lhsT("wo", dcx, hc, sp), rhs,
                                          start=first_mm,
                                          stop=(hc == nHc - 1 and
                                                (sp, mp) == (1, 0)))
                            first_mm = False
                    mm.then_inc(et, 1)
                c_et.inc()
                st["mm_done"] = c_et.v
                mmv = c_et.v
                stg = a32[:, (dc % 2) * 512:(dc % 2) * 512 + 512]
                stv = chs[dc % 2].v
                stgx = stg

                @blk.vector
                def _(v):
                    v.wait_ge(et, mmv)
                    chs[dcx % 2].wait(v, stv)
                    v.tensor_copy(stgx, psx[:, 0:512])
                    v.drain().then_inc(ev, 1)
                c_ev.inc()
                self.bank_free(bank_key, "ev")
                dv = c_ev.v
                off = (s * T + qgi * 512)

                @blk.gpsimd
                def _(g):
                    g.wait_ge(ev, dv)
                    chs[dcx % 2].dma(g, self.attT[dcx * 128:dcx * 128 + 128,
                                                  off:off + 512], stgx)

    def emit_combine_final(self):
        blk = self.blk
        c = self.cfg
        B, T, D = c["B"], c["T"], c["D"]
        nDc = D // 128
        a32 = self.a32
        ev, c_ev = self.S("ev"), self.C("ev")
        es, c_es = self.S("es"), self.C("es")
        et, c_et = self.S("et"), self.C("et")
        x_t = self.x.rearrange("(n p) d -> n p d", p=128)
        out_t = self.out.rearrange("(n p) d -> n p d", p=128)
        chg = self.chan("ld_fin")
        chxx = [self.chan("ld_px0"), self.chan("ld_px1")]
        chout = [self.chan("st_out0"), self.chan("st_out1")]
        first = True
        grp_done = 0
        xf_free = {}
        pa_free = {}
        stv_prev = {}

        for s in range(B):
            for qgi in range(T // 512):
                sx, qx = s, qgi
                first_l = first
                first = False
                gv = grp_done

                @blk.sync
                def _(q):
                    if first_l:
                        self.qbar(q)
                    else:
                        q.wait_ge(ev, gv)
                    chg.dma(q, a32[:, 0:4096].rearrange("p (c t) -> p c t", t=512),
                            self.attT.rearrange("(c p) n -> p c n", p=128)[
                                :, :, sx * T + qx * 512:sx * T + qx * 512 + 512])
                    chg.dma(q, a32[:, 4096:8192].rearrange("p (c t) -> p c t", t=512),
                            self.ffnT.rearrange("(c p) n -> p c n", p=128)[
                                :, :, sx * T + qx * 512:sx * T + qx * 512 + 512])
                loads = chg.v

                @blk.vector
                def _(v):
                    chg.wait(v, loads)
                    v.tensor_tensor(a32[:, 0:4096], a32[:, 0:4096],
                                    a32[:, 4096:8192], ALU.add)
                    v.drain().then_inc(ev, 1)
                c_ev.inc()
                fin_done = c_ev.v

                for j in range(4):
                    ti = (s * T + qgi * 512) // 128 + j
                    jx, tix = j, ti
                    xev = xf_free.get(j % 2, 0)

                    @blk.sync
                    def _(q):
                        q.wait_ge(ev, max(xev, fin_done))
                        chxx[jx % 2].dma(q, a32[:, 4096 + (jx % 2) * 1024:
                                                4096 + (jx % 2) * 1024 + 1024],
                                         x_t[tix])
                    xload = chxx[j % 2].v

                    pav = pa_free.get(j % 2, 0)

                    @blk.tensor
                    def _(t):
                        t.wait_ge(ev, fin_done)
                        t.wait_ge(ev, pav)
                        psl = (jx % 2) * 1024
                        for dc in range(nDc):
                            t.transpose(
                                self.ps_a[:, psl + dc * 128:psl + dc * 128 + 128],
                                a32[:, dc * 512 + jx * 128:
                                    dc * 512 + jx * 128 + 128],
                                self.s_idf[:])
                        t.drain().then_inc(et, 1)
                    c_et.inc()
                    trv = c_et.v
                    stv = chout[j % 2].v

                    @blk.vector
                    def _(v):
                        v.wait_ge(et, trv)
                        chxx[jx % 2].wait(v, xload)
                        chout[jx % 2].wait(v, stv)
                        psl = (jx % 2) * 1024
                        ostg = a32[:, 6144 + (jx % 2) * 1024:
                                   6144 + (jx % 2) * 1024 + 1024]
                        v.tensor_tensor(ostg, self.ps_a[:, psl:psl + 1024],
                                        a32[:, 4096 + (jx % 2) * 1024:
                                            4096 + (jx % 2) * 1024 + 1024],
                                        ALU.add)
                        v.drain().then_inc(ev, 1)
                    c_ev.inc()
                    xf_free[j % 2] = c_ev.v
                    pa_free[j % 2] = c_ev.v
                    osv = c_ev.v

                    @blk.gpsimd
                    def _(g):
                        g.wait_ge(ev, osv)
                        chout[jx % 2].dma(
                            g, out_t[tix],
                            a32[:, 6144 + (jx % 2) * 1024:
                                6144 + (jx % 2) * 1024 + 1024])
                grp_done = c_ev.v


_CACHE = {}


def _get_nc():
    if "full" not in _CACHE:
        _CACHE["full"] = MK(CFG_FULL).build()
    return _CACHE["full"]


def kernel(L0_attn_x, L0_attn_Wq, L0_attn_Wkv_down, L0_attn_Wk_up,
           L0_attn_Wv_up, L0_attn_Wo, L0_ffn_w_gate, L0_ffn_w_up,
           L0_ffn_w_down):
    x = np.ascontiguousarray(np.asarray(L0_attn_x, dtype=np.float32))
    B_all, T, D = x.shape
    per = B_all // NCORES
    nc = _get_nc()
    idf = np.eye(128, dtype=np.float32)
    idb = np.eye(128, dtype=ml_dtypes.bfloat16)
    common = dict(
        Wq=np.ascontiguousarray(L0_attn_Wq, dtype=np.float32),
        Wkv=np.ascontiguousarray(L0_attn_Wkv_down, dtype=np.float32),
        Wku=np.ascontiguousarray(L0_attn_Wk_up, dtype=np.float32),
        Wvu=np.ascontiguousarray(L0_attn_Wv_up, dtype=np.float32),
        Wo=np.ascontiguousarray(L0_attn_Wo, dtype=np.float32),
        Wg=np.ascontiguousarray(L0_ffn_w_gate, dtype=np.float32),
        Wu=np.ascontiguousarray(L0_ffn_w_up, dtype=np.float32),
        Wd=np.ascontiguousarray(L0_ffn_w_down, dtype=np.float32),
        idf=idf, idb=idb,
    )
    in_maps = []
    for ci in range(NCORES):
        m = dict(common)
        m["x"] = np.ascontiguousarray(
            x[ci * per:(ci + 1) * per].reshape(per * T, D))
        in_maps.append(m)
    res = run_bass_kernel_spmd(nc, in_maps, core_ids=list(range(NCORES)))
    out = np.concatenate(
        [np.asarray(r["out"]).reshape(per, T, D) for r in res.results], axis=0)
    return out


# version 23
# speedup vs baseline: 1.0035x; 1.0035x over previous
import numpy as np
import ml_dtypes
from contextlib import ExitStack

import concourse.bass as bass
import concourse.mybir as mybir
import concourse.bass_utils as _bu
from concourse.bass_utils import run_bass_kernel_spmd

_orig_run_command = _bu.run_command


def _run_command_ldwopt(cmd, *a, **kw):
    if isinstance(cmd, list):
        cmd = ["--enable-ldw-opt=true" if c == "--enable-ldw-opt=false" else c
               for c in cmd]
    return _orig_run_command(cmd, *a, **kw)


_bu.run_command = _run_command_ldwopt

F32 = mybir.dt.float32
BF16 = mybir.dt.bfloat16
I32 = mybir.dt.int32
ALU = mybir.AluOpType
AF = mybir.ActivationFunctionType
AX = mybir.AxisListType

NCORES = 8
CFG_FULL = dict(B=4, T=2048, D=1024, H=8, dh=256, L=64, I=3072)


class Ctr:
    def __init__(self):
        self.v = 0

    def inc(self, n=1):
        self.v += n
        return self.v


class Chan:

    def __init__(self, sem):
        self.sem = sem
        self.c = Ctr()

    def dma(self, q, out, in_):
        q.dma_start(out=out, in_=in_).then_inc(self.sem, 16)
        self.c.inc(16)
        return self.c.v

    @property
    def v(self):
        return self.c.v

    def wait(self, eng, v=None):
        eng.wait_ge(self.sem, self.c.v if v is None else v)


class MK:
    def __init__(self, cfg):
        self.cfg = dict(cfg)
        self.nc = bass.Bass("TRN2", target_bir_lowering=False, debug=False)
        self.bank = {}
        self.stg_store = {}

    def build(self):
        c = self.cfg
        nc = self.nc
        B, T, D, H, dh, L, I = (c["B"], c["T"], c["D"], c["H"], c["dh"],
                                c["L"], c["I"])
        TOK = B * T
        HD = H * dh
        self.TOK, self.HD = TOK, HD

        din = lambda n, s, dt=F32: nc.dram_tensor(n, s, dt, kind="ExternalInput").ap()
        self.x = din("x", [TOK, D])
        self.Wq = din("Wq", [HD, D])
        self.Wkv = din("Wkv", [L, D])
        self.Wku = din("Wku", [HD, L])
        self.Wvu = din("Wvu", [HD, L])
        self.Wo = din("Wo", [D, HD])
        self.Wg = din("Wg", [I, D])
        self.Wu = din("Wu", [I, D])
        self.Wd = din("Wd", [D, I])
        self.idf = din("idf", [128, 128])
        self.idb = din("idb", [128, 128], BF16)
        self.out = nc.dram_tensor("out", [TOK, D], F32, kind="ExternalOutput").ap()

        dbg = getattr(MK, "DEBUG_DUMP", ())

        def dsc(n, s, dt=F32):
            if n in dbg:
                return nc.dram_tensor(n, s, dt, kind="ExternalOutput").ap()
            return nc.dram_tensor(n, s, dt).ap()
        self.xnT_hi = dsc("xnT_hi", [D, TOK], BF16)
        self.xnT_lo = dsc("xnT_lo", [D, TOK], BF16)
        self.xqT_hi = dsc("xqT_hi", [D, TOK], BF16)
        self.xqT_lo = dsc("xqT_lo", [D, TOK], BF16)
        self.q_hi = dsc("q_hi", [B, HD, T], BF16)
        self.q_lo = dsc("q_lo", [B, HD, T], BF16)
        self.k_hi = dsc("k_hi", [B, HD, T], BF16)
        self.k_lo = dsc("k_lo", [B, HD, T], BF16)
        self.v_hi = dsc("v_hi", [B, T, HD], BF16)
        self.v_lo = dsc("v_lo", [B, T, HD], BF16)
        self.oT_hi = dsc("oT_hi", [B, HD, T], BF16)
        self.oT_lo = dsc("oT_lo", [B, HD, T], BF16)
        self.upT = dsc("upT", [I, TOK])
        self.gateT = dsc("gateT", [I, TOK])
        self.ffnT = dsc("ffnT", [D, TOK])
        self.attT = dsc("attT", [D, TOK])
        self.dbg_raw = {}
        for nm in ("up_raw", "gate_raw"):
            if nm in dbg:
                self.dbg_raw[nm] = dsc(nm, [I, TOK])
        self.cc_in = [dsc(f"cc_in{i}", [1, 2]) for i in range(4)]
        self.cc_out = [dsc(f"cc_out{i}", [NCORES, 2]) for i in range(4)]

        self._ctx = ExitStack()
        e = self._ctx.enter_context
        sb = lambda n, s, dt: e(nc.sbuf_tensor(n, s, dt))
        self.s_idf = sb("s_idf", [128, 128], F32)
        self.s_on16 = sb("s_on16", [128, 128], BF16)
        self.s_idb = sb("s_idb", [128, 128], BF16)
        self.s_ones = sb("s_ones", [1, 128], F32)
        self.s_sc = sb("s_sc", [128, 16], F32)
        self.s_st = sb("s_st", [128, 256], F32)
        self.s_cc = sb("s_cc", [1, 32], F32)
        self.s_sr = sb("s_sr", [1, 16], F32)
        self.s_vec = sb("s_vec", [128, 16], F32)
        self.a16 = sb("a16", [128, 78 * 1024], BF16)
        self.a32 = sb("a32", [128, 9216], F32)
        self.ai = sb("ai", [128, 3072], I32)
        self.ps_a = e(nc.psum_tensor("ps_a", [128, 2048], F32))
        self.ps_b = e(nc.psum_tensor("ps_b", [128, 1024], BF16))
        self.ps_c = e(nc.psum_tensor("ps_c", [128, 1024], BF16))
        self.ps_d = e(nc.psum_tensor("ps_d", [128, 512], F32))
        self.ps_e = e(nc.psum_tensor("ps_e", [128, 512], F32))

        self.sems = {}
        for name in ["cc", "ev", "es", "et"]:
            self.sems[name] = (e(nc.semaphore(name)), Ctr())
        self.chans = {}
        self._enter = e
        self.blk = e(nc.Block())

        self.emit_all()
        self._ctx.close()
        return nc

    def S(self, n):
        return self.sems[n][0]

    def C(self, n):
        return self.sems[n][1]

    def chan(self, name):
        if name not in self.chans:
            self.chans[name] = Chan(self._enter(self.nc.semaphore("ch_" + name)))
        return self.chans[name]

    def bank_wait(self, eng, *keys):
        for k in keys:
            if k in self.bank:
                sem_name, val = self.bank[k]
                eng.wait_ge(self.S(sem_name), val)

    def bank_free(self, key, sem_name):
        self.bank[key] = (sem_name, self.C(sem_name).v)

    def qbar(self, q):
        for n in ["ev", "es", "et"]:
            q.wait_ge(self.S(n), self.C(n).v)
        for name, ch in list(self.chans.items()):
            if name.startswith("st_"):
                ch.wait(q)

    SL = dict(s_x=0, r_x=1, s_up=2, r_up=3, s_g=4, r_g=5, b_g=6,
              s_gs=7, r_gs=8, s_o=9, r_o=10)
    MX = dict(tmp=0, xn=1, up=2, gate=3, gs=4, o=5)

    def emit_all(self):
        c = self.cfg
        B = c["B"]
        self.emit_preamble()
        self.emit_ln()
        self.emit_scale_round(0, [("xn", "s_x", "r_x", 32767.0)])
        self.emit_prep_lhsT(self.Wkv, 69632, dequant=False, tag="kvd")
        self.emit_prep_lhsT(self.Wku, 74752, dequant=False, tag="kup")
        self.emit_prep_lhsT(self.Wvu, 70656, dequant=False, tag="vup")
        self.emit_prep_lhsT(self.Wq, 0, dequant=False, tag="wq")
        for s in range(B):
            self.emit_proj_q(s)
        for s in range(B):
            self.emit_proj_kv(s)
        self.emit_xq()
        self.emit_prep_lhsT(self.Wu, 0, dequant=True, tag="wu")
        xq_ev = self.C("ev").v
        wu_es = self.PREP_DONE_wu
        self.emit_ffn_mm(self.upT, "up", "wu")
        up_et = self.C("et").v
        self.emit_prep_lhsT(self.Wg, 0, dequant=True, tag="wg",
                            stage_off=2048, start_ev=xq_ev, start_es=wu_es,
                            copy_et=up_et)
        self.emit_ffn_mm(self.gateT, "gate", "wg")
        self.emit_scale_round(1, [("up", "s_up", "r_up", 32767.0),
                                  ("gate", "s_g", "r_g", 32767.0)])
        for s in range(B):
            for h in range(c["H"]):
                self.emit_flash(s, h)
        self.emit_prep_lhsT(self.Wo, 0, dequant=False, tag="wo")
        nu = B * (c["T"] // 512)
        nc_t = self._ffn_c_state()["nTile"]
        nd_t = self._ffn_d_state()["nTile"]
        for j in range(nu // 2):
            self.emit_combine_mm_range(1)
            self.emit_ffn_c_range(-(-nc_t // (nu // 2)))
        self.emit_ffn_c_range(10 ** 9)
        self.emit_scale_round(2, [("gs", "s_gs", "r_gs", 32767.0)])
        for j in range(nu - nu // 2):
            self.emit_combine_mm_range(1)
            self.emit_ffn_d_range(-(-nd_t // (nu - nu // 2)))
        self.emit_ffn_d_range(10 ** 9)
        self.emit_scale_round(3, [("o", "s_o", "r_o", 32767.0)])
        self.emit_prep_lhsT(self.Wd, 0, dequant=True, tag="wd")
        self.emit_ffn_down()

        chans = list(self.chans.values())

        @self.blk.gpsimd
        def _(g):
            for ch in chans:
                ch.wait(g)

    def emit_preamble(self):
        blk = self.blk
        ev, c_ev = self.S("ev"), self.C("ev")
        ch = self.chan("ld_pre")

        @blk.sync
        def _(q):
            ch.dma(q, self.s_idf[:], self.idf)
            ch.dma(q, self.s_idb[:], self.idb)

        @blk.vector
        def _(v):
            ch.wait(v)
            v.memset(self.s_ones[:], 1.0)
            v.memset(self.s_st[:], -1e30)
            v.memset(self.s_cc[:], 0.0)
            v.memset(self.s_sr[:], 0.0)
            v.memset(self.s_vec[:], 0.0)
            v.drain()
            v.memset(self.s_sc[:, 11:12], 1e-5)
            v.memset(self.s_sc[:, 12:13], 0.0)
            v.memset(self.s_sc[:, 13:14], -3.0)
            v.memset(self.s_on16[:], 1.0)
            v.drain().then_inc(ev, 1)
        c_ev.inc()

    def emit_ln(self):
        blk = self.blk
        D = self.cfg["D"]
        TOK = self.TOK
        nDc = D // 128
        nTt = TOK // 128
        ST = 512
        nST = TOK // ST
        a16, a32 = self.a16, self.a32
        ev, c_ev = self.S("ev"), self.C("ev")
        es, c_es = self.S("es"), self.C("es")
        et, c_et = self.S("et"), self.C("et")
        x_t = self.x.rearrange("(n p) d -> n p d", p=128)
        MXC = self.MX
        chx = [self.chan("ld_x0"), self.chan("ld_x1")]
        chst = self.chan("st_p0")

        free_at = {}
        split_free = {}
        store_done = []
        pre_ev = c_ev.v

        @blk.sync
        def _(q):
            q.wait_ge(ev, pre_ev)
            for i in range(min(2, nTt)):
                chx[i % 2].dma(q, a32[:, (i % 2) * 1024:(i % 2) * 1024 + 1024],
                               x_t[i])

        for sti in range(nST):
            for j in range(ST // 128):
                ti = sti * (ST // 128) + j
                xbuf = a32[:, (ti % 2) * 1024:(ti % 2) * 1024 + 1024]
                d = a32[:, 2048:3072]
                xn = a32[:, 3072:4096]
                sp_h = a16[:, 16384 + (ti % 2) * 1024:16384 + (ti % 2) * 1024 + 1024]
                sp_l = a16[:, 18432 + (ti % 2) * 1024:18432 + (ti % 2) * 1024 + 1024]
                ldv = chx[ti % 2].v

                @blk.vector
                def _(v):
                    chx[ti % 2].wait(v, ldv)
                    v.reduce_sum(self.s_vec[:, 0:1], xbuf, axis=AX.X)
                    v.drain()
                    v.tensor_scalar(self.s_vec[:, 1:2], self.s_vec[:, 0:1],
                                    1.0 / D, None, ALU.mult)
                    v.drain()
                    v.tensor_scalar(d, xbuf, self.s_vec[:, 1:2], None, ALU.subtract)
                    v.drain()
                    v.tensor_tensor(xn, d, d, ALU.mult)
                    v.drain()
                    v.reduce_sum(self.s_vec[:, 2:3], xn, axis=AX.X)
                    v.drain().then_inc(ev, 1)
                c_ev.inc()
                free_at[ti] = c_ev.v

                if ti + 2 < nTt:
                    evv, nt = free_at[ti], ti + 2

                    @blk.sync
                    def _(q):
                        q.wait_ge(ev, evv)
                        chx[nt % 2].dma(q, a32[:, (nt % 2) * 1024:
                                               (nt % 2) * 1024 + 1024], x_t[nt])

                t_ev = c_ev.v

                @blk.scalar
                def _(s):
                    s.wait_ge(ev, t_ev)
                    s.activation(self.s_vec[:, 3:4], self.s_vec[:, 2:3], AF.Ln,
                                 bias=self.s_sc[:, 11:12], scale=1.0 / D)
                    s.drain()
                    s.activation(self.s_vec[:, 4:5], self.s_vec[:, 3:4], AF.Exp,
                                 bias=self.s_sc[:, 12:13], scale=-0.5)
                    s.drain().then_inc(es, 1)
                c_es.inc()

                t_es = c_es.v
                etv = split_free.get(ti - 2, 0)

                @blk.vector
                def _(v):
                    v.wait_ge(es, t_es)
                    v.wait_ge(et, etv)
                    v.tensor_scalar(xn, d, self.s_vec[:, 4:5], None, ALU.mult)
                    v.drain()
                    v.reduce_max(self.s_st[:, MXC["tmp"]:MXC["tmp"] + 1], xn,
                                 axis=AX.X, apply_absolute_value=True)
                    v.drain()
                    v.tensor_tensor(self.s_st[:, MXC["xn"]:MXC["xn"] + 1],
                                    self.s_st[:, MXC["xn"]:MXC["xn"] + 1],
                                    self.s_st[:, MXC["tmp"]:MXC["tmp"] + 1],
                                    ALU.max)
                    v.drain()
                    v.tensor_copy(sp_h, xn)
                    v.drain()
                    v.tensor_tensor(sp_l, xn, sp_h, ALU.subtract)
                    v.drain().then_inc(ev, 1)
                c_ev.inc()

                tv = c_ev.v

                @blk.tensor
                def _(t):
                    t.wait_ge(ev, tv)
                    self.bank_wait(t, "b", "c")
                    for cdx in range(nDc):
                        t.transpose(self.ps_b[:, cdx * 128:cdx * 128 + 128],
                                    sp_h[:, cdx * 128:cdx * 128 + 128],
                                    self.s_idb[:])
                    for cdx in range(nDc):
                        t.transpose(self.ps_c[:, cdx * 128:cdx * 128 + 128],
                                    sp_l[:, cdx * 128:cdx * 128 + 128],
                                    self.s_idb[:])
                    t.drain().then_inc(et, 1)
                c_et.inc()
                split_free[ti] = c_et.v

                t_et = c_et.v
                stv = store_done[sti - 1] if sti >= 1 else 0
                jj = j

                @blk.vector
                def _(v):
                    v.wait_ge(et, t_et)
                    if jj == 0:
                        chst.wait(v, stv)
                    dst_h = a16[:, 0:8 * ST].rearrange(
                        "p (c t) -> p c t", t=ST)[:, :, jj * 128:jj * 128 + 128]
                    dst_l = a16[:, 4096:4096 + 8 * ST].rearrange(
                        "p (c t) -> p c t", t=ST)[:, :, jj * 128:jj * 128 + 128]
                    v.tensor_copy(dst_h,
                                  self.ps_b[:].rearrange("p (c t) -> p c t", t=128))
                    v.drain()
                    v.tensor_copy(dst_l,
                                  self.ps_c[:].rearrange("p (c t) -> p c t", t=128))
                    v.drain().then_inc(ev, 1)
                c_ev.inc()
                self.bank_free("b", "ev")
                self.bank_free("c", "ev")

            t_ev2 = c_ev.v
            sti_ = sti

            @blk.gpsimd
            def _(g):
                g.wait_ge(ev, t_ev2)
                src_h = a16[:, 0:8 * ST].rearrange("p (c t) -> p c t", t=ST)
                src_l = a16[:, 4096:4096 + 8 * ST].rearrange("p (c t) -> p c t", t=ST)
                dst_h = self.xnT_hi.rearrange("(c p) n -> p c n", p=128)[
                    :, :, sti_ * ST:sti_ * ST + ST]
                dst_l = self.xnT_lo.rearrange("(c p) n -> p c n", p=128)[
                    :, :, sti_ * ST:sti_ * ST + ST]
                chst.dma(g, dst_h, src_h)
                chst.dma(g, dst_l, src_l)
            store_done.append(chst.v)

    def emit_scale_round(self, round_i, items):
        blk = self.blk
        cc, c_cc = self.S("cc"), self.C("cc")
        ev, c_ev = self.S("ev"), self.C("ev")
        et, c_et = self.S("et"), self.C("et")
        MXC, SL = self.MX, self.SL
        n = len(items)
        chcc = self.chan("st_cc")

        tv = c_ev.v

        sb = list(getattr(self, "_sbank", []))

        @blk.tensor
        def _(t):
            t.wait_ge(ev, tv)
            self.bank_wait(t, "a")
            for v_ in sb:
                t.wait_ge(self.S("es"), v_)
            for i, (mx, _, _, _) in enumerate(items):
                t.transpose(self.ps_a[0:1, i * 128:(i + 1) * 128],
                            self.s_st[:, MXC[mx]:MXC[mx] + 1], self.s_idf[:])
            t.drain().then_inc(et, 1)
        c_et.inc()

        tev = c_et.v

        @blk.vector
        def _(v):
            v.wait_ge(et, tev)
            for i in range(n):
                v.reduce_max(self.s_cc[0:1, i:i + 1],
                             self.ps_a[0:1, i * 128:(i + 1) * 128], axis=AX.X)
                v.drain()
            v.drain().then_inc(ev, 1)
        c_ev.inc()
        self.bank_free("a", "ev")
        self.psa_guard = c_ev.v
        ccv = c_ev.v

        @blk.gpsimd
        def _(g):
            g.wait_ge(ev, ccv)
            chcc.dma(g, self.cc_in[round_i][0:1, 0:2], self.s_cc[0:1, 0:2])
            chcc.wait(g)
            g.collective_compute(
                "AllGather", ALU.bypass,
                replica_groups=[list(range(NCORES))],
                ins=[self.cc_in[round_i].opt()],
                outs=[self.cc_out[round_i].opt()],
            ).then_inc(cc, 1)
            c_cc.inc()
            g.wait_ge(cc, c_cc.v)
            chcc.dma(g, self.s_cc[0:1, 8:8 + 2 * NCORES],
                     self.cc_out[round_i].flatten().unsqueeze(0))
        gatherv = chcc.v

        @blk.vector
        def _(v):
            chcc.wait(v, gatherv)
            for i, (mx, s_slot, r_slot, qmax) in enumerate(items):
                gathered = self.s_cc[0:1, 8:8 + 2 * NCORES].rearrange(
                    "p (c k) -> p c k", k=2)[:, :, i:i + 1].transpose([0, 2, 1])
                v.reduce_max(self.s_sr[0:1, 2 * i:2 * i + 1], gathered, axis=AX.X)
                v.drain()
                v.tensor_scalar(self.s_sr[0:1, 2 * i:2 * i + 1],
                                self.s_sr[0:1, 2 * i:2 * i + 1],
                                1.0 / qmax, 1e-12, ALU.mult, ALU.max)
                v.drain()
                v.reciprocal(self.s_sr[0:1, 2 * i + 1:2 * i + 2],
                             self.s_sr[0:1, 2 * i:2 * i + 1])
                v.drain()
            v.drain().then_inc(ev, 1)
        c_ev.inc()
        bv = c_ev.v

        @blk.tensor
        def _(t):
            t.wait_ge(ev, bv)
            self.bank_wait(t, "d")
            t.matmul(self.ps_d[:, 0:2 * n], self.s_ones[:],
                     self.s_sr[0:1, 0:2 * n], start=True, stop=True)
            t.drain().then_inc(et, 1)
        c_et.inc()
        mv = c_et.v

        @blk.vector
        def _(v):
            v.wait_ge(et, mv)
            for i, (mx, s_slot, r_slot, qmax) in enumerate(items):
                v.tensor_copy(self.s_sc[:, SL[s_slot]:SL[s_slot] + 1],
                              self.ps_d[:, 2 * i:2 * i + 1])
                v.drain()
                v.tensor_copy(self.s_sc[:, SL[r_slot]:SL[r_slot] + 1],
                              self.ps_d[:, 2 * i + 1:2 * i + 2])
                v.drain()
                if s_slot == "s_g":
                    v.tensor_scalar(self.s_sc[:, SL["b_g"]:SL["b_g"] + 1],
                                    self.s_sc[:, SL["s_g"]:SL["s_g"] + 1],
                                    -32768.0, None, ALU.mult)
                    v.drain()
            v.drain().then_inc(ev, 1)
        c_ev.inc()
        self.bank_free("d", "ev")
        self.scale_ready = c_ev.v

    def emit_prep_lhsT(self, W, off, dequant, tag,
                       stage_off=0, start_ev=None, start_es=None,
                       copy_et=None):
        blk = self.blk
        rows, cols = W.shape
        rp = min(128, rows)
        cw = min(128, cols)
        nMt = (rows + 127) // 128
        nCc = (cols + cw - 1) // cw
        half = nMt * nCc * rp
        setattr(self, "OFF_" + tag, (off, half, rows, cols))
        ev, c_ev = self.S("ev"), self.C("ev")
        es, c_es = self.S("es"), self.C("es")
        et, c_et = self.S("et"), self.C("et")
        a16, a32, ai = self.a16, self.a32, self.ai
        SPL_HI, SPL_LO = 49152, 52224
        w_t = W.rearrange("(n p) d -> n p d", p=128) if rows >= 128 else None
        chw = [self.chan("ld_w0"), self.chan("ld_w1")]
        load_done = {}
        split_done = {}
        tr_done = {}

        @blk.sync
        def _(q):
            if start_ev is None:
                self.qbar(q)
            else:
                q.wait_ge(ev, start_ev)
                q.wait_ge(es, start_es)
            for i in range(min(2, nMt)):
                src = w_t[i] if rows >= 128 else W
                load_done[i] = chw[i % 2].dma(
                    q, a32[0:rp, stage_off + (i % 2) * cols:
                           stage_off + (i % 2) * cols + cols], src)

        first_copy = [copy_et]
        for mt in range(nMt):
            wb = a32[0:rp, stage_off + (mt % 2) * cols:
                     stage_off + (mt % 2) * cols + cols]
            hi = a16[0:rp, SPL_HI:SPL_HI + cols]
            lo = a16[0:rp, SPL_LO:SPL_LO + cols]

            qv = load_done[mt]
            etv = tr_done.get(mt - 1, 0)
            mtx = mt

            @blk.vector
            def _(v):
                chw[mtx % 2].wait(v, qv)
                v.wait_ge(et, etv)
                if dequant:
                    nblk = cols // 32
                    bm = self.s_st[:, 64:64 + nblk]
                    rcp = self.s_st[:, 160:160 + nblk]
                    w3 = wb.rearrange("p (b k) -> p b k", k=32)
                    v.reduce_max(bm[0:rp, :], w3, axis=AX.X,
                                 apply_absolute_value=True)
                    v.drain()
                    v.tensor_scalar(bm[0:rp, :], bm[0:rp, :], 1.0 / 7.0, 1e-12,
                                    ALU.mult, ALU.max)
                    v.drain()
                    v.reciprocal(rcp[0:rp, :], bm[0:rp, :])
                    v.drain()
                    tmp = a32[0:rp, 6144:6144 + cols]
                    v.tensor_tensor(tmp.rearrange("p (b k) -> p b k", k=32), w3,
                                    rcp[0:rp, :].unsqueeze(2).broadcast_to(
                                        [rp, nblk, 32]), ALU.mult)
                    v.drain()
                    ki = ai[0:rp, 0:cols]
                    v.tensor_scalar(ki, tmp, 1.0, 8.0, ALU.mult, ALU.add)
                    v.drain()
                    v.tensor_scalar(tmp, ki, -8.0, None, ALU.add)
                    v.drain()
                    v.tensor_tensor(tmp.rearrange("p (b k) -> p b k", k=32),
                                    tmp.rearrange("p (b k) -> p b k", k=32),
                                    bm[0:rp, :].unsqueeze(2).broadcast_to(
                                        [rp, nblk, 32]), ALU.mult)
                    v.drain()
                    src = tmp
                else:
                    src = wb
                v.tensor_copy(hi, src)
                v.drain()
                v.tensor_tensor(lo, src, hi, ALU.subtract)
                v.drain().then_inc(ev, 1)
            c_ev.inc()
            split_done[mt] = c_ev.v

            if mt + 2 < nMt:
                evv, nt = split_done[mt], mt + 2

                @blk.sync
                def _(q):
                    q.wait_ge(ev, evv)
                    load_done[nt] = chw[nt % 2].dma(
                        q, a32[0:rp, stage_off + (nt % 2) * cols:
                               stage_off + (nt % 2) * cols + cols],
                        w_t[nt])

            per_bank = 1024 // rp
            for hl in range(2):
                ps = self.ps_b if hl == 0 else self.ps_c
                bank_key = "b" if hl == 0 else "c"
                srcbuf = hi if hl == 0 else lo
                for bs in range(0, nCc, per_bank):
                    nb = min(per_bank, nCc - bs)
                    sv = split_done[mt]
                    bsc, nbc, psx, srcx = bs, nb, ps, srcbuf

                    @blk.tensor
                    def _(t):
                        t.wait_ge(ev, sv)
                        self.bank_wait(t, bank_key)
                        for k in range(nbc):
                            t.transpose(psx[0:cw, k * rp:k * rp + rp],
                                        srcx[:, (bsc + k) * cw:(bsc + k) * cw + cw],
                                        self.s_idb[0:rp, 0:rp])
                        t.drain().then_inc(et, 1)
                    c_et.inc()
                    tev = c_et.v
                    mtc, hlc = mt, hl

                    cw_et = first_copy[0]
                    first_copy[0] = None

                    @blk.scalar
                    def _(s):
                        if cw_et is not None:
                            s.wait_ge(et, cw_et)
                        s.wait_ge(et, tev)
                        dst = a16[0:cw,
                                  off + hlc * half + (mtc * nCc + bsc) * rp:
                                  off + hlc * half + (mtc * nCc + bsc) * rp +
                                  nbc * rp]
                        s.copy(dst, psx[0:cw, 0:nbc * rp])
                        s.drain().then_inc(es, 1)
                    c_es.inc()
                    self.bank_free(bank_key, "es")
            tr_done[mt] = c_et.v
        setattr(self, "PREP_DONE_" + tag, c_es.v)

    def lhsT(self, tag, mt, cc_, hl):
        off, half, rows, cols = getattr(self, "OFF_" + tag)
        rp = min(128, rows)
        cw = min(128, cols)
        nCc = (cols + cw - 1) // cw
        base = off + hl * half + (mt * nCc + cc_) * rp
        return self.a16[0:cw, base:base + rp]

    def emit_proj_q(self, s):
        blk = self.blk
        c = self.cfg
        T, D, H, dh, L = c["T"], c["D"], c["H"], c["dh"], c["L"]
        nDc = D // 128
        nTg = T // 512
        a16, a32 = self.a16, self.a32
        ev, c_ev = self.S("ev"), self.C("ev")
        et, c_et = self.S("et"), self.C("et")

        XN_HI = 32768
        XN_LO = 32768 + nDc * T
        KV_HI = 65536
        KV_LO = 65536 + T
        STG_Q = (65536, 66560)
        STG_KV = (0, 1024)
        chxn = self.chan("ld_xn")

        @blk.sync
        def _(q):
            self.qbar(q)
            for src, dst in [(self.xnT_hi, XN_HI), (self.xnT_lo, XN_LO)]:
                chxn.dma(q, a16[:, dst:dst + nDc * T].rearrange(
                    "p (c t) -> p c t", t=T),
                    src.rearrange("(c p) n -> p c n", p=128)[
                        :, :, s * T:(s + 1) * T])
        xn_ready = chxn.v

        xnt = lambda hl, dc, t0, w: a16[:, (XN_HI if hl == 0 else XN_LO) +
                                        dc * T + t0:
                                        (XN_HI if hl == 0 else XN_LO) +
                                        dc * T + t0 + w]

        for h in range(H):
            for dc2 in range(dh // 128):
                for tg in range(nTg):
                    hh, dd, tt = h, dc2, tg

                    psx, bk = ((self.ps_d, "d") if self._ess_i % 2 == 0
                               else (self.ps_e, "e"))

                    @blk.tensor
                    def _(t):
                        chxn.wait(t, xn_ready)
                        self.bank_wait(t, bk)
                        mi = hh * (dh // 128) + dd
                        first = True
                        for dc in range(nDc):
                            for (sp, mp) in ((0, 0), (0, 1), (1, 0)):
                                t.matmul(psx[:, 0:512],
                                         self.lhsT("wq", mi, dc, sp),
                                         xnt(mp, dc, tt * 512, 512),
                                         start=first,
                                         stop=(dc == nDc - 1 and
                                               (sp, mp) == (1, 0)))
                                first = False
                        t.drain().then_inc(et, 1)
                    c_et.inc()
                    self._evac_split_store(
                        bank_key=bk, ps=psx, stg=STG_Q, n=512,
                        dst_hi=self.q_hi[s, h * dh + dc2 * 128:
                                         h * dh + dc2 * 128 + 128,
                                         tg * 512:tg * 512 + 512],
                        dst_lo=self.q_lo[s, h * dh + dc2 * 128:
                                         h * dh + dc2 * 128 + 128,
                                         tg * 512:tg * 512 + 512])


    def emit_proj_kv(self, s):
        blk = self.blk
        c = self.cfg
        T, D, H, dh, L = c["T"], c["D"], c["H"], c["dh"], c["L"]
        nDc = D // 128
        nTg = T // 512
        a16, a32 = self.a16, self.a32
        ev, c_ev = self.S("ev"), self.C("ev")
        et, c_et = self.S("et"), self.C("et")

        XN_HI = 32768
        XN_LO = 32768 + nDc * T
        KV_HI = 65536
        KV_LO = 65536 + T
        STG_Q = (65536, 66560)
        STG_KV = (0, 1024)
        chxn = self.chan("ld_xn")

        @blk.sync
        def _(q):
            self.qbar(q)
            for src, dst in [(self.xnT_hi, XN_HI), (self.xnT_lo, XN_LO)]:
                chxn.dma(q, a16[:, dst:dst + nDc * T].rearrange(
                    "p (c t) -> p c t", t=T),
                    src.rearrange("(c p) n -> p c n", p=128)[
                        :, :, s * T:(s + 1) * T])
        xn_ready = chxn.v

        xnt = lambda hl, dc, t0, w: a16[:, (XN_HI if hl == 0 else XN_LO) +
                                        dc * T + t0:
                                        (XN_HI if hl == 0 else XN_LO) +
                                        dc * T + t0 + w]

        for tg in range(nTg):
            tgc = tg

            @blk.tensor
            def _(t):
                chxn.wait(t, xn_ready)
                self.bank_wait(t, "d")
                first = True
                for dc in range(nDc):
                    for (sp, mp) in ((0, 0), (0, 1), (1, 0)):
                        t.matmul(self.ps_d[0:64, 0:512],
                                 self.lhsT("kvd", 0, dc, sp),
                                 xnt(mp, dc, tgc * 512, 512),
                                 start=first, stop=(dc == nDc - 1 and
                                                    (sp, mp) == (1, 0)))
                        first = False
                t.drain().then_inc(et, 1)
            c_et.inc()
            mmv = c_et.v

            qs0 = self.chan("st_ess0").v
            qs1 = self.chan("st_ess1").v

            @blk.vector
            def _(v):
                v.wait_ge(et, mmv)
                self.chan("st_ess0").wait(v, qs0)
                self.chan("st_ess1").wait(v, qs1)
                kvf = a32[0:64, 0:512]
                v.tensor_copy(kvf, self.ps_d[0:64, 0:512])
                v.drain()
                v.tensor_copy(a16[0:64, KV_HI + tgc * 512:KV_HI + tgc * 512 + 512],
                              kvf)
                v.drain()
                v.tensor_tensor(a16[0:64, KV_LO + tgc * 512:KV_LO + tgc * 512 + 512],
                                kvf,
                                a16[0:64, KV_HI + tgc * 512:KV_HI + tgc * 512 + 512],
                                ALU.subtract)
                v.drain().then_inc(ev, 1)
            c_ev.inc()
            self.bank_free("d", "ev")
        kv_ready = c_ev.v

        kvt = lambda hl, t0, w: a16[0:64, (KV_HI if hl == 0 else KV_LO) +
                                    t0:(KV_HI if hl == 0 else KV_LO) + t0 + w]

        for h in range(H):
            for dc2 in range(dh // 128):
                for tg in range(nTg):
                    hh, dd, tt = h, dc2, tg

                    psx, bk = ((self.ps_d, "d") if self._ess_i % 2 == 0
                               else (self.ps_e, "e"))

                    @blk.tensor
                    def _(t):
                        t.wait_ge(ev, kv_ready)
                        self.bank_wait(t, bk)
                        mi = hh * (dh // 128) + dd
                        first = True
                        for (sp, mp) in ((0, 0), (0, 1), (1, 0)):
                            t.matmul(psx[:, 0:512],
                                     self.lhsT("kup", mi, 0, sp),
                                     kvt(mp, tt * 512, 512),
                                     start=first, stop=((sp, mp) == (1, 0)))
                            first = False
                        t.drain().then_inc(et, 1)
                    c_et.inc()
                    self._evac_split_store(
                        bank_key=bk, ps=psx, stg=STG_KV, n=512,
                        dst_hi=self.k_hi[s, h * dh + dc2 * 128:
                                         h * dh + dc2 * 128 + 128,
                                         tg * 512:tg * 512 + 512],
                        dst_lo=self.k_lo[s, h * dh + dc2 * 128:
                                         h * dh + dc2 * 128 + 128,
                                         tg * 512:tg * 512 + 512])

        nTc = T // 128
        voff, vhalf, _, _ = getattr(self, "OFF_vup")
        for tc in range(nTc):
            for hg in range(H * dh // 512):
                tcc, hgc = tc, hg

                psx, bk = ((self.ps_d, "d") if self._ess_i % 2 == 0
                           else (self.ps_e, "e"))

                @blk.tensor
                def _(t):
                    t.wait_ge(ev, kv_ready)
                    self.bank_wait(t, bk)
                    first = True
                    for (sp, mp) in ((0, 0), (0, 1), (1, 0)):
                        rhs = a16[0:64, voff + mp * vhalf + hgc * 512:
                                  voff + mp * vhalf + hgc * 512 + 512]
                        t.matmul(psx[:, 0:512],
                                 kvt(sp, tcc * 128, 128), rhs,
                                 start=first, stop=((sp, mp) == (1, 0)))
                        first = False
                    t.drain().then_inc(et, 1)
                c_et.inc()
                self._evac_split_store(
                    bank_key=bk, ps=psx, stg=STG_KV, n=512,
                    dst_hi=self.v_hi[s, tc * 128:tc * 128 + 128,
                                     hg * 512:hg * 512 + 512],
                    dst_lo=self.v_lo[s, tc * 128:tc * 128 + 128,
                                     hg * 512:hg * 512 + 512])


    _ess_i = 0

    def _evac_split_store(self, bank_key, stg, n, dst_hi, dst_lo, ps=None):
        blk = self.blk
        a16 = self.a16
        ev, c_ev = self.S("ev"), self.C("ev")
        slot = self._ess_i % 2
        self._ess_i += 1
        ch = self.chan(f"st_ess{slot}")
        ps = self.ps_d if ps is None else ps
        base = stg[slot]
        hi = a16[:, base:base + n]
        lo = a16[:, base + 512:base + 512 + n]
        tv = self.C("et").v
        stv = ch.v

        @blk.vector
        def _(v):
            v.wait_ge(self.S("et"), tv)
            ch.wait(v, stv)
            v.tensor_copy(hi, ps[:, 0:n])
            v.drain()
            v.tensor_tensor(lo, ps[:, 0:n], hi, ALU.subtract)
            v.drain().then_inc(ev, 1)
        c_ev.inc()
        self.bank_free(bank_key, "ev")
        sv = c_ev.v

        @blk.gpsimd
        def _(g):
            g.wait_ge(ev, sv)
            ch.dma(g, dst_hi, hi)
            ch.dma(g, dst_lo, lo)

    def emit_xq(self):
        blk = self.blk
        D, TOK = self.cfg["D"], self.TOK
        a16, a32, ai = self.a16, self.a32, self.ai
        ev, c_ev = self.S("ev"), self.C("ev")
        SL = self.SL
        FW = min(1024, TOK)
        nTile = (D // 128) * (TOK // FW)
        nF = TOK // FW
        _t4 = lambda ap: ap.rearrange("(c p) (n f) -> c n p f", p=128, f=FW)
        hi_t4, lo_t4 = _t4(self.xnT_hi), _t4(self.xnT_lo)
        qhi_t4, qlo_t4 = _t4(self.xqT_hi), _t4(self.xqT_lo)
        hi_t = lambda i: hi_t4[i // nF, i % nF]
        lo_t = lambda i: lo_t4[i // nF, i % nF]
        qhi_t = lambda i: qhi_t4[i // nF, i % nF]
        qlo_t = lambda i: qlo_t4[i // nF, i % nF]
        IN_HI, IN_LO, ST_HI, ST_LO = 55296, 59392, 63488, 67584
        chi = [self.chan("ld_xq0"), self.chan("ld_xq1")]
        cho = [self.chan("st_xq0"), self.chan("st_xq1")]
        load_done = {}
        proc_done = {}
        scl = self.scale_ready

        @blk.sync
        def _(q):
            self.qbar(q)
            q.wait_ge(ev, scl)
            for i in range(min(2, nTile)):
                chi[i % 2].dma(q, a16[:, IN_HI + (i % 2) * FW:
                                      IN_HI + (i % 2) * FW + FW], hi_t(i))
                load_done[i] = chi[i % 2].dma(
                    q, a16[:, IN_LO + (i % 2) * FW:IN_LO + (i % 2) * FW + FW],
                    lo_t(i))

        for i in range(nTile):
            sl = i % 2
            qv = load_done[i]
            stv = cho[sl].v
            slx = sl

            @blk.vector
            def _(v):
                chi[slx].wait(v, qv)
                cho[slx].wait(v, stv)
                xh = a16[:, IN_HI + slx * FW:IN_HI + slx * FW + FW]
                xl = a16[:, IN_LO + slx * FW:IN_LO + slx * FW + FW]
                xf = a32[:, 0:FW]
                qf = a32[:, 1024:1024 + FW]
                ki = ai[:, 0:FW]
                sh = a16[:, ST_HI + slx * FW:ST_HI + slx * FW + FW]
                slo = a16[:, ST_LO + slx * FW:ST_LO + slx * FW + FW]
                v.tensor_tensor(xf, xh, xl, ALU.add)
                v.drain()
                v.tensor_scalar(ki, xf, self.s_sc[:, SL["r_x"]:SL["r_x"] + 1],
                                32768.0, ALU.mult, ALU.add)
                v.drain()
                v.tensor_scalar(sh, ki, -32768.0,
                                self.s_sc[:, SL["s_x"]:SL["s_x"] + 1],
                                ALU.add, ALU.mult)
                v.drain()
                v.tensor_scalar(qf, ki, -32768.0,
                                self.s_sc[:, SL["s_x"]:SL["s_x"] + 1],
                                ALU.add, ALU.mult)
                v.drain()
                v.tensor_tensor(slo, qf, sh, ALU.subtract)
                v.drain().then_inc(ev, 1)
            c_ev.inc()
            proc_done[i] = c_ev.v

            if i + 2 < nTile:
                evv, nt = proc_done[i], i + 2

                @blk.sync
                def _(q):
                    q.wait_ge(ev, evv)
                    chi[nt % 2].dma(q, a16[:, IN_HI + (nt % 2) * FW:
                                           IN_HI + (nt % 2) * FW + FW],
                                    hi_t(nt))
                    load_done[nt] = chi[nt % 2].dma(
                        q, a16[:, IN_LO + (nt % 2) * FW:IN_LO + (nt % 2) * FW + FW],
                        lo_t(nt))

            pv = proc_done[i]
            ii = i

            @blk.gpsimd
            def _(g):
                g.wait_ge(ev, pv)
                cho[slx].dma(g, qhi_t(ii),
                             a16[:, ST_HI + slx * FW:ST_HI + slx * FW + FW])
                cho[slx].dma(g, qlo_t(ii),
                             a16[:, ST_LO + slx * FW:ST_LO + slx * FW + FW])

    def emit_ffn_mm(self, dstT, mx_name, tag):
        blk = self.blk
        c = self.cfg
        D, I, TOK = c["D"], c["I"], self.TOK
        nDc, nIc, nTg = D // 128, I // 128, TOK // 512
        a16, a32 = self.a16, self.a32
        ev, c_ev = self.S("ev"), self.C("ev")
        es, c_es = self.S("es"), self.C("es")
        et, c_et = self.S("et"), self.C("et")
        MXC = self.MX
        XQ = 55296
        qhi = self.xqT_hi.rearrange("(c p) n -> p c n", p=128)
        qlo = self.xqT_lo.rearrange("(c p) n -> p c n", p=128)
        chq = [self.chan("ld_fq0"), self.chan("ld_fq1")]
        chs = [self.chan("st_fs0"), self.chan("st_fs1")]
        load_done = {}
        tg_mm = {}
        prep_es = getattr(self, "PREP_DONE_" + tag)

        @blk.sync
        def _(q):
            self.qbar(q)
            q.wait_ge(es, prep_es)
            for i in range(min(2, nTg)):
                sl = i % 2
                chq[sl].dma(q, a16[:, XQ + sl * 8192:XQ + sl * 8192 + 4096]
                            .rearrange("p (c t) -> p c t", t=512),
                            qhi[:, :, i * 512:i * 512 + 512])
                load_done[i] = chq[sl].dma(
                    q, a16[:, XQ + sl * 8192 + 4096:XQ + sl * 8192 + 8192]
                    .rearrange("p (c t) -> p c t", t=512),
                    qlo[:, :, i * 512:i * 512 + 512])

        for tg in range(nTg):
            sl = tg % 2
            for ic in range(nIc):
                bank_key = "d" if ic % 2 == 0 else "e"
                ps = self.ps_d if ic % 2 == 0 else self.ps_e
                qv = load_done[tg]
                slx, icx, tgx, psx = sl, ic, tg, ps

                @blk.tensor
                def _(t):
                    chq[slx].wait(t, qv)
                    self.bank_wait(t, bank_key)
                    first = True
                    for dc in range(nDc):
                        for (sp, mp) in ((0, 0), (0, 1), (1, 0)):
                            rhs = a16[:, XQ + slx * 8192 + mp * 4096 + dc * 512:
                                      XQ + slx * 8192 + mp * 4096 + dc * 512 + 512]
                            t.matmul(psx[:, 0:512],
                                     self.lhsT(tag, icx, dc, sp), rhs,
                                     start=first,
                                     stop=(dc == nDc - 1 and (sp, mp) == (1, 0)))
                            first = False
                    t.drain().then_inc(et, 1)
                c_et.inc()
                mmv = c_et.v

                stg = a32[:, (ic % 2) * 512:(ic % 2) * 512 + 512]
                stv = chs[ic % 2].v
                stgx = stg

                @blk.vector
                def _(v):
                    v.wait_ge(et, mmv)
                    chs[icx % 2].wait(v, stv)
                    v.tensor_copy(stgx, psx[:, 0:512])
                    v.drain()
                    v.reduce_max(self.s_st[:, MXC["tmp"]:MXC["tmp"] + 1], stgx,
                                 axis=AX.X, apply_absolute_value=True)
                    v.drain()
                    v.tensor_tensor(self.s_st[:, MXC[mx_name]:MXC[mx_name] + 1],
                                    self.s_st[:, MXC[mx_name]:MXC[mx_name] + 1],
                                    self.s_st[:, MXC["tmp"]:MXC["tmp"] + 1],
                                    ALU.max)
                    v.drain().then_inc(ev, 1)
                c_ev.inc()
                self.bank_free(bank_key, "ev")
                dv = c_ev.v

                dbg_t = self.dbg_raw.get(mx_name + "_raw")

                @blk.gpsimd
                def _(g):
                    g.wait_ge(ev, dv)
                    chs[icx % 2].dma(g, dstT[icx * 128:icx * 128 + 128,
                                             tgx * 512:tgx * 512 + 512], stgx)
                    if dbg_t is not None:
                        chs[icx % 2].dma(g, dbg_t[icx * 128:icx * 128 + 128,
                                                  tgx * 512:tgx * 512 + 512],
                                         stgx)
            tg_mm[tg] = c_et.v

            if tg + 2 < nTg:
                etv, nt = tg_mm[tg], tg + 2

                @blk.sync
                def _(q):
                    q.wait_ge(et, etv)
                    sl2 = nt % 2
                    chq[sl2].dma(q, a16[:, XQ + sl2 * 8192:XQ + sl2 * 8192 + 4096]
                                 .rearrange("p (c t) -> p c t", t=512),
                                 qhi[:, :, nt * 512:nt * 512 + 512])
                    load_done[nt] = chq[sl2].dma(
                        q, a16[:, XQ + sl2 * 8192 + 4096:XQ + sl2 * 8192 + 8192]
                        .rearrange("p (c t) -> p c t", t=512),
                        qlo[:, :, nt * 512:nt * 512 + 512])

    def emit_flash(self, s, h):
        blk = self.blk
        c = self.cfg
        T, dh, H = c["T"], c["dh"], c["H"]
        nKc = T // 128
        nQg = T // 512
        ndc = dh // 128
        a16, a32 = self.a16, self.a32
        ev, c_ev = self.S("ev"), self.C("ev")
        es, c_es = self.S("es"), self.C("es")
        et, c_et = self.S("et"), self.C("et")
        ET_HI, ET_LO, OTS = 24576, 32768, 40960
        head_idx = s * H + h
        par = head_idx % 2
        BASE = [0, 45056][par]
        K_, V_, Q_ = BASE, BASE + 8192, BASE + 16384
        chf = [self.chan("ld_fl0"), self.chan("ld_fl1")]
        cho = [self.chan("st_fl0"), self.chan("st_fl1")]
        guard = getattr(self, "psa_guard", 0)
        if not hasattr(self, "_fl"):
            ev0 = self.C("et").v
            self._fl = {0: dict(loads=0, lastpv=ev0, issued=False),
                        1: dict(loads=0, lastpv=ev0, issued=False)}
            self._et_free = 0
            self._sbank = [0, 0, 0, 0]

        def issue_loads(si, hi_, parity):
            base = [0, 45056][parity]
            ch = chf[parity]
            lastpv = self._fl[parity]["lastpv"]

            @blk.sync
            def _(q):
                if si == 0 and hi_ == 0:
                    self.qbar(q)
                else:
                    q.wait_ge(et, lastpv)
                for hl, src in ((0, self.k_hi), (1, self.k_lo)):
                    ch.dma(q, a16[:, base + hl * 4096:base + hl * 4096 + ndc * T]
                           .rearrange("p (c t) -> p c t", t=T),
                           src.rearrange("b (c p) t -> b p c t", p=128)[
                               si, :, ndc * hi_:ndc * hi_ + ndc, :])
                for hl, src in ((0, self.v_hi), (1, self.v_lo)):
                    ch.dma(q, a16[:, base + 8192 + hl * 4096:
                                  base + 8192 + hl * 4096 + nKc * dh]
                           .rearrange("p (c t) -> p c t", t=dh),
                           src.rearrange("b (c p) m -> b p c m", p=128)[
                               si, :, :, hi_ * dh:(hi_ + 1) * dh])
                for hl, src in ((0, self.q_hi), (1, self.q_lo)):
                    ch.dma(q, a16[:, base + 16384 + hl * 4096:
                                  base + 16384 + hl * 4096 + ndc * T]
                           .rearrange("p (c t) -> p c t", t=T),
                           src.rearrange("b (c p) t -> b p c t", p=128)[
                               si, :, ndc * hi_:ndc * hi_ + ndc, :])
            self._fl[parity]["loads"] = ch.v
            self._fl[parity]["issued"] = True

        if not self._fl[par]["issued"]:
            issue_loads(s, h, par)
        loads = self._fl[par]["loads"]
        self._fl[par]["issued"] = False
        if head_idx + 1 < c["B"] * H:
            nh = head_idx + 1
            issue_loads(nh // H, nh % H, nh % 2)

        kT = lambda hl, dc, t0, w: a16[:, K_ + hl * 4096 + dc * T + t0:
                                       K_ + hl * 4096 + dc * T + t0 + w]
        vT = lambda hl, kc, d0, w: a16[:, V_ + hl * 4096 + kc * dh + d0:
                                       V_ + hl * 4096 + kc * dh + d0 + w]
        qT = lambda hl, dc, t0, w: a16[:, Q_ + hl * 4096 + dc * T + t0:
                                       Q_ + hl * 4096 + dc * T + t0 + w]
        eT = lambda hl, kc: a16[:, (ET_HI if hl == 0 else ET_LO) + kc * 512:
                                (ET_HI if hl == 0 else ET_LO) + kc * 512 + 512]

        first_of_head = True
        for qg in range(nQg):
            split_done = {}
            sum_emitted = 0
            et_free = self._et_free

            def emit_sum(kc):
                svv = split_done[kc]
                kcx = kc

                @blk.tensor
                def _(t):
                    t.wait_ge(ev, svv)
                    if kcx == 0:
                        self.bank_wait(t, "d")
                    t.matmul(self.ps_d[:, 0:512], self.s_on16[:], eT(0, kcx),
                             start=(kcx == 0), stop=False)
                    mm = t.matmul(self.ps_d[:, 0:512], self.s_on16[:],
                                  eT(1, kcx), start=False,
                                  stop=(kcx == nKc - 1))
                    if kcx == nKc - 1:
                        mm.then_inc(et, 1)
                if kcx == nKc - 1:
                    c_et.inc()

            for kc in range(nKc):
                bank = kc % 4
                qgx, kcx, bk = qg, kc, bank
                freev = self._sbank[bank]
                foh = first_of_head
                first_of_head = False

                @blk.tensor
                def _(t):
                    if foh:
                        chf[par].wait(t, loads)
                        t.wait_ge(ev, guard)
                        self.bank_wait(t, "a")
                    t.wait_ge(es, freev)
                    first = True
                    for dc in range(ndc):
                        for (sp, mp) in ((0, 0), (0, 1), (1, 0)):
                            mm = t.matmul(
                                self.ps_a[:, bk * 512:bk * 512 + 512],
                                kT(sp, dc, kcx * 128, 128),
                                qT(mp, dc, qgx * 512, 512),
                                start=first,
                                stop=(dc == ndc - 1 and (sp, mp) == (1, 0)))
                            first = False
                    mm.then_inc(et, 1)
                c_et.inc()
                qkv = c_et.v

                @blk.scalar
                def _(s_):
                    s_.wait_ge(et, qkv)
                    s_.activation(a32[:, bk * 512:bk * 512 + 512],
                                  self.ps_a[:, bk * 512:bk * 512 + 512],
                                  AF.Exp, bias=self.s_sc[:, 13:14],
                                  scale=1.0 / 16.0)
                    s_.drain().then_inc(es, 1)
                c_es.inc()
                self._sbank[bank] = c_es.v
                expv = c_es.v

                @blk.vector
                def _(v):
                    v.wait_ge(es, expv)
                    v.wait_ge(et, et_free)
                    v.tensor_copy(eT(0, kcx), a32[:, bk * 512:bk * 512 + 512])
                    v.drain()
                    v.tensor_tensor(eT(1, kcx),
                                    a32[:, bk * 512:bk * 512 + 512],
                                    eT(0, kcx), ALU.subtract)
                    v.drain().then_inc(ev, 1)
                c_ev.inc()
                split_done[kc] = c_ev.v

                if kc >= 1:
                    emit_sum(kc - 1)
            emit_sum(nKc - 1)
            sum_et = c_et.v

            @blk.vector
            def _(v):
                v.wait_ge(et, sum_et)
                v.reciprocal(a32[:, 2048:2560], self.ps_d[:, 0:512])
                v.drain().then_inc(ev, 1)
            c_ev.inc()
            self.bank_free("d", "ev")
            rcv = c_ev.v

            for dc in range(ndc):
                ps = self.ps_e if dc == 0 else self.ps_d
                bk2 = "e" if dc == 0 else "d"
                psx, dcx = ps, dc
                svv = split_done[nKc - 1]

                @blk.tensor
                def _(t):
                    t.wait_ge(ev, svv)
                    self.bank_wait(t, bk2)
                    first = True
                    for kc in range(nKc):
                        for (sp, mp) in ((0, 0), (0, 1), (1, 0)):
                            mm = t.matmul(psx[:, 0:512],
                                          vT(sp, kc, dcx * 128, 128),
                                          eT(mp, kc),
                                          start=first,
                                          stop=(kc == nKc - 1 and
                                                (sp, mp) == (1, 0)))
                            first = False
                    mm.then_inc(et, 1)
                c_et.inc()
                pvv = c_et.v
                if dc == ndc - 1:
                    self._fl[par]["lastpv"] = c_et.v
                    self._et_free = c_et.v
                slot = qg % 2
                stv = cho[slot].v
                OTB = OTS + slot * 2048

                @blk.vector
                def _(v):
                    v.wait_ge(et, pvv)
                    v.wait_ge(ev, rcv)
                    cho[slot].wait(v, stv)
                    rm = a32[:, 2048:2560]
                    t1 = a32[:, 2560:3072]
                    hi = a16[:, OTB + dcx * 512:OTB + dcx * 512 + 512]
                    lo = a16[:, OTB + 1024 + dcx * 512:
                             OTB + 1024 + dcx * 512 + 512]
                    v.tensor_tensor(hi, psx[:, 0:512], rm, ALU.mult)
                    v.drain()
                    v.tensor_tensor(t1, psx[:, 0:512], rm, ALU.mult)
                    v.drain()
                    v.tensor_tensor(lo, t1, hi, ALU.subtract)
                    v.drain().then_inc(ev, 1)
                c_ev.inc()
                self.bank_free(bk2, "ev")

            ov = c_ev.v
            sx, hx, qgx2, slot2 = s, h, qg, qg % 2
            OTB2 = OTS + slot2 * 2048

            @blk.gpsimd
            def _(g):
                g.wait_ge(ev, ov)
                for hl, dst in ((0, self.oT_hi), (1, self.oT_lo)):
                    cho[slot2].dma(
                        g, dst.rearrange("b (c p) t -> b p c t", p=128)[
                            sx, :, ndc * hx:ndc * hx + ndc,
                            qgx2 * 512:qgx2 * 512 + 512],
                        a16[:, OTB2 + hl * 1024:OTB2 + hl * 1024 + 1024]
                        .rearrange("p (c t) -> p c t", t=512))

    def _ffn_c_state(self):
        if not hasattr(self, "_p4"):
            TOK = self.TOK
            FW = min(1024, TOK)
            self._p4 = dict(i=0, a=0, FW=FW,
                            nTile=(self.cfg["I"] // 128) * (TOK // FW),
                            load_done={}, proc={}, store={}, kv={}, sig={},
                            started=False)
        return self._p4

    def emit_ffn_c_range(self, n):
        st = self._ffn_c_state()
        blk = self.blk
        a32, ai = self.a32, self.ai
        ev, c_ev = self.S("ev"), self.C("ev")
        es, c_es = self.S("es"), self.C("es")
        SL, MXC = self.SL, self.MX
        FW = st["FW"]
        nTile = st["nTile"]
        g_t4 = self.gateT.rearrange("(c p) (n f) -> c n p f", p=128, f=FW)
        nF = self.TOK // FW
        g_t = lambda i: g_t4[i // nF, i % nF]
        chi = [self.chan(f"ld_c{i}") for i in range(3)]
        cho = [self.chan("st_c0"), self.chan("st_c1")]
        load_done, proc_done, store_done = st["load_done"], st["proc"], st["store"]
        kv_done, sig_done = st["kv"], st["sig"]

        if not st["started"]:
            st["started"] = True
            scl = self.scale_ready
            wo_done = getattr(self, "PREP_DONE_wo", 0)

            @blk.gpsimd
            def _(q):
                q.wait_ge(ev, scl)
                q.wait_ge(self.S("es"), wo_done)
                for i in range(min(3, nTile)):
                    load_done[i] = chi[i % 3].dma(
                        q, a32[:, 1024 + (i % 3) * 1024:1024 + (i % 3) * 1024 + FW],
                        g_t(i))

        def emit_A(i):
            gbuf = a32[:, 1024 + (i % 3) * 1024:1024 + (i % 3) * 1024 + FW]
            ki = ai[:, (i % 3) * 1024:(i % 3) * 1024 + FW]
            qv = load_done[i]
            gbx, kix, ix = gbuf, ki, i

            @blk.vector
            def _(v):
                chi[ix % 3].wait(v, qv)
                v.tensor_scalar(kix, gbx, self.s_sc[:, SL["r_g"]:SL["r_g"] + 1],
                                32768.0, ALU.mult, ALU.add)
                v.drain().then_inc(ev, 1)
            c_ev.inc()
            kv_done[i] = c_ev.v
            sig = a32[:, 5120 + (i % 2) * 1024:5120 + (i % 2) * 1024 + FW]
            sfree = proc_done.get(i - 2, 0)
            stfree = store_done.get(i - 2, 0)
            kvv, sgx, slx = kv_done[i], sig, i % 2

            @blk.scalar
            def _(s_):
                s_.wait_ge(ev, kvv)
                s_.wait_ge(ev, sfree)
                cho[slx].wait(s_, stfree)
                s_.activation(sgx, kix, AF.Sigmoid,
                              bias=self.s_sc[:, SL["b_g"]:SL["b_g"] + 1],
                              scale=self.s_sc[:, SL["s_g"]:SL["s_g"] + 1])
                s_.drain().then_inc(es, 1)
            c_es.inc()
            sig_done[i] = c_es.v

            if i + 3 < nTile:
                evv, nt = kv_done[i], i + 3

                @blk.gpsimd
                def _(q):
                    q.wait_ge(ev, evv)
                    load_done[nt] = chi[nt % 3].dma(
                        q, a32[:, 1024 + (nt % 3) * 1024:
                               1024 + (nt % 3) * 1024 + FW], g_t(nt))

        def emit_B(i):
            sl = i % 2
            sig = a32[:, 5120 + sl * 1024:5120 + sl * 1024 + FW]
            sigq = a32[:, 7168:7168 + FW]
            gq = a32[:, 8192:8192 + FW]
            ki = ai[:, (i % 3) * 1024:(i % 3) * 1024 + FW]
            sv = sig_done[i]
            stv = store_done.get(i - 2, 0)
            gsx = sig
            sgx, sqx, gqx, kix, slx, ix = sig, sigq, gq, ki, sl, i

            @blk.vector
            def _(v):
                v.wait_ge(es, sv)
                cho[slx].wait(v, stv)
                v.tensor_scalar(gqx, kix, -32768.0,
                                self.s_sc[:, SL["s_g"]:SL["s_g"] + 1],
                                ALU.add, ALU.mult)
                v.drain()
                v.tensor_scalar(kix, sgx, 65536.0, 0.0, ALU.mult, ALU.add)
                v.drain()
                v.tensor_scalar(sqx, kix, 65535.0, 2.0 ** -16, ALU.min, ALU.mult)
                v.drain()
                v.tensor_tensor(gsx, gqx, sqx, ALU.mult)
                v.drain()
                v.reduce_max(self.s_st[:, MXC["tmp"]:MXC["tmp"] + 1], gsx,
                             axis=AX.X, apply_absolute_value=True)
                v.drain()
                v.tensor_tensor(self.s_st[:, MXC["gs"]:MXC["gs"] + 1],
                                self.s_st[:, MXC["gs"]:MXC["gs"] + 1],
                                self.s_st[:, MXC["tmp"]:MXC["tmp"] + 1], ALU.max)
                v.drain().then_inc(ev, 1)
            c_ev.inc()
            proc_done[i] = c_ev.v
            pv = proc_done[i]

            @blk.gpsimd
            def _(g):
                g.wait_ge(ev, pv)
                cho[slx].dma(g, g_t(ix), gsx)
            store_done[i] = cho[slx].v

        i0 = st["i"]
        i1 = min(i0 + n, nTile)
        st["i"] = i1
        for b in range(i0, i1):
            while st["a"] < min(b + 2, nTile):
                emit_A(st["a"])
                st["a"] += 1
            emit_B(b)

    def _ffn_d_state(self):
        if not hasattr(self, "_p5"):
            TOK = self.TOK
            FW = min(512, TOK)
            self._p5 = dict(i=0, FW=FW,
                            nTile=(self.cfg["I"] // 128) * (TOK // FW),
                            load_done={}, proc={}, store={}, started=False)
        return self._p5

    def emit_ffn_d_range(self, n):
        st = self._ffn_d_state()
        blk = self.blk
        a32, ai = self.a32, self.ai
        ev, c_ev = self.S("ev"), self.C("ev")
        SL, MXC = self.SL, self.MX
        FW = st["FW"]
        nTile = st["nTile"]
        nF = self.TOK // FW
        gs_t4 = self.gateT.rearrange("(c p) (n f) -> c n p f", p=128, f=FW)
        up_t4 = self.upT.rearrange("(c p) (n f) -> c n p f", p=128, f=FW)
        gs_t = lambda i: gs_t4[i // nF, i % nF]
        up_t = lambda i: up_t4[i // nF, i % nF]
        chi = [self.chan(f"ld_d{i}") for i in range(4)]
        cho = [self.chan("st_d0"), self.chan("st_d1")]
        load_done, proc_done, store_done = st["load_done"], st["proc"], st["store"]

        if not st["started"]:
            st["started"] = True
            scl = self.scale_ready
            pc0 = self.chan("st_c0").v
            pc1 = self.chan("st_c1").v

            @blk.gpsimd
            def _(q):
                q.wait_ge(ev, scl)
                self.chan("st_c0").wait(q, pc0)
                self.chan("st_c1").wait(q, pc1)
                for i in range(min(4, nTile)):
                    s4 = i % 4
                    chi[s4].dma(q, a32[:, 1024 + s4 * 512:1024 + s4 * 512 + FW],
                                gs_t(i))
                    load_done[i] = chi[s4].dma(
                        q, a32[:, 3072 + s4 * 512:3072 + s4 * 512 + FW],
                        up_t(i))

        i0 = st["i"]
        i1 = min(i0 + n, nTile)
        st["i"] = i1
        for i in range(i0, i1):
            sl = i % 2
            s4 = i % 4
            gsb = a32[:, 1024 + s4 * 512:1024 + s4 * 512 + FW]
            upb = a32[:, 3072 + s4 * 512:3072 + s4 * 512 + FW]
            g2 = a32[:, 5120:5120 + FW]
            uq = a32[:, 5632:5632 + FW]
            ob = a32[:, 6144 + sl * 512:6144 + sl * 512 + FW]
            ki = ai[:, 0:FW]
            ki2 = ai[:, 1024:1024 + FW]
            qv = load_done[i]
            stv = store_done.get(i - 2, 0)
            gx, ux, g2x, uqx, obx, slx, s4x = gsb, upb, g2, uq, ob, sl, i % 4

            @blk.vector
            def _(v):
                chi[s4x].wait(v, qv)
                cho[slx].wait(v, stv)
                v.tensor_scalar(ki, gx,
                                self.s_sc[:, SL["r_gs"]:SL["r_gs"] + 1],
                                32768.0, ALU.mult, ALU.add)
                v.drain()
                v.tensor_scalar(ki2, ux,
                                self.s_sc[:, SL["r_up"]:SL["r_up"] + 1],
                                32768.0, ALU.mult, ALU.add)
                v.drain()
                v.tensor_scalar(g2x, ki, -32768.0,
                                self.s_sc[:, SL["s_gs"]:SL["s_gs"] + 1],
                                ALU.add, ALU.mult)
                v.drain()
                v.tensor_scalar(uqx, ki2, -32768.0,
                                self.s_sc[:, SL["s_up"]:SL["s_up"] + 1],
                                ALU.add, ALU.mult)
                v.drain()
                v.tensor_tensor(obx, g2x, uqx, ALU.mult)
                v.drain()
                v.reduce_max(self.s_st[:, MXC["tmp"]:MXC["tmp"] + 1], obx,
                             axis=AX.X, apply_absolute_value=True)
                v.drain()
                v.tensor_tensor(self.s_st[:, MXC["o"]:MXC["o"] + 1],
                                self.s_st[:, MXC["o"]:MXC["o"] + 1],
                                self.s_st[:, MXC["tmp"]:MXC["tmp"] + 1], ALU.max)
                v.drain().then_inc(ev, 1)
            c_ev.inc()
            proc_done[i] = c_ev.v

            if i + 4 < nTile:
                evv, nt = proc_done[i], i + 4

                @blk.gpsimd
                def _(q):
                    q.wait_ge(ev, evv)
                    sl4 = nt % 4
                    chi[sl4].dma(q, a32[:, 1024 + sl4 * 512:
                                        1024 + sl4 * 512 + FW], gs_t(nt))
                    load_done[nt] = chi[sl4].dma(
                        q, a32[:, 3072 + sl4 * 512:3072 + sl4 * 512 + FW],
                        up_t(nt))

            pv = proc_done[i]
            ii = i

            @blk.gpsimd
            def _(g):
                g.wait_ge(ev, pv)
                cho[slx].dma(g, up_t(ii), obx)
            store_done[i] = cho[slx].v

    def emit_ffn_down(self):
        blk = self.blk
        c = self.cfg
        D, I, TOK = c["D"], c["I"], self.TOK
        nIc, nDc = I // 128, D // 128
        TW = 256 if TOK >= 512 else TOK
        nG = TOK // TW
        a16, a32, ai = self.a16, self.a32, self.ai
        ev, c_ev = self.S("ev"), self.C("ev")
        es, c_es = self.S("es"), self.C("es")
        et, c_et = self.S("et"), self.C("et")
        SL = self.SL
        o_t = self.upT.rearrange("(c p) n -> c p n", p=128)
        chi = [self.chan("ld_o0"), self.chan("ld_o1")]
        chs = [self.chan("st_dn0"), self.chan("st_dn1")]
        load_done = {}
        qdq_done = {}
        mm_done = {}
        prep_es = self.PREP_DONE_wd
        ch7 = self.chan("ld_fin")
        chx7 = self.chan("ld_px0")
        chout = [self.chan("st_out0"), self.chan("st_out1")]
        x_t7 = self.x.rearrange("(n p) d -> n p d", p=128)
        out_t7 = self.out.rearrange("(n p) d -> n p d", p=128)
        nJ = TW // 128
        f7 = dict(first=True, at0=self.chan("st_at0").v,
                  at1=self.chan("st_at1").v, tr={}, fin={}, ov={},
                  st={}, ld={}, xv={}, chsv={})
        self._fd_first = True

        def emit_lq(g, ics):
            gsl = g % 2
            OQ = 49152 + gsl * 12288
            mmv = mm_done.get(g - 2, 0)
            for ic in ics:
                sl = ic % 2
                ob = a32[:, sl * 512:sl * 512 + TW]
                evv = qdq_done.get((g, ic - 2),
                                   qdq_done.get((g - 1, nIc - 1), 0))
                obx, gx, icx, slx, OQx = ob, g, ic, sl, OQ
                first_l = self._fd_first
                self._fd_first = False

                @blk.sync
                def _(q):
                    if first_l:
                        self.qbar(q)
                        q.wait_ge(es, prep_es)
                    q.wait_ge(ev, evv)
                    load_done[(gx, icx)] = chi[slx].dma(
                        q, obx, o_t[icx, :, gx * TW:gx * TW + TW])

                qv = load_done[(g, ic)]
                mmvx = mmv

                @blk.vector
                def _(v):
                    chi[slx].wait(v, qv)
                    v.wait_ge(et, mmvx)
                    ki = ai[:, 0:TW]
                    qf = a32[:, 1024:1024 + TW]
                    hi = a16[:, OQx + icx * TW:OQx + icx * TW + TW]
                    lo = a16[:, OQx + nIc * TW + icx * TW:
                             OQx + nIc * TW + icx * TW + TW]
                    v.tensor_scalar(ki, obx,
                                    self.s_sc[:, SL["r_o"]:SL["r_o"] + 1],
                                    32768.0, ALU.mult, ALU.add)
                    v.drain()
                    v.tensor_scalar(hi, ki, -32768.0,
                                    self.s_sc[:, SL["s_o"]:SL["s_o"] + 1],
                                    ALU.add, ALU.mult)
                    v.drain()
                    v.tensor_scalar(qf, ki, -32768.0,
                                    self.s_sc[:, SL["s_o"]:SL["s_o"] + 1],
                                    ALU.add, ALU.mult)
                    v.drain()
                    v.tensor_tensor(lo, qf, hi, ALU.subtract)
                    v.drain().then_inc(ev, 1)
                c_ev.inc()
                qdq_done[(g, ic)] = c_ev.v

        def f7_loads(gc):
            off = gc * TW
            v0, v1 = f7["chsv"][gc]
            first_f = f7["first"]
            f7["first"] = False
            at0v, at1v = f7["at0"], f7["at1"]
            trl = f7["tr"].get(gc - 1, 0)
            stp = [f7["st"].get((gc - 1, j), 0) for j in range(2)]
            ovp = f7["ov"].get((gc - 1, nJ - 1), 0)

            @blk.sync
            def _(q):
                if first_f:
                    self.chan("st_at0").wait(q, at0v)
                    self.chan("st_at1").wait(q, at1v)
                chs[0].wait(q, v0)
                chs[1].wait(q, v1)
                q.wait_ge(et, trl)
                chout[0].wait(q, stp[0])
                chout[1].wait(q, stp[1])
                ch7.dma(q, a32[:, 3072:5120].rearrange("p (c t) -> p c t",
                                                       t=TW),
                        self.attT.rearrange("(c p) n -> p c n", p=128)[
                            :, :, off:off + TW])
                ch7.dma(q, a32[:, 5120:7168].rearrange("p (c t) -> p c t",
                                                       t=TW),
                        self.ffnT.rearrange("(c p) n -> p c n", p=128)[
                            :, :, off:off + TW])
            f7["ld"][gc] = ch7.v
            ovpx = ovp

            @blk.sync
            def _(q):
                q.wait_ge(ev, ovpx)
                for j in range(nJ):
                    chx7.dma(q, a32[:, 7168 + j * 1024:7168 + j * 1024 + 1024],
                             x_t7[off // 128 + j])
            f7["xv"][gc] = chx7.v

        def f7_fin(gc):
            ldv = f7["ld"][gc]

            @blk.vector
            def _(v):
                ch7.wait(v, ldv)
                v.tensor_tensor(a32[:, 3072:5120], a32[:, 3072:5120],
                                a32[:, 5120:7168], ALU.add)
                v.drain().then_inc(ev, 1)
            c_ev.inc()
            f7["fin"][gc] = c_ev.v

        def f7_tr(gc):
            fdx = f7["fin"][gc]
            pav = f7["ov"].get((gc - 1, nJ - 1), 0)

            @blk.tensor
            def _(t):
                t.wait_ge(ev, fdx)
                t.wait_ge(ev, pav)
                for j in range(nJ):
                    for dc in range(nDc):
                        t.transpose(
                            self.ps_a[:, j * 1024 + dc * 128:
                                      j * 1024 + dc * 128 + 128],
                            a32[:, 3072 + dc * TW + j * 128:
                                3072 + dc * TW + j * 128 + 128],
                            self.s_idf[:])
                t.drain().then_inc(et, 1)
            c_et.inc()
            f7["tr"][gc] = c_et.v

        def f7_ostg(gc, j):
            off = gc * TW
            trv = f7["tr"][gc]
            xv = f7["xv"][gc]
            stvp = f7["st"].get((gc - 1, j), 0)
            jx = j

            @blk.vector
            def _(v):
                v.wait_ge(et, trv)
                chx7.wait(v, xv)
                chout[jx].wait(v, stvp)
                v.tensor_tensor(a32[:, 5120 + jx * 1024:5120 + jx * 1024 + 1024],
                                self.ps_a[:, jx * 1024:jx * 1024 + 1024],
                                a32[:, 7168 + jx * 1024:7168 + jx * 1024 + 1024],
                                ALU.add)
                v.drain().then_inc(ev, 1)
            c_ev.inc()
            f7["ov"][(gc, j)] = c_ev.v
            ovv = c_ev.v
            tix = off // 128 + j

            @blk.gpsimd
            def _(q):
                q.wait_ge(ev, ovv)
                chout[jx].dma(q, out_t7[tix],
                              a32[:, 5120 + jx * 1024:5120 + jx * 1024 + 1024])
            f7["st"][(gc, j)] = chout[j].v

        emit_lq(0, range(nIc))
        for g in range(nG):
            gsl = g % 2
            OQ = 49152 + gsl * 12288
            if g >= 1:
                f7_loads(g - 1)
            ics_per_dc = nIc // nDc
            for dc in range(nDc):
                bank_key = "d" if dc % 2 == 0 else "e"
                ps = self.ps_d if dc % 2 == 0 else self.ps_e
                evv = qdq_done[(g, nIc - 1)]
                psx, dcx, gx, OQx = ps, dc, g, OQ

                @blk.tensor
                def _(t):
                    t.wait_ge(ev, evv)
                    self.bank_wait(t, bank_key)
                    first_mm = True
                    for ic in range(nIc):
                        for (sp, mp) in ((0, 0), (0, 1), (1, 0)):
                            rhs = a16[:, OQx + mp * nIc * TW + ic * TW:
                                      OQx + mp * nIc * TW + ic * TW + TW]
                            t.matmul(psx[:, 0:TW],
                                     self.lhsT("wd", dcx, ic, sp), rhs,
                                     start=first_mm,
                                     stop=(ic == nIc - 1 and (sp, mp) == (1, 0)))
                            first_mm = False
                    t.drain().then_inc(et, 1)
                c_et.inc()
                mm_done[g] = c_et.v
                mmv2 = c_et.v

                stg = a32[:, 2048 + (dc % 2) * 512:2048 + (dc % 2) * 512 + TW]
                stv = chs[dc % 2].v
                stgx = stg

                @blk.vector
                def _(v):
                    v.wait_ge(et, mmv2)
                    chs[dcx % 2].wait(v, stv)
                    v.tensor_copy(stgx, psx[:, 0:TW])
                    v.drain().then_inc(ev, 1)
                c_ev.inc()
                self.bank_free(bank_key, "ev")
                dv = c_ev.v

                @blk.gpsimd
                def _(g_):
                    g_.wait_ge(ev, dv)
                    chs[dcx % 2].dma(g_, self.ffnT[dcx * 128:dcx * 128 + 128,
                                                   gx * TW:gx * TW + TW],
                                     stgx)

                if g + 1 < nG:
                    emit_lq(g + 1, range(dc * ics_per_dc,
                                         (dc + 1) * ics_per_dc))
                if g >= 1:
                    if dc == 3:
                        f7_fin(g - 1)
                    elif dc == 4:
                        f7_tr(g - 1)
                    elif dc == 5:
                        f7_ostg(g - 1, 0)
                    elif dc == 6 and nJ > 1:
                        f7_ostg(g - 1, 1)
            f7["chsv"][g] = (chs[0].v, chs[1].v)

        gl = nG - 1
        f7_loads(gl)
        f7_fin(gl)
        f7_tr(gl)
        for j in range(nJ):
            f7_ostg(gl, j)

    def _p7a_state(self):
        if not hasattr(self, "_p7a"):
            self._p7a = dict(u=0, started=False)
        return self._p7a

    def emit_combine_mm_range(self, n):
        st = self._p7a_state()
        blk = self.blk
        c = self.cfg
        B, T = c["B"], c["T"]
        HD = self.HD
        nHc = HD // 128
        nDc = c["D"] // 128
        a16, a32 = self.a16, self.a32
        ev, c_ev = self.S("ev"), self.C("ev")
        et, c_et = self.S("et"), self.C("et")
        OG_HI, OG_LO = 32768, 40960
        chg = self.chan("ld_og")
        chs = [self.chan("st_at0"), self.chan("st_at1")]
        units = [(s, qgi) for s in range(B) for qgi in range(T // 512)]
        u0 = st["u"]
        u1 = min(u0 + n, len(units))
        st["u"] = u1

        for u in range(u0, u1):
            s, qgi = units[u]
            first_l = not st["started"]
            st["started"] = True
            mm_prev = st.get("mm_done", 0)
            sx, qx = s, qgi

            @blk.sync
            def _(q):
                if first_l:
                    self.qbar(q)
                    q.wait_ge(self.S("es"), self.PREP_DONE_wo)
                else:
                    q.wait_ge(et, mm_prev)
                for hl, src, dsto in ((0, self.oT_hi, OG_HI),
                                      (1, self.oT_lo, OG_LO)):
                    chg.dma(q, a16[:, dsto:dsto + nHc * 512].rearrange(
                        "p (c t) -> p c t", t=512),
                        src.rearrange("b (c p) t -> b p c t", p=128)[
                            sx, :, :, qx * 512:qx * 512 + 512])
            loads = chg.v

            for dc in range(nDc):
                bank_key = "d" if dc % 2 == 0 else "e"
                ps = self.ps_d if dc % 2 == 0 else self.ps_e
                psx, dcx = ps, dc

                @blk.tensor
                def _(t):
                    chg.wait(t, loads)
                    self.bank_wait(t, bank_key)
                    first_mm = True
                    for hc in range(nHc):
                        for (sp, mp) in ((0, 0), (0, 1), (1, 0)):
                            rhs = a16[:, (OG_HI if mp == 0 else OG_LO) +
                                      hc * 512:
                                      (OG_HI if mp == 0 else OG_LO) +
                                      hc * 512 + 512]
                            mm = t.matmul(psx[:, 0:512],
                                          self.lhsT("wo", dcx, hc, sp), rhs,
                                          start=first_mm,
                                          stop=(hc == nHc - 1 and
                                                (sp, mp) == (1, 0)))
                            first_mm = False
                    mm.then_inc(et, 1)
                c_et.inc()
                st["mm_done"] = c_et.v
                mmv = c_et.v
                stg = a32[:, (dc % 2) * 512:(dc % 2) * 512 + 512]
                stv = chs[dc % 2].v
                stgx = stg

                @blk.vector
                def _(v):
                    v.wait_ge(et, mmv)
                    chs[dcx % 2].wait(v, stv)
                    v.tensor_copy(stgx, psx[:, 0:512])
                    v.drain().then_inc(ev, 1)
                c_ev.inc()
                self.bank_free(bank_key, "ev")
                dv = c_ev.v
                off = (s * T + qgi * 512)

                @blk.gpsimd
                def _(g):
                    g.wait_ge(ev, dv)
                    chs[dcx % 2].dma(g, self.attT[dcx * 128:dcx * 128 + 128,
                                                  off:off + 512], stgx)

    def emit_combine_final(self):
        blk = self.blk
        c = self.cfg
        B, T, D = c["B"], c["T"], c["D"]
        nDc = D // 128
        a32 = self.a32
        ev, c_ev = self.S("ev"), self.C("ev")
        es, c_es = self.S("es"), self.C("es")
        et, c_et = self.S("et"), self.C("et")
        x_t = self.x.rearrange("(n p) d -> n p d", p=128)
        out_t = self.out.rearrange("(n p) d -> n p d", p=128)
        chg = self.chan("ld_fin")
        chxx = [self.chan("ld_px0"), self.chan("ld_px1")]
        chout = [self.chan("st_out0"), self.chan("st_out1")]
        first = True
        grp_done = 0
        xf_free = {}
        pa_free = {}
        stv_prev = {}

        for s in range(B):
            for qgi in range(T // 512):
                sx, qx = s, qgi
                first_l = first
                first = False
                gv = grp_done

                @blk.sync
                def _(q):
                    if first_l:
                        self.qbar(q)
                    else:
                        q.wait_ge(ev, gv)
                    chg.dma(q, a32[:, 0:4096].rearrange("p (c t) -> p c t", t=512),
                            self.attT.rearrange("(c p) n -> p c n", p=128)[
                                :, :, sx * T + qx * 512:sx * T + qx * 512 + 512])
                    chg.dma(q, a32[:, 4096:8192].rearrange("p (c t) -> p c t", t=512),
                            self.ffnT.rearrange("(c p) n -> p c n", p=128)[
                                :, :, sx * T + qx * 512:sx * T + qx * 512 + 512])
                loads = chg.v

                @blk.vector
                def _(v):
                    chg.wait(v, loads)
                    v.tensor_tensor(a32[:, 0:4096], a32[:, 0:4096],
                                    a32[:, 4096:8192], ALU.add)
                    v.drain().then_inc(ev, 1)
                c_ev.inc()
                fin_done = c_ev.v

                for j in range(4):
                    ti = (s * T + qgi * 512) // 128 + j
                    jx, tix = j, ti
                    xev = xf_free.get(j % 2, 0)

                    @blk.sync
                    def _(q):
                        q.wait_ge(ev, max(xev, fin_done))
                        chxx[jx % 2].dma(q, a32[:, 4096 + (jx % 2) * 1024:
                                                4096 + (jx % 2) * 1024 + 1024],
                                         x_t[tix])
                    xload = chxx[j % 2].v

                    pav = pa_free.get(j % 2, 0)

                    @blk.tensor
                    def _(t):
                        t.wait_ge(ev, fin_done)
                        t.wait_ge(ev, pav)
                        psl = (jx % 2) * 1024
                        for dc in range(nDc):
                            t.transpose(
                                self.ps_a[:, psl + dc * 128:psl + dc * 128 + 128],
                                a32[:, dc * 512 + jx * 128:
                                    dc * 512 + jx * 128 + 128],
                                self.s_idf[:])
                        t.drain().then_inc(et, 1)
                    c_et.inc()
                    trv = c_et.v
                    stv = chout[j % 2].v

                    @blk.vector
                    def _(v):
                        v.wait_ge(et, trv)
                        chxx[jx % 2].wait(v, xload)
                        chout[jx % 2].wait(v, stv)
                        psl = (jx % 2) * 1024
                        ostg = a32[:, 6144 + (jx % 2) * 1024:
                                   6144 + (jx % 2) * 1024 + 1024]
                        v.tensor_tensor(ostg, self.ps_a[:, psl:psl + 1024],
                                        a32[:, 4096 + (jx % 2) * 1024:
                                            4096 + (jx % 2) * 1024 + 1024],
                                        ALU.add)
                        v.drain().then_inc(ev, 1)
                    c_ev.inc()
                    xf_free[j % 2] = c_ev.v
                    pa_free[j % 2] = c_ev.v
                    osv = c_ev.v

                    @blk.gpsimd
                    def _(g):
                        g.wait_ge(ev, osv)
                        chout[jx % 2].dma(
                            g, out_t[tix],
                            a32[:, 6144 + (jx % 2) * 1024:
                                6144 + (jx % 2) * 1024 + 1024])
                grp_done = c_ev.v


_CACHE = {}


def _get_nc():
    if "full" not in _CACHE:
        _CACHE["full"] = MK(CFG_FULL).build()
    return _CACHE["full"]


def kernel(L0_attn_x, L0_attn_Wq, L0_attn_Wkv_down, L0_attn_Wk_up,
           L0_attn_Wv_up, L0_attn_Wo, L0_ffn_w_gate, L0_ffn_w_up,
           L0_ffn_w_down):
    x = np.ascontiguousarray(np.asarray(L0_attn_x, dtype=np.float32))
    B_all, T, D = x.shape
    per = B_all // NCORES
    nc = _get_nc()
    idf = np.eye(128, dtype=np.float32)
    idb = np.eye(128, dtype=ml_dtypes.bfloat16)
    common = dict(
        Wq=np.ascontiguousarray(L0_attn_Wq, dtype=np.float32),
        Wkv=np.ascontiguousarray(L0_attn_Wkv_down, dtype=np.float32),
        Wku=np.ascontiguousarray(L0_attn_Wk_up, dtype=np.float32),
        Wvu=np.ascontiguousarray(L0_attn_Wv_up, dtype=np.float32),
        Wo=np.ascontiguousarray(L0_attn_Wo, dtype=np.float32),
        Wg=np.ascontiguousarray(L0_ffn_w_gate, dtype=np.float32),
        Wu=np.ascontiguousarray(L0_ffn_w_up, dtype=np.float32),
        Wd=np.ascontiguousarray(L0_ffn_w_down, dtype=np.float32),
        idf=idf, idb=idb,
    )
    in_maps = []
    for ci in range(NCORES):
        m = dict(common)
        m["x"] = np.ascontiguousarray(
            x[ci * per:(ci + 1) * per].reshape(per * T, D))
        in_maps.append(m)
    res = run_bass_kernel_spmd(nc, in_maps, core_ids=list(range(NCORES)))
    out = np.concatenate(
        [np.asarray(r["out"]).reshape(per, T, D) for r in res.results], axis=0)
    return out


# version 24
# speedup vs baseline: 1.0254x; 1.0219x over previous
import numpy as np
import ml_dtypes
from contextlib import ExitStack

import concourse.bass as bass
import concourse.mybir as mybir
import concourse.bass_utils as _bu
from concourse.bass_utils import run_bass_kernel_spmd

_orig_run_command = _bu.run_command


def _run_command_ldwopt(cmd, *a, **kw):
    if isinstance(cmd, list):
        cmd = ["--enable-ldw-opt=true" if c == "--enable-ldw-opt=false" else c
               for c in cmd]
    return _orig_run_command(cmd, *a, **kw)


_bu.run_command = _run_command_ldwopt

F32 = mybir.dt.float32
BF16 = mybir.dt.bfloat16
I32 = mybir.dt.int32
ALU = mybir.AluOpType
AF = mybir.ActivationFunctionType
AX = mybir.AxisListType

NCORES = 8
CFG_FULL = dict(B=4, T=2048, D=1024, H=8, dh=256, L=64, I=3072)


class Ctr:
    def __init__(self):
        self.v = 0

    def inc(self, n=1):
        self.v += n
        return self.v


class Chan:

    def __init__(self, sem):
        self.sem = sem
        self.c = Ctr()

    def dma(self, q, out, in_):
        q.dma_start(out=out, in_=in_).then_inc(self.sem, 16)
        self.c.inc(16)
        return self.c.v

    @property
    def v(self):
        return self.c.v

    def wait(self, eng, v=None):
        eng.wait_ge(self.sem, self.c.v if v is None else v)


class MK:
    def __init__(self, cfg):
        self.cfg = dict(cfg)
        self.nc = bass.Bass("TRN2", target_bir_lowering=False, debug=False)
        self.bank = {}
        self.stg_store = {}

    def build(self):
        c = self.cfg
        nc = self.nc
        B, T, D, H, dh, L, I = (c["B"], c["T"], c["D"], c["H"], c["dh"],
                                c["L"], c["I"])
        TOK = B * T
        HD = H * dh
        self.TOK, self.HD = TOK, HD

        din = lambda n, s, dt=F32: nc.dram_tensor(n, s, dt, kind="ExternalInput").ap()
        self.x = din("x", [TOK, D])
        self.Wq = din("Wq", [HD, D])
        self.Wkv = din("Wkv", [L, D])
        self.Wku = din("Wku", [HD, L])
        self.Wvu = din("Wvu", [HD, L])
        self.Wo = din("Wo", [D, HD])
        self.Wg = din("Wg", [I, D])
        self.Wu = din("Wu", [I, D])
        self.Wd = din("Wd", [D, I])
        self.idf = din("idf", [128, 128])
        self.idb = din("idb", [128, 128], BF16)
        self.out = nc.dram_tensor("out", [TOK, D], F32, kind="ExternalOutput").ap()

        dbg = getattr(MK, "DEBUG_DUMP", ())

        def dsc(n, s, dt=F32):
            if n in dbg:
                return nc.dram_tensor(n, s, dt, kind="ExternalOutput").ap()
            return nc.dram_tensor(n, s, dt).ap()
        self.xnT_hi = dsc("xnT_hi", [D, TOK], BF16)
        self.xnT_lo = dsc("xnT_lo", [D, TOK], BF16)
        self.xqT_hi = dsc("xqT_hi", [D, TOK], BF16)
        self.xqT_lo = dsc("xqT_lo", [D, TOK], BF16)
        self.q_hi = dsc("q_hi", [B, HD, T], BF16)
        self.q_lo = dsc("q_lo", [B, HD, T], BF16)
        self.k_hi = dsc("k_hi", [B, HD, T], BF16)
        self.k_lo = dsc("k_lo", [B, HD, T], BF16)
        self.v_hi = dsc("v_hi", [B, T, HD], BF16)
        self.v_lo = dsc("v_lo", [B, T, HD], BF16)
        self.oT_hi = dsc("oT_hi", [B, HD, T], BF16)
        self.oT_lo = dsc("oT_lo", [B, HD, T], BF16)
        self.upT = dsc("upT", [I, TOK])
        self.gateT = dsc("gateT", [I, TOK])
        self.ffnT = dsc("ffnT", [D, TOK])
        self.attT = dsc("attT", [D, TOK])
        self.dbg_raw = {}
        for nm in ("up_raw", "gate_raw"):
            if nm in dbg:
                self.dbg_raw[nm] = dsc(nm, [I, TOK])
        self.cc_in = [dsc(f"cc_in{i}", [1, 2]) for i in range(4)]
        self.cc_out = [dsc(f"cc_out{i}", [NCORES, 2]) for i in range(4)]

        self._ctx = ExitStack()
        e = self._ctx.enter_context
        sb = lambda n, s, dt: e(nc.sbuf_tensor(n, s, dt))
        self.s_idf = sb("s_idf", [128, 128], F32)
        self.s_on16 = sb("s_on16", [128, 128], BF16)
        self.s_idb = sb("s_idb", [128, 128], BF16)
        self.s_ones = sb("s_ones", [1, 128], F32)
        self.s_sc = sb("s_sc", [128, 16], F32)
        self.s_st = sb("s_st", [128, 256], F32)
        self.s_cc = sb("s_cc", [1, 32], F32)
        self.s_sr = sb("s_sr", [1, 16], F32)
        self.s_vec = sb("s_vec", [128, 16], F32)
        self.a16 = sb("a16", [128, 78 * 1024], BF16)
        self.a32 = sb("a32", [128, 9216], F32)
        self.ai = sb("ai", [128, 3072], I32)
        self.ps_a = e(nc.psum_tensor("ps_a", [128, 2048], F32))
        self.ps_b = e(nc.psum_tensor("ps_b", [128, 1024], BF16))
        self.ps_c = e(nc.psum_tensor("ps_c", [128, 1024], BF16))
        self.ps_d = e(nc.psum_tensor("ps_d", [128, 512], F32))
        self.ps_e = e(nc.psum_tensor("ps_e", [128, 512], F32))

        self.sems = {}
        for name in ["cc", "ev", "es", "et"]:
            self.sems[name] = (e(nc.semaphore(name)), Ctr())
        self.chans = {}
        self._enter = e
        self.blk = e(nc.Block())

        self.emit_all()
        self._ctx.close()
        return nc

    def S(self, n):
        return self.sems[n][0]

    def C(self, n):
        return self.sems[n][1]

    def chan(self, name):
        if name not in self.chans:
            self.chans[name] = Chan(self._enter(self.nc.semaphore("ch_" + name)))
        return self.chans[name]

    def bank_wait(self, eng, *keys):
        for k in keys:
            if k in self.bank:
                sem_name, val = self.bank[k]
                eng.wait_ge(self.S(sem_name), val)

    def bank_free(self, key, sem_name):
        self.bank[key] = (sem_name, self.C(sem_name).v)

    def qbar(self, q):
        for n in ["ev", "es", "et"]:
            q.wait_ge(self.S(n), self.C(n).v)
        for name, ch in list(self.chans.items()):
            if name.startswith("st_"):
                ch.wait(q)

    SL = dict(s_x=0, r_x=1, s_up=2, r_up=3, s_g=4, r_g=5, b_g=6,
              s_gs=7, r_gs=8, s_o=9, r_o=10)
    MX = dict(tmp=0, xn=1, up=2, gate=3, gs=4, o=5)

    def emit_all(self):
        c = self.cfg
        B = c["B"]
        self.emit_preamble()
        self.emit_ln()
        self.emit_scale_round(0, [("xn", "s_x", "r_x", 32767.0)])
        self.emit_prep_lhsT(self.Wkv, 69632, dequant=False, tag="kvd")
        self.emit_prep_lhsT(self.Wku, 74752, dequant=False, tag="kup")
        self.emit_prep_lhsT(self.Wvu, 70656, dequant=False, tag="vup")
        self.emit_prep_lhsT(self.Wq, 0, dequant=False, tag="wq")
        for s in range(B):
            self.emit_proj_q(s)
        for s in range(B):
            self.emit_proj_kv(s)
        self.emit_xq()
        self.emit_prep_lhsT(self.Wu, 0, dequant=True, tag="wu")
        self.emit_ffn_mm(self.upT, "up", "wu")
        self.emit_prep_lhsT(self.Wg, 0, dequant=True, tag="wg")
        self.emit_ffn_mm(self.gateT, "gate", "wg")
        self.emit_scale_round(1, [("up", "s_up", "r_up", 32767.0),
                                  ("gate", "s_g", "r_g", 32767.0)])
        for s in range(B):
            for h in range(c["H"]):
                self.emit_flash(s, h)
        self.emit_prep_lhsT(self.Wo, 0, dequant=False, tag="wo")
        nu = B * (c["T"] // 512)
        nDc = c["D"] // 128
        nc_t = self._ffn_c_state()["nTile"]
        nd_t = self._ffn_d_state()["nTile"]
        ns1 = (nu // 2) * nDc
        for j in range(ns1):
            self.emit_combine_mm_range(1)
            self.emit_ffn_c_range(-(-nc_t // ns1))
        self.emit_ffn_c_range(10 ** 9)
        self.emit_scale_round(2, [("gs", "s_gs", "r_gs", 32767.0)])
        ns2 = (nu - nu // 2) * nDc
        for j in range(ns2):
            self.emit_combine_mm_range(1)
            self.emit_ffn_d_range(-(-nd_t // ns2))
        self.emit_ffn_d_range(10 ** 9)
        self.emit_scale_round(3, [("o", "s_o", "r_o", 32767.0)])
        self.emit_prep_lhsT(self.Wd, 0, dequant=True, tag="wd")
        self.emit_ffn_down()

        chans = list(self.chans.values())

        @self.blk.gpsimd
        def _(g):
            for ch in chans:
                ch.wait(g)

    def emit_preamble(self):
        blk = self.blk
        ev, c_ev = self.S("ev"), self.C("ev")
        ch = self.chan("ld_pre")

        @blk.sync
        def _(q):
            ch.dma(q, self.s_idf[:], self.idf)
            ch.dma(q, self.s_idb[:], self.idb)

        @blk.vector
        def _(v):
            ch.wait(v)
            v.memset(self.s_ones[:], 1.0)
            v.memset(self.s_st[:], -1e30)
            v.memset(self.s_cc[:], 0.0)
            v.memset(self.s_sr[:], 0.0)
            v.memset(self.s_vec[:], 0.0)
            v.drain()
            v.memset(self.s_sc[:, 11:12], 1e-5)
            v.memset(self.s_sc[:, 12:13], 0.0)
            v.memset(self.s_sc[:, 13:14], -3.0)
            v.memset(self.s_on16[:], 1.0)
            v.drain().then_inc(ev, 1)
        c_ev.inc()

    def emit_ln(self):
        blk = self.blk
        D = self.cfg["D"]
        TOK = self.TOK
        nDc = D // 128
        nTt = TOK // 128
        ST = 512
        nST = TOK // ST
        a16, a32 = self.a16, self.a32
        ev, c_ev = self.S("ev"), self.C("ev")
        es, c_es = self.S("es"), self.C("es")
        et, c_et = self.S("et"), self.C("et")
        x_t = self.x.rearrange("(n p) d -> n p d", p=128)
        MXC = self.MX
        chx = [self.chan("ld_x0"), self.chan("ld_x1")]
        chst = self.chan("st_p0")

        free_at = {}
        split_free = {}
        store_done = []
        pre_ev = c_ev.v

        @blk.sync
        def _(q):
            q.wait_ge(ev, pre_ev)
            for i in range(min(2, nTt)):
                chx[i % 2].dma(q, a32[:, (i % 2) * 1024:(i % 2) * 1024 + 1024],
                               x_t[i])

        for sti in range(nST):
            for j in range(ST // 128):
                ti = sti * (ST // 128) + j
                xbuf = a32[:, (ti % 2) * 1024:(ti % 2) * 1024 + 1024]
                d = a32[:, 2048:3072]
                xn = a32[:, 3072:4096]
                sp_h = a16[:, 16384 + (ti % 2) * 1024:16384 + (ti % 2) * 1024 + 1024]
                sp_l = a16[:, 18432 + (ti % 2) * 1024:18432 + (ti % 2) * 1024 + 1024]
                ldv = chx[ti % 2].v

                @blk.vector
                def _(v):
                    chx[ti % 2].wait(v, ldv)
                    v.reduce_sum(self.s_vec[:, 0:1], xbuf, axis=AX.X)
                    v.drain()
                    v.tensor_scalar(self.s_vec[:, 1:2], self.s_vec[:, 0:1],
                                    1.0 / D, None, ALU.mult)
                    v.drain()
                    v.tensor_scalar(d, xbuf, self.s_vec[:, 1:2], None, ALU.subtract)
                    v.drain()
                    v.tensor_tensor(xn, d, d, ALU.mult)
                    v.drain()
                    v.reduce_sum(self.s_vec[:, 2:3], xn, axis=AX.X)
                    v.drain().then_inc(ev, 1)
                c_ev.inc()
                free_at[ti] = c_ev.v

                if ti + 2 < nTt:
                    evv, nt = free_at[ti], ti + 2

                    @blk.sync
                    def _(q):
                        q.wait_ge(ev, evv)
                        chx[nt % 2].dma(q, a32[:, (nt % 2) * 1024:
                                               (nt % 2) * 1024 + 1024], x_t[nt])

                t_ev = c_ev.v

                @blk.scalar
                def _(s):
                    s.wait_ge(ev, t_ev)
                    s.activation(self.s_vec[:, 3:4], self.s_vec[:, 2:3], AF.Ln,
                                 bias=self.s_sc[:, 11:12], scale=1.0 / D)
                    s.drain()
                    s.activation(self.s_vec[:, 4:5], self.s_vec[:, 3:4], AF.Exp,
                                 bias=self.s_sc[:, 12:13], scale=-0.5)
                    s.drain().then_inc(es, 1)
                c_es.inc()

                t_es = c_es.v
                etv = split_free.get(ti - 2, 0)

                @blk.vector
                def _(v):
                    v.wait_ge(es, t_es)
                    v.wait_ge(et, etv)
                    v.tensor_scalar(xn, d, self.s_vec[:, 4:5], None, ALU.mult)
                    v.drain()
                    v.reduce_max(self.s_st[:, MXC["tmp"]:MXC["tmp"] + 1], xn,
                                 axis=AX.X, apply_absolute_value=True)
                    v.drain()
                    v.tensor_tensor(self.s_st[:, MXC["xn"]:MXC["xn"] + 1],
                                    self.s_st[:, MXC["xn"]:MXC["xn"] + 1],
                                    self.s_st[:, MXC["tmp"]:MXC["tmp"] + 1],
                                    ALU.max)
                    v.drain()
                    v.tensor_copy(sp_h, xn)
                    v.drain()
                    v.tensor_tensor(sp_l, xn, sp_h, ALU.subtract)
                    v.drain().then_inc(ev, 1)
                c_ev.inc()

                tv = c_ev.v

                @blk.tensor
                def _(t):
                    t.wait_ge(ev, tv)
                    self.bank_wait(t, "b", "c")
                    for cdx in range(nDc):
                        t.transpose(self.ps_b[:, cdx * 128:cdx * 128 + 128],
                                    sp_h[:, cdx * 128:cdx * 128 + 128],
                                    self.s_idb[:])
                    for cdx in range(nDc):
                        t.transpose(self.ps_c[:, cdx * 128:cdx * 128 + 128],
                                    sp_l[:, cdx * 128:cdx * 128 + 128],
                                    self.s_idb[:])
                    t.drain().then_inc(et, 1)
                c_et.inc()
                split_free[ti] = c_et.v

                t_et = c_et.v
                stv = store_done[sti - 1] if sti >= 1 else 0
                jj = j

                @blk.vector
                def _(v):
                    v.wait_ge(et, t_et)
                    if jj == 0:
                        chst.wait(v, stv)
                    dst_h = a16[:, 0:8 * ST].rearrange(
                        "p (c t) -> p c t", t=ST)[:, :, jj * 128:jj * 128 + 128]
                    dst_l = a16[:, 4096:4096 + 8 * ST].rearrange(
                        "p (c t) -> p c t", t=ST)[:, :, jj * 128:jj * 128 + 128]
                    v.tensor_copy(dst_h,
                                  self.ps_b[:].rearrange("p (c t) -> p c t", t=128))
                    v.drain()
                    v.tensor_copy(dst_l,
                                  self.ps_c[:].rearrange("p (c t) -> p c t", t=128))
                    v.drain().then_inc(ev, 1)
                c_ev.inc()
                self.bank_free("b", "ev")
                self.bank_free("c", "ev")

            t_ev2 = c_ev.v
            sti_ = sti

            @blk.gpsimd
            def _(g):
                g.wait_ge(ev, t_ev2)
                src_h = a16[:, 0:8 * ST].rearrange("p (c t) -> p c t", t=ST)
                src_l = a16[:, 4096:4096 + 8 * ST].rearrange("p (c t) -> p c t", t=ST)
                dst_h = self.xnT_hi.rearrange("(c p) n -> p c n", p=128)[
                    :, :, sti_ * ST:sti_ * ST + ST]
                dst_l = self.xnT_lo.rearrange("(c p) n -> p c n", p=128)[
                    :, :, sti_ * ST:sti_ * ST + ST]
                chst.dma(g, dst_h, src_h)
                chst.dma(g, dst_l, src_l)
            store_done.append(chst.v)

    def emit_scale_round(self, round_i, items):
        blk = self.blk
        cc, c_cc = self.S("cc"), self.C("cc")
        ev, c_ev = self.S("ev"), self.C("ev")
        et, c_et = self.S("et"), self.C("et")
        MXC, SL = self.MX, self.SL
        n = len(items)
        chcc = self.chan("st_cc")

        tv = c_ev.v

        sb = list(getattr(self, "_sbank", []))

        @blk.tensor
        def _(t):
            t.wait_ge(ev, tv)
            self.bank_wait(t, "a")
            for v_ in sb:
                t.wait_ge(self.S("es"), v_)
            for i, (mx, _, _, _) in enumerate(items):
                t.transpose(self.ps_a[0:1, i * 128:(i + 1) * 128],
                            self.s_st[:, MXC[mx]:MXC[mx] + 1], self.s_idf[:])
            t.drain().then_inc(et, 1)
        c_et.inc()

        tev = c_et.v

        @blk.vector
        def _(v):
            v.wait_ge(et, tev)
            for i in range(n):
                v.reduce_max(self.s_cc[0:1, i:i + 1],
                             self.ps_a[0:1, i * 128:(i + 1) * 128], axis=AX.X)
                v.drain()
            v.drain().then_inc(ev, 1)
        c_ev.inc()
        self.bank_free("a", "ev")
        self.psa_guard = c_ev.v
        ccv = c_ev.v

        @blk.gpsimd
        def _(g):
            g.wait_ge(ev, ccv)
            chcc.dma(g, self.cc_in[round_i][0:1, 0:2], self.s_cc[0:1, 0:2])
            chcc.wait(g)
            g.collective_compute(
                "AllGather", ALU.bypass,
                replica_groups=[list(range(NCORES))],
                ins=[self.cc_in[round_i].opt()],
                outs=[self.cc_out[round_i].opt()],
            ).then_inc(cc, 1)
            c_cc.inc()
            g.wait_ge(cc, c_cc.v)
            chcc.dma(g, self.s_cc[0:1, 8:8 + 2 * NCORES],
                     self.cc_out[round_i].flatten().unsqueeze(0))
        gatherv = chcc.v

        @blk.vector
        def _(v):
            chcc.wait(v, gatherv)
            for i, (mx, s_slot, r_slot, qmax) in enumerate(items):
                gathered = self.s_cc[0:1, 8:8 + 2 * NCORES].rearrange(
                    "p (c k) -> p c k", k=2)[:, :, i:i + 1].transpose([0, 2, 1])
                v.reduce_max(self.s_sr[0:1, 2 * i:2 * i + 1], gathered, axis=AX.X)
                v.drain()
                v.tensor_scalar(self.s_sr[0:1, 2 * i:2 * i + 1],
                                self.s_sr[0:1, 2 * i:2 * i + 1],
                                1.0 / qmax, 1e-12, ALU.mult, ALU.max)
                v.drain()
                v.reciprocal(self.s_sr[0:1, 2 * i + 1:2 * i + 2],
                             self.s_sr[0:1, 2 * i:2 * i + 1])
                v.drain()
            v.drain().then_inc(ev, 1)
        c_ev.inc()
        bv = c_ev.v

        @blk.tensor
        def _(t):
            t.wait_ge(ev, bv)
            self.bank_wait(t, "d")
            t.matmul(self.ps_d[:, 0:2 * n], self.s_ones[:],
                     self.s_sr[0:1, 0:2 * n], start=True, stop=True)
            t.drain().then_inc(et, 1)
        c_et.inc()
        mv = c_et.v

        @blk.vector
        def _(v):
            v.wait_ge(et, mv)
            for i, (mx, s_slot, r_slot, qmax) in enumerate(items):
                v.tensor_copy(self.s_sc[:, SL[s_slot]:SL[s_slot] + 1],
                              self.ps_d[:, 2 * i:2 * i + 1])
                v.drain()
                v.tensor_copy(self.s_sc[:, SL[r_slot]:SL[r_slot] + 1],
                              self.ps_d[:, 2 * i + 1:2 * i + 2])
                v.drain()
                if s_slot == "s_g":
                    v.tensor_scalar(self.s_sc[:, SL["b_g"]:SL["b_g"] + 1],
                                    self.s_sc[:, SL["s_g"]:SL["s_g"] + 1],
                                    -32768.0, None, ALU.mult)
                    v.drain()
            v.drain().then_inc(ev, 1)
        c_ev.inc()
        self.bank_free("d", "ev")
        self.scale_ready = c_ev.v

    def emit_prep_lhsT(self, W, off, dequant, tag):
        blk = self.blk
        rows, cols = W.shape
        rp = min(128, rows)
        cw = min(128, cols)
        nMt = (rows + 127) // 128
        nCc = (cols + cw - 1) // cw
        half = nMt * nCc * rp
        setattr(self, "OFF_" + tag, (off, half, rows, cols))
        ev, c_ev = self.S("ev"), self.C("ev")
        es, c_es = self.S("es"), self.C("es")
        et, c_et = self.S("et"), self.C("et")
        a16, a32, ai = self.a16, self.a32, self.ai
        SPL_HI, SPL_LO = 49152, 52224
        w_t = W.rearrange("(n p) d -> n p d", p=128) if rows >= 128 else None
        chw = [self.chan("ld_w0"), self.chan("ld_w1")]
        load_done = {}
        split_done = {}
        tr_done = {}

        @blk.sync
        def _(q):
            self.qbar(q)
            for i in range(min(2, nMt)):
                src = w_t[i] if rows >= 128 else W
                load_done[i] = chw[i % 2].dma(
                    q, a32[0:rp, (i % 2) * cols:(i % 2) * cols + cols], src)

        for mt in range(nMt):
            wb = a32[0:rp, (mt % 2) * cols:(mt % 2) * cols + cols]
            hi = a16[0:rp, SPL_HI:SPL_HI + cols]
            lo = a16[0:rp, SPL_LO:SPL_LO + cols]

            qv = load_done[mt]
            etv = tr_done.get(mt - 1, 0)
            mtx = mt

            @blk.vector
            def _(v):
                chw[mtx % 2].wait(v, qv)
                v.wait_ge(et, etv)
                if dequant:
                    nblk = cols // 32
                    bm = self.s_st[:, 64:64 + nblk]
                    rcp = self.s_st[:, 160:160 + nblk]
                    w3 = wb.rearrange("p (b k) -> p b k", k=32)
                    v.reduce_max(bm[0:rp, :], w3, axis=AX.X,
                                 apply_absolute_value=True)
                    v.drain()
                    v.tensor_scalar(bm[0:rp, :], bm[0:rp, :], 1.0 / 7.0, 1e-12,
                                    ALU.mult, ALU.max)
                    v.drain()
                    v.reciprocal(rcp[0:rp, :], bm[0:rp, :])
                    v.drain()
                    tmp = a32[0:rp, 6144:6144 + cols]
                    v.tensor_tensor(tmp.rearrange("p (b k) -> p b k", k=32), w3,
                                    rcp[0:rp, :].unsqueeze(2).broadcast_to(
                                        [rp, nblk, 32]), ALU.mult)
                    v.drain()
                    ki = ai[0:rp, 0:cols]
                    v.tensor_scalar(ki, tmp, 1.0, 8.0, ALU.mult, ALU.add)
                    v.drain()
                    v.tensor_scalar(tmp, ki, -8.0, None, ALU.add)
                    v.drain()
                    v.tensor_tensor(tmp.rearrange("p (b k) -> p b k", k=32),
                                    tmp.rearrange("p (b k) -> p b k", k=32),
                                    bm[0:rp, :].unsqueeze(2).broadcast_to(
                                        [rp, nblk, 32]), ALU.mult)
                    v.drain()
                    src = tmp
                else:
                    src = wb
                v.tensor_copy(hi, src)
                v.drain()
                v.tensor_tensor(lo, src, hi, ALU.subtract)
                v.drain().then_inc(ev, 1)
            c_ev.inc()
            split_done[mt] = c_ev.v

            if mt + 2 < nMt:
                evv, nt = split_done[mt], mt + 2

                @blk.sync
                def _(q):
                    q.wait_ge(ev, evv)
                    load_done[nt] = chw[nt % 2].dma(
                        q, a32[0:rp, (nt % 2) * cols:(nt % 2) * cols + cols],
                        w_t[nt])

            per_bank = 1024 // rp
            for hl in range(2):
                ps = self.ps_b if hl == 0 else self.ps_c
                bank_key = "b" if hl == 0 else "c"
                srcbuf = hi if hl == 0 else lo
                for bs in range(0, nCc, per_bank):
                    nb = min(per_bank, nCc - bs)
                    sv = split_done[mt]
                    bsc, nbc, psx, srcx = bs, nb, ps, srcbuf

                    @blk.tensor
                    def _(t):
                        t.wait_ge(ev, sv)
                        self.bank_wait(t, bank_key)
                        for k in range(nbc):
                            t.transpose(psx[0:cw, k * rp:k * rp + rp],
                                        srcx[:, (bsc + k) * cw:(bsc + k) * cw + cw],
                                        self.s_idb[0:rp, 0:rp])
                        t.drain().then_inc(et, 1)
                    c_et.inc()
                    tev = c_et.v
                    mtc, hlc = mt, hl

                    @blk.scalar
                    def _(s):
                        s.wait_ge(et, tev)
                        dst = a16[0:cw,
                                  off + hlc * half + (mtc * nCc + bsc) * rp:
                                  off + hlc * half + (mtc * nCc + bsc) * rp +
                                  nbc * rp]
                        s.copy(dst, psx[0:cw, 0:nbc * rp])
                        s.drain().then_inc(es, 1)
                    c_es.inc()
                    self.bank_free(bank_key, "es")
            tr_done[mt] = c_et.v
        setattr(self, "PREP_DONE_" + tag, c_es.v)

    def lhsT(self, tag, mt, cc_, hl):
        off, half, rows, cols = getattr(self, "OFF_" + tag)
        rp = min(128, rows)
        cw = min(128, cols)
        nCc = (cols + cw - 1) // cw
        base = off + hl * half + (mt * nCc + cc_) * rp
        return self.a16[0:cw, base:base + rp]

    def emit_proj_q(self, s):
        blk = self.blk
        c = self.cfg
        T, D, H, dh, L = c["T"], c["D"], c["H"], c["dh"], c["L"]
        nDc = D // 128
        nTg = T // 512
        a16, a32 = self.a16, self.a32
        ev, c_ev = self.S("ev"), self.C("ev")
        et, c_et = self.S("et"), self.C("et")

        XN_HI = 32768
        XN_LO = 32768 + nDc * T
        KV_HI = 65536
        KV_LO = 65536 + T
        STG_Q = (65536, 66560)
        STG_KV = (0, 1024)
        chxn = self.chan("ld_xn")

        @blk.sync
        def _(q):
            self.qbar(q)
            for src, dst in [(self.xnT_hi, XN_HI), (self.xnT_lo, XN_LO)]:
                chxn.dma(q, a16[:, dst:dst + nDc * T].rearrange(
                    "p (c t) -> p c t", t=T),
                    src.rearrange("(c p) n -> p c n", p=128)[
                        :, :, s * T:(s + 1) * T])
        xn_ready = chxn.v

        xnt = lambda hl, dc, t0, w: a16[:, (XN_HI if hl == 0 else XN_LO) +
                                        dc * T + t0:
                                        (XN_HI if hl == 0 else XN_LO) +
                                        dc * T + t0 + w]

        for h in range(H):
            for dc2 in range(dh // 128):
                for tg in range(nTg):
                    hh, dd, tt = h, dc2, tg

                    psx, bk = ((self.ps_d, "d") if self._ess_i % 2 == 0
                               else (self.ps_e, "e"))

                    @blk.tensor
                    def _(t):
                        chxn.wait(t, xn_ready)
                        self.bank_wait(t, bk)
                        mi = hh * (dh // 128) + dd
                        first = True
                        for dc in range(nDc):
                            for (sp, mp) in ((0, 0), (0, 1), (1, 0)):
                                t.matmul(psx[:, 0:512],
                                         self.lhsT("wq", mi, dc, sp),
                                         xnt(mp, dc, tt * 512, 512),
                                         start=first,
                                         stop=(dc == nDc - 1 and
                                               (sp, mp) == (1, 0)))
                                first = False
                        t.drain().then_inc(et, 1)
                    c_et.inc()
                    self._evac_split_store(
                        bank_key=bk, ps=psx, stg=STG_Q, n=512,
                        dst_hi=self.q_hi[s, h * dh + dc2 * 128:
                                         h * dh + dc2 * 128 + 128,
                                         tg * 512:tg * 512 + 512],
                        dst_lo=self.q_lo[s, h * dh + dc2 * 128:
                                         h * dh + dc2 * 128 + 128,
                                         tg * 512:tg * 512 + 512])


    def emit_proj_kv(self, s):
        blk = self.blk
        c = self.cfg
        T, D, H, dh, L = c["T"], c["D"], c["H"], c["dh"], c["L"]
        nDc = D // 128
        nTg = T // 512
        a16, a32 = self.a16, self.a32
        ev, c_ev = self.S("ev"), self.C("ev")
        et, c_et = self.S("et"), self.C("et")

        XN_HI = 32768
        XN_LO = 32768 + nDc * T
        KV_HI = 65536
        KV_LO = 65536 + T
        STG_Q = (65536, 66560)
        STG_KV = (0, 1024)
        chxn = self.chan("ld_xn")

        @blk.sync
        def _(q):
            self.qbar(q)
            for src, dst in [(self.xnT_hi, XN_HI), (self.xnT_lo, XN_LO)]:
                chxn.dma(q, a16[:, dst:dst + nDc * T].rearrange(
                    "p (c t) -> p c t", t=T),
                    src.rearrange("(c p) n -> p c n", p=128)[
                        :, :, s * T:(s + 1) * T])
        xn_ready = chxn.v

        xnt = lambda hl, dc, t0, w: a16[:, (XN_HI if hl == 0 else XN_LO) +
                                        dc * T + t0:
                                        (XN_HI if hl == 0 else XN_LO) +
                                        dc * T + t0 + w]

        for tg in range(nTg):
            tgc = tg

            @blk.tensor
            def _(t):
                chxn.wait(t, xn_ready)
                self.bank_wait(t, "d")
                first = True
                for dc in range(nDc):
                    for (sp, mp) in ((0, 0), (0, 1), (1, 0)):
                        t.matmul(self.ps_d[0:64, 0:512],
                                 self.lhsT("kvd", 0, dc, sp),
                                 xnt(mp, dc, tgc * 512, 512),
                                 start=first, stop=(dc == nDc - 1 and
                                                    (sp, mp) == (1, 0)))
                        first = False
                t.drain().then_inc(et, 1)
            c_et.inc()
            mmv = c_et.v

            qs0 = self.chan("st_ess0").v
            qs1 = self.chan("st_ess1").v

            @blk.vector
            def _(v):
                v.wait_ge(et, mmv)
                self.chan("st_ess0").wait(v, qs0)
                self.chan("st_ess1").wait(v, qs1)
                kvf = a32[0:64, 0:512]
                v.tensor_copy(kvf, self.ps_d[0:64, 0:512])
                v.drain()
                v.tensor_copy(a16[0:64, KV_HI + tgc * 512:KV_HI + tgc * 512 + 512],
                              kvf)
                v.drain()
                v.tensor_tensor(a16[0:64, KV_LO + tgc * 512:KV_LO + tgc * 512 + 512],
                                kvf,
                                a16[0:64, KV_HI + tgc * 512:KV_HI + tgc * 512 + 512],
                                ALU.subtract)
                v.drain().then_inc(ev, 1)
            c_ev.inc()
            self.bank_free("d", "ev")
        kv_ready = c_ev.v

        kvt = lambda hl, t0, w: a16[0:64, (KV_HI if hl == 0 else KV_LO) +
                                    t0:(KV_HI if hl == 0 else KV_LO) + t0 + w]

        for h in range(H):
            for dc2 in range(dh // 128):
                for tg in range(nTg):
                    hh, dd, tt = h, dc2, tg

                    psx, bk = ((self.ps_d, "d") if self._ess_i % 2 == 0
                               else (self.ps_e, "e"))

                    @blk.tensor
                    def _(t):
                        t.wait_ge(ev, kv_ready)
                        self.bank_wait(t, bk)
                        mi = hh * (dh // 128) + dd
                        first = True
                        for (sp, mp) in ((0, 0), (0, 1), (1, 0)):
                            t.matmul(psx[:, 0:512],
                                     self.lhsT("kup", mi, 0, sp),
                                     kvt(mp, tt * 512, 512),
                                     start=first, stop=((sp, mp) == (1, 0)))
                            first = False
                        t.drain().then_inc(et, 1)
                    c_et.inc()
                    self._evac_split_store(
                        bank_key=bk, ps=psx, stg=STG_KV, n=512,
                        dst_hi=self.k_hi[s, h * dh + dc2 * 128:
                                         h * dh + dc2 * 128 + 128,
                                         tg * 512:tg * 512 + 512],
                        dst_lo=self.k_lo[s, h * dh + dc2 * 128:
                                         h * dh + dc2 * 128 + 128,
                                         tg * 512:tg * 512 + 512])

        nTc = T // 128
        voff, vhalf, _, _ = getattr(self, "OFF_vup")
        for tc in range(nTc):
            for hg in range(H * dh // 512):
                tcc, hgc = tc, hg

                psx, bk = ((self.ps_d, "d") if self._ess_i % 2 == 0
                           else (self.ps_e, "e"))

                @blk.tensor
                def _(t):
                    t.wait_ge(ev, kv_ready)
                    self.bank_wait(t, bk)
                    first = True
                    for (sp, mp) in ((0, 0), (0, 1), (1, 0)):
                        rhs = a16[0:64, voff + mp * vhalf + hgc * 512:
                                  voff + mp * vhalf + hgc * 512 + 512]
                        t.matmul(psx[:, 0:512],
                                 kvt(sp, tcc * 128, 128), rhs,
                                 start=first, stop=((sp, mp) == (1, 0)))
                        first = False
                    t.drain().then_inc(et, 1)
                c_et.inc()
                self._evac_split_store(
                    bank_key=bk, ps=psx, stg=STG_KV, n=512,
                    dst_hi=self.v_hi[s, tc * 128:tc * 128 + 128,
                                     hg * 512:hg * 512 + 512],
                    dst_lo=self.v_lo[s, tc * 128:tc * 128 + 128,
                                     hg * 512:hg * 512 + 512])


    _ess_i = 0

    def _evac_split_store(self, bank_key, stg, n, dst_hi, dst_lo, ps=None):
        blk = self.blk
        a16 = self.a16
        ev, c_ev = self.S("ev"), self.C("ev")
        slot = self._ess_i % 2
        self._ess_i += 1
        ch = self.chan(f"st_ess{slot}")
        ps = self.ps_d if ps is None else ps
        base = stg[slot]
        hi = a16[:, base:base + n]
        lo = a16[:, base + 512:base + 512 + n]
        tv = self.C("et").v
        stv = ch.v

        @blk.vector
        def _(v):
            v.wait_ge(self.S("et"), tv)
            ch.wait(v, stv)
            v.tensor_copy(hi, ps[:, 0:n])
            v.drain()
            v.tensor_tensor(lo, ps[:, 0:n], hi, ALU.subtract)
            v.drain().then_inc(ev, 1)
        c_ev.inc()
        self.bank_free(bank_key, "ev")
        sv = c_ev.v

        @blk.gpsimd
        def _(g):
            g.wait_ge(ev, sv)
            ch.dma(g, dst_hi, hi)
            ch.dma(g, dst_lo, lo)

    def emit_xq(self):
        blk = self.blk
        D, TOK = self.cfg["D"], self.TOK
        a16, a32, ai = self.a16, self.a32, self.ai
        ev, c_ev = self.S("ev"), self.C("ev")
        SL = self.SL
        FW = min(1024, TOK)
        nTile = (D // 128) * (TOK // FW)
        nF = TOK // FW
        _t4 = lambda ap: ap.rearrange("(c p) (n f) -> c n p f", p=128, f=FW)
        hi_t4, lo_t4 = _t4(self.xnT_hi), _t4(self.xnT_lo)
        qhi_t4, qlo_t4 = _t4(self.xqT_hi), _t4(self.xqT_lo)
        hi_t = lambda i: hi_t4[i // nF, i % nF]
        lo_t = lambda i: lo_t4[i // nF, i % nF]
        qhi_t = lambda i: qhi_t4[i // nF, i % nF]
        qlo_t = lambda i: qlo_t4[i // nF, i % nF]
        IN_HI, IN_LO, ST_HI, ST_LO = 55296, 59392, 63488, 67584
        chi = [self.chan("ld_xq0"), self.chan("ld_xq1")]
        cho = [self.chan("st_xq0"), self.chan("st_xq1")]
        load_done = {}
        proc_done = {}
        scl = self.scale_ready

        @blk.sync
        def _(q):
            self.qbar(q)
            q.wait_ge(ev, scl)
            for i in range(min(2, nTile)):
                chi[i % 2].dma(q, a16[:, IN_HI + (i % 2) * FW:
                                      IN_HI + (i % 2) * FW + FW], hi_t(i))
                load_done[i] = chi[i % 2].dma(
                    q, a16[:, IN_LO + (i % 2) * FW:IN_LO + (i % 2) * FW + FW],
                    lo_t(i))

        for i in range(nTile):
            sl = i % 2
            qv = load_done[i]
            stv = cho[sl].v
            slx = sl

            @blk.vector
            def _(v):
                chi[slx].wait(v, qv)
                cho[slx].wait(v, stv)
                xh = a16[:, IN_HI + slx * FW:IN_HI + slx * FW + FW]
                xl = a16[:, IN_LO + slx * FW:IN_LO + slx * FW + FW]
                xf = a32[:, 0:FW]
                qf = a32[:, 1024:1024 + FW]
                ki = ai[:, 0:FW]
                sh = a16[:, ST_HI + slx * FW:ST_HI + slx * FW + FW]
                slo = a16[:, ST_LO + slx * FW:ST_LO + slx * FW + FW]
                v.tensor_tensor(xf, xh, xl, ALU.add)
                v.drain()
                v.tensor_scalar(ki, xf, self.s_sc[:, SL["r_x"]:SL["r_x"] + 1],
                                32768.0, ALU.mult, ALU.add)
                v.drain()
                v.tensor_scalar(sh, ki, -32768.0,
                                self.s_sc[:, SL["s_x"]:SL["s_x"] + 1],
                                ALU.add, ALU.mult)
                v.drain()
                v.tensor_scalar(qf, ki, -32768.0,
                                self.s_sc[:, SL["s_x"]:SL["s_x"] + 1],
                                ALU.add, ALU.mult)
                v.drain()
                v.tensor_tensor(slo, qf, sh, ALU.subtract)
                v.drain().then_inc(ev, 1)
            c_ev.inc()
            proc_done[i] = c_ev.v

            if i + 2 < nTile:
                evv, nt = proc_done[i], i + 2

                @blk.sync
                def _(q):
                    q.wait_ge(ev, evv)
                    chi[nt % 2].dma(q, a16[:, IN_HI + (nt % 2) * FW:
                                           IN_HI + (nt % 2) * FW + FW],
                                    hi_t(nt))
                    load_done[nt] = chi[nt % 2].dma(
                        q, a16[:, IN_LO + (nt % 2) * FW:IN_LO + (nt % 2) * FW + FW],
                        lo_t(nt))

            pv = proc_done[i]
            ii = i

            @blk.gpsimd
            def _(g):
                g.wait_ge(ev, pv)
                cho[slx].dma(g, qhi_t(ii),
                             a16[:, ST_HI + slx * FW:ST_HI + slx * FW + FW])
                cho[slx].dma(g, qlo_t(ii),
                             a16[:, ST_LO + slx * FW:ST_LO + slx * FW + FW])

    def emit_ffn_mm(self, dstT, mx_name, tag):
        blk = self.blk
        c = self.cfg
        D, I, TOK = c["D"], c["I"], self.TOK
        nDc, nIc, nTg = D // 128, I // 128, TOK // 512
        a16, a32 = self.a16, self.a32
        ev, c_ev = self.S("ev"), self.C("ev")
        es, c_es = self.S("es"), self.C("es")
        et, c_et = self.S("et"), self.C("et")
        MXC = self.MX
        XQ = 55296
        qhi = self.xqT_hi.rearrange("(c p) n -> p c n", p=128)
        qlo = self.xqT_lo.rearrange("(c p) n -> p c n", p=128)
        chq = [self.chan("ld_fq0"), self.chan("ld_fq1")]
        chs = [self.chan("st_fs0"), self.chan("st_fs1")]
        load_done = {}
        tg_mm = {}
        prep_es = getattr(self, "PREP_DONE_" + tag)

        @blk.sync
        def _(q):
            self.qbar(q)
            q.wait_ge(es, prep_es)
            for i in range(min(2, nTg)):
                sl = i % 2
                chq[sl].dma(q, a16[:, XQ + sl * 8192:XQ + sl * 8192 + 4096]
                            .rearrange("p (c t) -> p c t", t=512),
                            qhi[:, :, i * 512:i * 512 + 512])
                load_done[i] = chq[sl].dma(
                    q, a16[:, XQ + sl * 8192 + 4096:XQ + sl * 8192 + 8192]
                    .rearrange("p (c t) -> p c t", t=512),
                    qlo[:, :, i * 512:i * 512 + 512])

        for tg in range(nTg):
            sl = tg % 2
            for ic in range(nIc):
                bank_key = "d" if ic % 2 == 0 else "e"
                ps = self.ps_d if ic % 2 == 0 else self.ps_e
                qv = load_done[tg]
                slx, icx, tgx, psx = sl, ic, tg, ps

                @blk.tensor
                def _(t):
                    chq[slx].wait(t, qv)
                    self.bank_wait(t, bank_key)
                    first = True
                    for dc in range(nDc):
                        for (sp, mp) in ((0, 0), (0, 1), (1, 0)):
                            rhs = a16[:, XQ + slx * 8192 + mp * 4096 + dc * 512:
                                      XQ + slx * 8192 + mp * 4096 + dc * 512 + 512]
                            t.matmul(psx[:, 0:512],
                                     self.lhsT(tag, icx, dc, sp), rhs,
                                     start=first,
                                     stop=(dc == nDc - 1 and (sp, mp) == (1, 0)))
                            first = False
                    t.drain().then_inc(et, 1)
                c_et.inc()
                mmv = c_et.v

                stg = a32[:, (ic % 2) * 512:(ic % 2) * 512 + 512]
                stv = chs[ic % 2].v
                stgx = stg

                @blk.vector
                def _(v):
                    v.wait_ge(et, mmv)
                    chs[icx % 2].wait(v, stv)
                    v.tensor_copy(stgx, psx[:, 0:512])
                    v.drain()
                    v.reduce_max(self.s_st[:, MXC["tmp"]:MXC["tmp"] + 1], stgx,
                                 axis=AX.X, apply_absolute_value=True)
                    v.drain()
                    v.tensor_tensor(self.s_st[:, MXC[mx_name]:MXC[mx_name] + 1],
                                    self.s_st[:, MXC[mx_name]:MXC[mx_name] + 1],
                                    self.s_st[:, MXC["tmp"]:MXC["tmp"] + 1],
                                    ALU.max)
                    v.drain().then_inc(ev, 1)
                c_ev.inc()
                self.bank_free(bank_key, "ev")
                dv = c_ev.v

                dbg_t = self.dbg_raw.get(mx_name + "_raw")

                @blk.gpsimd
                def _(g):
                    g.wait_ge(ev, dv)
                    chs[icx % 2].dma(g, dstT[icx * 128:icx * 128 + 128,
                                             tgx * 512:tgx * 512 + 512], stgx)
                    if dbg_t is not None:
                        chs[icx % 2].dma(g, dbg_t[icx * 128:icx * 128 + 128,
                                                  tgx * 512:tgx * 512 + 512],
                                         stgx)
            tg_mm[tg] = c_et.v

            if tg + 2 < nTg:
                etv, nt = tg_mm[tg], tg + 2

                @blk.sync
                def _(q):
                    q.wait_ge(et, etv)
                    sl2 = nt % 2
                    chq[sl2].dma(q, a16[:, XQ + sl2 * 8192:XQ + sl2 * 8192 + 4096]
                                 .rearrange("p (c t) -> p c t", t=512),
                                 qhi[:, :, nt * 512:nt * 512 + 512])
                    load_done[nt] = chq[sl2].dma(
                        q, a16[:, XQ + sl2 * 8192 + 4096:XQ + sl2 * 8192 + 8192]
                        .rearrange("p (c t) -> p c t", t=512),
                        qlo[:, :, nt * 512:nt * 512 + 512])

    def emit_flash(self, s, h):
        blk = self.blk
        c = self.cfg
        T, dh, H = c["T"], c["dh"], c["H"]
        nKc = T // 128
        nQg = T // 512
        ndc = dh // 128
        a16, a32 = self.a16, self.a32
        ev, c_ev = self.S("ev"), self.C("ev")
        es, c_es = self.S("es"), self.C("es")
        et, c_et = self.S("et"), self.C("et")
        ET_HI, ET_LO, OTS = 24576, 32768, 40960
        head_idx = s * H + h
        par = head_idx % 2
        BASE = [0, 45056][par]
        K_, V_, Q_ = BASE, BASE + 8192, BASE + 16384
        chf = [self.chan("ld_fl0"), self.chan("ld_fl1")]
        cho = [self.chan("st_fl0"), self.chan("st_fl1")]
        guard = getattr(self, "psa_guard", 0)
        if not hasattr(self, "_fl"):
            ev0 = self.C("et").v
            self._fl = {0: dict(loads=0, lastpv=ev0, issued=False),
                        1: dict(loads=0, lastpv=ev0, issued=False)}
            self._et_free = 0
            self._sbank = [0, 0, 0, 0]

        def issue_loads(si, hi_, parity):
            base = [0, 45056][parity]
            ch = chf[parity]
            lastpv = self._fl[parity]["lastpv"]

            @blk.sync
            def _(q):
                if si == 0 and hi_ == 0:
                    self.qbar(q)
                else:
                    q.wait_ge(et, lastpv)
                for hl, src in ((0, self.k_hi), (1, self.k_lo)):
                    ch.dma(q, a16[:, base + hl * 4096:base + hl * 4096 + ndc * T]
                           .rearrange("p (c t) -> p c t", t=T),
                           src.rearrange("b (c p) t -> b p c t", p=128)[
                               si, :, ndc * hi_:ndc * hi_ + ndc, :])
                for hl, src in ((0, self.v_hi), (1, self.v_lo)):
                    ch.dma(q, a16[:, base + 8192 + hl * 4096:
                                  base + 8192 + hl * 4096 + nKc * dh]
                           .rearrange("p (c t) -> p c t", t=dh),
                           src.rearrange("b (c p) m -> b p c m", p=128)[
                               si, :, :, hi_ * dh:(hi_ + 1) * dh])
                for hl, src in ((0, self.q_hi), (1, self.q_lo)):
                    ch.dma(q, a16[:, base + 16384 + hl * 4096:
                                  base + 16384 + hl * 4096 + ndc * T]
                           .rearrange("p (c t) -> p c t", t=T),
                           src.rearrange("b (c p) t -> b p c t", p=128)[
                               si, :, ndc * hi_:ndc * hi_ + ndc, :])
            self._fl[parity]["loads"] = ch.v
            self._fl[parity]["issued"] = True

        if not self._fl[par]["issued"]:
            issue_loads(s, h, par)
        loads = self._fl[par]["loads"]
        self._fl[par]["issued"] = False
        if head_idx + 1 < c["B"] * H:
            nh = head_idx + 1
            issue_loads(nh // H, nh % H, nh % 2)

        kT = lambda hl, dc, t0, w: a16[:, K_ + hl * 4096 + dc * T + t0:
                                       K_ + hl * 4096 + dc * T + t0 + w]
        vT = lambda hl, kc, d0, w: a16[:, V_ + hl * 4096 + kc * dh + d0:
                                       V_ + hl * 4096 + kc * dh + d0 + w]
        qT = lambda hl, dc, t0, w: a16[:, Q_ + hl * 4096 + dc * T + t0:
                                       Q_ + hl * 4096 + dc * T + t0 + w]
        eT = lambda hl, kc: a16[:, (ET_HI if hl == 0 else ET_LO) + kc * 512:
                                (ET_HI if hl == 0 else ET_LO) + kc * 512 + 512]

        first_of_head = True
        for qg in range(nQg):
            split_done = {}
            sum_emitted = 0
            et_free = self._et_free

            def emit_sum(kc):
                svv = split_done[kc]
                kcx = kc

                @blk.tensor
                def _(t):
                    t.wait_ge(ev, svv)
                    if kcx == 0:
                        self.bank_wait(t, "d")
                    t.matmul(self.ps_d[:, 0:512], self.s_on16[:], eT(0, kcx),
                             start=(kcx == 0), stop=False)
                    mm = t.matmul(self.ps_d[:, 0:512], self.s_on16[:],
                                  eT(1, kcx), start=False,
                                  stop=(kcx == nKc - 1))
                    if kcx == nKc - 1:
                        mm.then_inc(et, 1)
                if kcx == nKc - 1:
                    c_et.inc()

            for kc in range(nKc):
                bank = kc % 4
                qgx, kcx, bk = qg, kc, bank
                freev = self._sbank[bank]
                foh = first_of_head
                first_of_head = False

                @blk.tensor
                def _(t):
                    if foh:
                        chf[par].wait(t, loads)
                        t.wait_ge(ev, guard)
                        self.bank_wait(t, "a")
                    t.wait_ge(es, freev)
                    first = True
                    for dc in range(ndc):
                        for (sp, mp) in ((0, 0), (0, 1), (1, 0)):
                            mm = t.matmul(
                                self.ps_a[:, bk * 512:bk * 512 + 512],
                                kT(sp, dc, kcx * 128, 128),
                                qT(mp, dc, qgx * 512, 512),
                                start=first,
                                stop=(dc == ndc - 1 and (sp, mp) == (1, 0)))
                            first = False
                    mm.then_inc(et, 1)
                c_et.inc()
                qkv = c_et.v

                @blk.scalar
                def _(s_):
                    s_.wait_ge(et, qkv)
                    s_.activation(a32[:, bk * 512:bk * 512 + 512],
                                  self.ps_a[:, bk * 512:bk * 512 + 512],
                                  AF.Exp, bias=self.s_sc[:, 13:14],
                                  scale=1.0 / 16.0)
                    s_.drain().then_inc(es, 1)
                c_es.inc()
                self._sbank[bank] = c_es.v
                expv = c_es.v

                @blk.vector
                def _(v):
                    v.wait_ge(es, expv)
                    v.wait_ge(et, et_free)
                    v.tensor_copy(eT(0, kcx), a32[:, bk * 512:bk * 512 + 512])
                    v.drain()
                    v.tensor_tensor(eT(1, kcx),
                                    a32[:, bk * 512:bk * 512 + 512],
                                    eT(0, kcx), ALU.subtract)
                    v.drain().then_inc(ev, 1)
                c_ev.inc()
                split_done[kc] = c_ev.v

                if kc >= 1:
                    emit_sum(kc - 1)
            emit_sum(nKc - 1)
            sum_et = c_et.v

            @blk.vector
            def _(v):
                v.wait_ge(et, sum_et)
                v.reciprocal(a32[:, 2048:2560], self.ps_d[:, 0:512])
                v.drain().then_inc(ev, 1)
            c_ev.inc()
            self.bank_free("d", "ev")
            rcv = c_ev.v

            for dc in range(ndc):
                ps = self.ps_e if dc == 0 else self.ps_d
                bk2 = "e" if dc == 0 else "d"
                psx, dcx = ps, dc
                svv = split_done[nKc - 1]

                @blk.tensor
                def _(t):
                    t.wait_ge(ev, svv)
                    self.bank_wait(t, bk2)
                    first = True
                    for kc in range(nKc):
                        for (sp, mp) in ((0, 0), (0, 1), (1, 0)):
                            mm = t.matmul(psx[:, 0:512],
                                          vT(sp, kc, dcx * 128, 128),
                                          eT(mp, kc),
                                          start=first,
                                          stop=(kc == nKc - 1 and
                                                (sp, mp) == (1, 0)))
                            first = False
                    mm.then_inc(et, 1)
                c_et.inc()
                pvv = c_et.v
                if dc == ndc - 1:
                    self._fl[par]["lastpv"] = c_et.v
                    self._et_free = c_et.v
                slot = qg % 2
                stv = cho[slot].v
                OTB = OTS + slot * 2048

                @blk.vector
                def _(v):
                    v.wait_ge(et, pvv)
                    v.wait_ge(ev, rcv)
                    cho[slot].wait(v, stv)
                    rm = a32[:, 2048:2560]
                    t1 = a32[:, 2560:3072]
                    hi = a16[:, OTB + dcx * 512:OTB + dcx * 512 + 512]
                    lo = a16[:, OTB + 1024 + dcx * 512:
                             OTB + 1024 + dcx * 512 + 512]
                    v.tensor_tensor(hi, psx[:, 0:512], rm, ALU.mult)
                    v.drain()
                    v.tensor_tensor(t1, psx[:, 0:512], rm, ALU.mult)
                    v.drain()
                    v.tensor_tensor(lo, t1, hi, ALU.subtract)
                    v.drain().then_inc(ev, 1)
                c_ev.inc()
                self.bank_free(bk2, "ev")

            ov = c_ev.v
            sx, hx, qgx2, slot2 = s, h, qg, qg % 2
            OTB2 = OTS + slot2 * 2048

            @blk.gpsimd
            def _(g):
                g.wait_ge(ev, ov)
                for hl, dst in ((0, self.oT_hi), (1, self.oT_lo)):
                    cho[slot2].dma(
                        g, dst.rearrange("b (c p) t -> b p c t", p=128)[
                            sx, :, ndc * hx:ndc * hx + ndc,
                            qgx2 * 512:qgx2 * 512 + 512],
                        a16[:, OTB2 + hl * 1024:OTB2 + hl * 1024 + 1024]
                        .rearrange("p (c t) -> p c t", t=512))

    def _ffn_c_state(self):
        if not hasattr(self, "_p4"):
            TOK = self.TOK
            FW = min(1024, TOK)
            self._p4 = dict(i=0, FW=FW,
                            nTile=(self.cfg["I"] // 128) * (TOK // FW),
                            load_done={}, proc={}, store={}, started=False)
        return self._p4

    def emit_ffn_c_range(self, n):
        st = self._ffn_c_state()
        blk = self.blk
        a32, ai = self.a32, self.ai
        ev, c_ev = self.S("ev"), self.C("ev")
        es, c_es = self.S("es"), self.C("es")
        SL, MXC = self.SL, self.MX
        FW = st["FW"]
        nTile = st["nTile"]
        g_t4 = self.gateT.rearrange("(c p) (n f) -> c n p f", p=128, f=FW)
        nF = self.TOK // FW
        g_t = lambda i: g_t4[i // nF, i % nF]
        chi = [self.chan(f"ld_c{i}") for i in range(4)]
        cho = [self.chan("st_c0"), self.chan("st_c1")]
        load_done, proc_done, store_done = st["load_done"], st["proc"], st["store"]

        if not st["started"]:
            st["started"] = True
            scl = self.scale_ready
            wo_done = getattr(self, "PREP_DONE_wo", 0)

            @blk.gpsimd
            def _(q):
                q.wait_ge(ev, scl)
                q.wait_ge(self.S("es"), wo_done)
                for i in range(min(4, nTile)):
                    load_done[i] = chi[i % 4].dma(
                        q, a32[:, 1024 + (i % 4) * 1024:1024 + (i % 4) * 1024 + FW],
                        g_t(i))

        i0 = st["i"]
        i1 = min(i0 + n, nTile)
        st["i"] = i1
        for i in range(i0, i1):
            sl = i % 2
            gbuf = a32[:, 1024 + (i % 4) * 1024:1024 + (i % 4) * 1024 + FW]
            sig = a32[:, 5120:5120 + FW]
            sigq = a32[:, 6144:6144 + FW]
            gq = a32[:, 7168 + sl * 1024:7168 + sl * 1024 + FW]
            gs = gq
            ki = ai[:, sl * 1024:sl * 1024 + FW]
            ki2 = ai[:, 2048:2048 + FW]
            qv = load_done[i]
            stv = store_done.get(i - 2, 0)
            gbx, gsx, kix, slx, ix = gbuf, gs, ki, sl, i

            @blk.vector
            def _(v):
                chi[ix % 4].wait(v, qv)
                cho[slx].wait(v, stv)
                v.tensor_scalar(kix, gbx, self.s_sc[:, SL["r_g"]:SL["r_g"] + 1],
                                32768.0, ALU.mult, ALU.add)
                v.drain().then_inc(ev, 1)
            c_ev.inc()
            kv = c_ev.v
            sgx = sig

            @blk.scalar
            def _(s_):
                s_.wait_ge(ev, kv)
                s_.activation(sgx, kix, AF.Sigmoid,
                              bias=self.s_sc[:, SL["b_g"]:SL["b_g"] + 1],
                              scale=self.s_sc[:, SL["s_g"]:SL["s_g"] + 1])
                s_.drain().then_inc(es, 1)
            c_es.inc()
            sv = c_es.v
            sqx, gqx, k2x = sigq, gq, ki2

            @blk.vector
            def _(v):
                v.wait_ge(es, sv)
                v.tensor_scalar(k2x, sgx, 65536.0, 0.0, ALU.mult, ALU.add)
                v.drain()
                v.tensor_scalar(sqx, k2x, 65535.0, 2.0 ** -16, ALU.min, ALU.mult)
                v.drain()
                v.tensor_scalar(gqx, kix, -32768.0,
                                self.s_sc[:, SL["s_g"]:SL["s_g"] + 1],
                                ALU.add, ALU.mult)
                v.drain()
                v.tensor_tensor(gsx, gsx, sqx, ALU.mult)
                v.drain()
                v.reduce_max(self.s_st[:, MXC["tmp"]:MXC["tmp"] + 1], gsx,
                             axis=AX.X, apply_absolute_value=True)
                v.drain()
                v.tensor_tensor(self.s_st[:, MXC["gs"]:MXC["gs"] + 1],
                                self.s_st[:, MXC["gs"]:MXC["gs"] + 1],
                                self.s_st[:, MXC["tmp"]:MXC["tmp"] + 1], ALU.max)
                v.drain().then_inc(ev, 1)
            c_ev.inc()
            proc_done[i] = c_ev.v

            if i + 4 < nTile:
                evv, nt = c_ev.v - 1, i + 4

                @blk.gpsimd
                def _(q):
                    q.wait_ge(ev, evv)
                    load_done[nt] = chi[nt % 4].dma(
                        q, a32[:, 1024 + (nt % 4) * 1024:
                               1024 + (nt % 4) * 1024 + FW], g_t(nt))

            pv = proc_done[i]
            ii = i

            @blk.gpsimd
            def _(g):
                g.wait_ge(ev, pv)
                cho[slx].dma(g, g_t(ii), gsx)
            store_done[i] = cho[slx].v

    def _ffn_d_state(self):
        if not hasattr(self, "_p5"):
            TOK = self.TOK
            FW = min(512, TOK)
            self._p5 = dict(i=0, FW=FW,
                            nTile=(self.cfg["I"] // 128) * (TOK // FW),
                            load_done={}, proc={}, store={}, started=False)
        return self._p5

    def emit_ffn_d_range(self, n):
        st = self._ffn_d_state()
        blk = self.blk
        a32, ai = self.a32, self.ai
        ev, c_ev = self.S("ev"), self.C("ev")
        SL, MXC = self.SL, self.MX
        FW = st["FW"]
        nTile = st["nTile"]
        nF = self.TOK // FW
        gs_t4 = self.gateT.rearrange("(c p) (n f) -> c n p f", p=128, f=FW)
        up_t4 = self.upT.rearrange("(c p) (n f) -> c n p f", p=128, f=FW)
        gs_t = lambda i: gs_t4[i // nF, i % nF]
        up_t = lambda i: up_t4[i // nF, i % nF]
        chi = [self.chan(f"ld_d{i}") for i in range(4)]
        cho = [self.chan("st_d0"), self.chan("st_d1")]
        load_done, proc_done, store_done = st["load_done"], st["proc"], st["store"]

        if not st["started"]:
            st["started"] = True
            scl = self.scale_ready
            pc0 = self.chan("st_c0").v
            pc1 = self.chan("st_c1").v

            @blk.gpsimd
            def _(q):
                q.wait_ge(ev, scl)
                self.chan("st_c0").wait(q, pc0)
                self.chan("st_c1").wait(q, pc1)
                for i in range(min(4, nTile)):
                    s4 = i % 4
                    chi[s4].dma(q, a32[:, 1024 + s4 * 512:1024 + s4 * 512 + FW],
                                gs_t(i))
                    load_done[i] = chi[s4].dma(
                        q, a32[:, 3072 + s4 * 512:3072 + s4 * 512 + FW],
                        up_t(i))

        i0 = st["i"]
        i1 = min(i0 + n, nTile)
        st["i"] = i1
        for i in range(i0, i1):
            sl = i % 2
            s4 = i % 4
            gsb = a32[:, 1024 + s4 * 512:1024 + s4 * 512 + FW]
            upb = a32[:, 3072 + s4 * 512:3072 + s4 * 512 + FW]
            g2 = a32[:, 5120:5120 + FW]
            uq = a32[:, 5632:5632 + FW]
            ob = a32[:, 6144 + sl * 512:6144 + sl * 512 + FW]
            ki = ai[:, 0:FW]
            ki2 = ai[:, 1024:1024 + FW]
            qv = load_done[i]
            stv = store_done.get(i - 2, 0)
            gx, ux, g2x, uqx, obx, slx, s4x = gsb, upb, g2, uq, ob, sl, i % 4

            @blk.vector
            def _(v):
                chi[s4x].wait(v, qv)
                cho[slx].wait(v, stv)
                v.tensor_scalar(ki, gx,
                                self.s_sc[:, SL["r_gs"]:SL["r_gs"] + 1],
                                32768.0, ALU.mult, ALU.add)
                v.drain()
                v.tensor_scalar(ki2, ux,
                                self.s_sc[:, SL["r_up"]:SL["r_up"] + 1],
                                32768.0, ALU.mult, ALU.add)
                v.drain()
                v.tensor_scalar(g2x, ki, -32768.0,
                                self.s_sc[:, SL["s_gs"]:SL["s_gs"] + 1],
                                ALU.add, ALU.mult)
                v.drain()
                v.tensor_scalar(uqx, ki2, -32768.0,
                                self.s_sc[:, SL["s_up"]:SL["s_up"] + 1],
                                ALU.add, ALU.mult)
                v.drain()
                v.tensor_tensor(obx, g2x, uqx, ALU.mult)
                v.drain()
                v.reduce_max(self.s_st[:, MXC["tmp"]:MXC["tmp"] + 1], obx,
                             axis=AX.X, apply_absolute_value=True)
                v.drain()
                v.tensor_tensor(self.s_st[:, MXC["o"]:MXC["o"] + 1],
                                self.s_st[:, MXC["o"]:MXC["o"] + 1],
                                self.s_st[:, MXC["tmp"]:MXC["tmp"] + 1], ALU.max)
                v.drain().then_inc(ev, 1)
            c_ev.inc()
            proc_done[i] = c_ev.v

            if i + 4 < nTile:
                evv, nt = proc_done[i], i + 4

                @blk.gpsimd
                def _(q):
                    q.wait_ge(ev, evv)
                    sl4 = nt % 4
                    chi[sl4].dma(q, a32[:, 1024 + sl4 * 512:
                                        1024 + sl4 * 512 + FW], gs_t(nt))
                    load_done[nt] = chi[sl4].dma(
                        q, a32[:, 3072 + sl4 * 512:3072 + sl4 * 512 + FW],
                        up_t(nt))

            pv = proc_done[i]
            ii = i

            @blk.gpsimd
            def _(g):
                g.wait_ge(ev, pv)
                cho[slx].dma(g, up_t(ii), obx)
            store_done[i] = cho[slx].v

    def emit_ffn_down(self):
        blk = self.blk
        c = self.cfg
        D, I, TOK = c["D"], c["I"], self.TOK
        nIc, nDc = I // 128, D // 128
        TW = 256 if TOK >= 512 else TOK
        nG = TOK // TW
        a16, a32, ai = self.a16, self.a32, self.ai
        ev, c_ev = self.S("ev"), self.C("ev")
        es, c_es = self.S("es"), self.C("es")
        et, c_et = self.S("et"), self.C("et")
        SL = self.SL
        o_t = self.upT.rearrange("(c p) n -> c p n", p=128)
        chi = [self.chan("ld_o0"), self.chan("ld_o1")]
        chs = [self.chan("st_dn0"), self.chan("st_dn1")]
        load_done = {}
        qdq_done = {}
        mm_done = {}
        prep_es = self.PREP_DONE_wd
        ch7 = self.chan("ld_fin")
        chx7 = self.chan("ld_px0")
        chout = [self.chan("st_out0"), self.chan("st_out1")]
        x_t7 = self.x.rearrange("(n p) d -> n p d", p=128)
        out_t7 = self.out.rearrange("(n p) d -> n p d", p=128)
        nJ = TW // 128
        f7 = dict(first=True, at0=self.chan("st_at0").v,
                  at1=self.chan("st_at1").v, tr={}, fin={}, ov={},
                  st={}, ld={}, xv={}, chsv={})
        self._fd_first = True

        def emit_lq(g, ics):
            gsl = g % 2
            OQ = 49152 + gsl * 12288
            mmv = mm_done.get(g - 2, 0)
            for ic in ics:
                sl = ic % 2
                ob = a32[:, sl * 512:sl * 512 + TW]
                evv = qdq_done.get((g, ic - 2),
                                   qdq_done.get((g - 1, nIc - 1), 0))
                obx, gx, icx, slx, OQx = ob, g, ic, sl, OQ
                first_l = self._fd_first
                self._fd_first = False

                @blk.sync
                def _(q):
                    if first_l:
                        self.qbar(q)
                        q.wait_ge(es, prep_es)
                    q.wait_ge(ev, evv)
                    load_done[(gx, icx)] = chi[slx].dma(
                        q, obx, o_t[icx, :, gx * TW:gx * TW + TW])

                qv = load_done[(g, ic)]
                mmvx = mmv

                @blk.vector
                def _(v):
                    chi[slx].wait(v, qv)
                    v.wait_ge(et, mmvx)
                    ki = ai[:, 0:TW]
                    qf = a32[:, 1024:1024 + TW]
                    hi = a16[:, OQx + icx * TW:OQx + icx * TW + TW]
                    lo = a16[:, OQx + nIc * TW + icx * TW:
                             OQx + nIc * TW + icx * TW + TW]
                    v.tensor_scalar(ki, obx,
                                    self.s_sc[:, SL["r_o"]:SL["r_o"] + 1],
                                    32768.0, ALU.mult, ALU.add)
                    v.drain()
                    v.tensor_scalar(hi, ki, -32768.0,
                                    self.s_sc[:, SL["s_o"]:SL["s_o"] + 1],
                                    ALU.add, ALU.mult)
                    v.drain()
                    v.tensor_scalar(qf, ki, -32768.0,
                                    self.s_sc[:, SL["s_o"]:SL["s_o"] + 1],
                                    ALU.add, ALU.mult)
                    v.drain()
                    v.tensor_tensor(lo, qf, hi, ALU.subtract)
                    v.drain().then_inc(ev, 1)
                c_ev.inc()
                qdq_done[(g, ic)] = c_ev.v

        def f7_loads(gc):
            off = gc * TW
            v0, v1 = f7["chsv"][gc]
            first_f = f7["first"]
            f7["first"] = False
            at0v, at1v = f7["at0"], f7["at1"]
            trl = f7["tr"].get(gc - 1, 0)
            stp = [f7["st"].get((gc - 1, j), 0) for j in range(2)]
            ovp = f7["ov"].get((gc - 1, nJ - 1), 0)

            @blk.sync
            def _(q):
                if first_f:
                    self.chan("st_at0").wait(q, at0v)
                    self.chan("st_at1").wait(q, at1v)
                chs[0].wait(q, v0)
                chs[1].wait(q, v1)
                q.wait_ge(et, trl)
                chout[0].wait(q, stp[0])
                chout[1].wait(q, stp[1])
                ch7.dma(q, a32[:, 3072:5120].rearrange("p (c t) -> p c t",
                                                       t=TW),
                        self.attT.rearrange("(c p) n -> p c n", p=128)[
                            :, :, off:off + TW])
                ch7.dma(q, a32[:, 5120:7168].rearrange("p (c t) -> p c t",
                                                       t=TW),
                        self.ffnT.rearrange("(c p) n -> p c n", p=128)[
                            :, :, off:off + TW])
            f7["ld"][gc] = ch7.v
            ovpx = ovp

            @blk.sync
            def _(q):
                q.wait_ge(ev, ovpx)
                for j in range(nJ):
                    chx7.dma(q, a32[:, 7168 + j * 1024:7168 + j * 1024 + 1024],
                             x_t7[off // 128 + j])
            f7["xv"][gc] = chx7.v

        def f7_fin(gc):
            ldv = f7["ld"][gc]

            @blk.vector
            def _(v):
                ch7.wait(v, ldv)
                v.tensor_tensor(a32[:, 3072:5120], a32[:, 3072:5120],
                                a32[:, 5120:7168], ALU.add)
                v.drain().then_inc(ev, 1)
            c_ev.inc()
            f7["fin"][gc] = c_ev.v

        def f7_tr(gc):
            fdx = f7["fin"][gc]
            pav = f7["ov"].get((gc - 1, nJ - 1), 0)

            @blk.tensor
            def _(t):
                t.wait_ge(ev, fdx)
                t.wait_ge(ev, pav)
                for j in range(nJ):
                    for dc in range(nDc):
                        t.transpose(
                            self.ps_a[:, j * 1024 + dc * 128:
                                      j * 1024 + dc * 128 + 128],
                            a32[:, 3072 + dc * TW + j * 128:
                                3072 + dc * TW + j * 128 + 128],
                            self.s_idf[:])
                t.drain().then_inc(et, 1)
            c_et.inc()
            f7["tr"][gc] = c_et.v

        def f7_ostg(gc, j):
            off = gc * TW
            trv = f7["tr"][gc]
            xv = f7["xv"][gc]
            stvp = f7["st"].get((gc - 1, j), 0)
            jx = j

            @blk.vector
            def _(v):
                v.wait_ge(et, trv)
                chx7.wait(v, xv)
                chout[jx].wait(v, stvp)
                v.tensor_tensor(a32[:, 5120 + jx * 1024:5120 + jx * 1024 + 1024],
                                self.ps_a[:, jx * 1024:jx * 1024 + 1024],
                                a32[:, 7168 + jx * 1024:7168 + jx * 1024 + 1024],
                                ALU.add)
                v.drain().then_inc(ev, 1)
            c_ev.inc()
            f7["ov"][(gc, j)] = c_ev.v
            ovv = c_ev.v
            tix = off // 128 + j

            @blk.gpsimd
            def _(q):
                q.wait_ge(ev, ovv)
                chout[jx].dma(q, out_t7[tix],
                              a32[:, 5120 + jx * 1024:5120 + jx * 1024 + 1024])
            f7["st"][(gc, j)] = chout[j].v

        emit_lq(0, range(nIc))
        for g in range(nG):
            gsl = g % 2
            OQ = 49152 + gsl * 12288
            if g >= 1:
                f7_loads(g - 1)
            ics_per_dc = nIc // nDc
            for dc in range(nDc):
                bank_key = "d" if dc % 2 == 0 else "e"
                ps = self.ps_d if dc % 2 == 0 else self.ps_e
                evv = qdq_done[(g, nIc - 1)]
                psx, dcx, gx, OQx = ps, dc, g, OQ

                @blk.tensor
                def _(t):
                    t.wait_ge(ev, evv)
                    self.bank_wait(t, bank_key)
                    first_mm = True
                    for ic in range(nIc):
                        for (sp, mp) in ((0, 0), (0, 1), (1, 0)):
                            rhs = a16[:, OQx + mp * nIc * TW + ic * TW:
                                      OQx + mp * nIc * TW + ic * TW + TW]
                            t.matmul(psx[:, 0:TW],
                                     self.lhsT("wd", dcx, ic, sp), rhs,
                                     start=first_mm,
                                     stop=(ic == nIc - 1 and (sp, mp) == (1, 0)))
                            first_mm = False
                    t.drain().then_inc(et, 1)
                c_et.inc()
                mm_done[g] = c_et.v
                mmv2 = c_et.v

                stg = a32[:, 2048 + (dc % 2) * 512:2048 + (dc % 2) * 512 + TW]
                stv = chs[dc % 2].v
                stgx = stg

                @blk.vector
                def _(v):
                    v.wait_ge(et, mmv2)
                    chs[dcx % 2].wait(v, stv)
                    v.tensor_copy(stgx, psx[:, 0:TW])
                    v.drain().then_inc(ev, 1)
                c_ev.inc()
                self.bank_free(bank_key, "ev")
                dv = c_ev.v

                @blk.gpsimd
                def _(g_):
                    g_.wait_ge(ev, dv)
                    chs[dcx % 2].dma(g_, self.ffnT[dcx * 128:dcx * 128 + 128,
                                                   gx * TW:gx * TW + TW],
                                     stgx)

                if g + 1 < nG:
                    emit_lq(g + 1, range(dc * ics_per_dc,
                                         (dc + 1) * ics_per_dc))
                if g >= 1:
                    if dc == 3:
                        f7_fin(g - 1)
                    elif dc == 4:
                        f7_tr(g - 1)
                    elif dc == 5:
                        f7_ostg(g - 1, 0)
                    elif dc == 6 and nJ > 1:
                        f7_ostg(g - 1, 1)
            f7["chsv"][g] = (chs[0].v, chs[1].v)

        gl = nG - 1
        f7_loads(gl)
        f7_fin(gl)
        f7_tr(gl)
        for j in range(nJ):
            f7_ostg(gl, j)

    def _p7a_state(self):
        if not hasattr(self, "_p7a"):
            self._p7a = dict(u=0, dc=0, started=False)
        return self._p7a

    def emit_combine_mm_range(self, n_dc):
        st = self._p7a_state()
        blk = self.blk
        c = self.cfg
        B, T = c["B"], c["T"]
        HD = self.HD
        nHc = HD // 128
        nDc = c["D"] // 128
        a16, a32 = self.a16, self.a32
        ev, c_ev = self.S("ev"), self.C("ev")
        et, c_et = self.S("et"), self.C("et")
        OG_HI, OG_LO = 32768, 40960
        chg = self.chan("ld_og")
        chs = [self.chan("st_at0"), self.chan("st_at1")]
        units = [(s, qgi) for s in range(B) for qgi in range(T // 512)]

        for _ in range(n_dc):
            u, dc = st["u"], st["dc"]
            if u >= len(units):
                return
            s, qgi = units[u]
            if dc == 0:
                first_l = not st["started"]
                st["started"] = True
                mm_prev = st.get("mm_done", 0)
                sx, qx = s, qgi

                @blk.sync
                def _(q):
                    if first_l:
                        self.qbar(q)
                        q.wait_ge(self.S("es"), self.PREP_DONE_wo)
                    else:
                        q.wait_ge(et, mm_prev)
                    for hl, src_, dsto in ((0, self.oT_hi, OG_HI),
                                           (1, self.oT_lo, OG_LO)):
                        chg.dma(q, a16[:, dsto:dsto + nHc * 512].rearrange(
                            "p (c t) -> p c t", t=512),
                            src_.rearrange("b (c p) t -> b p c t", p=128)[
                                sx, :, :, qx * 512:qx * 512 + 512])
                st["loads"] = chg.v

            loads = st["loads"]
            bank_key = "d" if dc % 2 == 0 else "e"
            ps = self.ps_d if dc % 2 == 0 else self.ps_e
            psx, dcx = ps, dc

            @blk.tensor
            def _(t):
                chg.wait(t, loads)
                self.bank_wait(t, bank_key)
                first_mm = True
                for hc in range(nHc):
                    for (sp, mp) in ((0, 0), (0, 1), (1, 0)):
                        rhs = a16[:, (OG_HI if mp == 0 else OG_LO) +
                                  hc * 512:
                                  (OG_HI if mp == 0 else OG_LO) +
                                  hc * 512 + 512]
                        mm = t.matmul(psx[:, 0:512],
                                      self.lhsT("wo", dcx, hc, sp), rhs,
                                      start=first_mm,
                                      stop=(hc == nHc - 1 and
                                            (sp, mp) == (1, 0)))
                        first_mm = False
                mm.then_inc(et, 1)
            c_et.inc()
            st["mm_done"] = c_et.v
            mmv = c_et.v
            stg = a32[:, (dc % 2) * 512:(dc % 2) * 512 + 512]
            stv = chs[dc % 2].v
            stgx = stg

            @blk.vector
            def _(v):
                v.wait_ge(et, mmv)
                chs[dcx % 2].wait(v, stv)
                v.tensor_copy(stgx, psx[:, 0:512])
                v.drain().then_inc(ev, 1)
            c_ev.inc()
            self.bank_free(bank_key, "ev")
            dv = c_ev.v
            off = (s * T + qgi * 512)

            @blk.gpsimd
            def _(g):
                g.wait_ge(ev, dv)
                chs[dcx % 2].dma(g, self.attT[dcx * 128:dcx * 128 + 128,
                                              off:off + 512], stgx)

            if dc + 1 == nDc:
                st["u"], st["dc"] = u + 1, 0
            else:
                st["dc"] = dc + 1

    def emit_combine_final(self):
        blk = self.blk
        c = self.cfg
        B, T, D = c["B"], c["T"], c["D"]
        nDc = D // 128
        a32 = self.a32
        ev, c_ev = self.S("ev"), self.C("ev")
        es, c_es = self.S("es"), self.C("es")
        et, c_et = self.S("et"), self.C("et")
        x_t = self.x.rearrange("(n p) d -> n p d", p=128)
        out_t = self.out.rearrange("(n p) d -> n p d", p=128)
        chg = self.chan("ld_fin")
        chxx = [self.chan("ld_px0"), self.chan("ld_px1")]
        chout = [self.chan("st_out0"), self.chan("st_out1")]
        first = True
        grp_done = 0
        xf_free = {}
        pa_free = {}
        stv_prev = {}

        for s in range(B):
            for qgi in range(T // 512):
                sx, qx = s, qgi
                first_l = first
                first = False
                gv = grp_done

                @blk.sync
                def _(q):
                    if first_l:
                        self.qbar(q)
                    else:
                        q.wait_ge(ev, gv)
                    chg.dma(q, a32[:, 0:4096].rearrange("p (c t) -> p c t", t=512),
                            self.attT.rearrange("(c p) n -> p c n", p=128)[
                                :, :, sx * T + qx * 512:sx * T + qx * 512 + 512])
                    chg.dma(q, a32[:, 4096:8192].rearrange("p (c t) -> p c t", t=512),
                            self.ffnT.rearrange("(c p) n -> p c n", p=128)[
                                :, :, sx * T + qx * 512:sx * T + qx * 512 + 512])
                loads = chg.v

                @blk.vector
                def _(v):
                    chg.wait(v, loads)
                    v.tensor_tensor(a32[:, 0:4096], a32[:, 0:4096],
                                    a32[:, 4096:8192], ALU.add)
                    v.drain().then_inc(ev, 1)
                c_ev.inc()
                fin_done = c_ev.v

                for j in range(4):
                    ti = (s * T + qgi * 512) // 128 + j
                    jx, tix = j, ti
                    xev = xf_free.get(j % 2, 0)

                    @blk.sync
                    def _(q):
                        q.wait_ge(ev, max(xev, fin_done))
                        chxx[jx % 2].dma(q, a32[:, 4096 + (jx % 2) * 1024:
                                                4096 + (jx % 2) * 1024 + 1024],
                                         x_t[tix])
                    xload = chxx[j % 2].v

                    pav = pa_free.get(j % 2, 0)

                    @blk.tensor
                    def _(t):
                        t.wait_ge(ev, fin_done)
                        t.wait_ge(ev, pav)
                        psl = (jx % 2) * 1024
                        for dc in range(nDc):
                            t.transpose(
                                self.ps_a[:, psl + dc * 128:psl + dc * 128 + 128],
                                a32[:, dc * 512 + jx * 128:
                                    dc * 512 + jx * 128 + 128],
                                self.s_idf[:])
                        t.drain().then_inc(et, 1)
                    c_et.inc()
                    trv = c_et.v
                    stv = chout[j % 2].v

                    @blk.vector
                    def _(v):
                        v.wait_ge(et, trv)
                        chxx[jx % 2].wait(v, xload)
                        chout[jx % 2].wait(v, stv)
                        psl = (jx % 2) * 1024
                        ostg = a32[:, 6144 + (jx % 2) * 1024:
                                   6144 + (jx % 2) * 1024 + 1024]
                        v.tensor_tensor(ostg, self.ps_a[:, psl:psl + 1024],
                                        a32[:, 4096 + (jx % 2) * 1024:
                                            4096 + (jx % 2) * 1024 + 1024],
                                        ALU.add)
                        v.drain().then_inc(ev, 1)
                    c_ev.inc()
                    xf_free[j % 2] = c_ev.v
                    pa_free[j % 2] = c_ev.v
                    osv = c_ev.v

                    @blk.gpsimd
                    def _(g):
                        g.wait_ge(ev, osv)
                        chout[jx % 2].dma(
                            g, out_t[tix],
                            a32[:, 6144 + (jx % 2) * 1024:
                                6144 + (jx % 2) * 1024 + 1024])
                grp_done = c_ev.v


_CACHE = {}


def _get_nc():
    if "full" not in _CACHE:
        _CACHE["full"] = MK(CFG_FULL).build()
    return _CACHE["full"]


def kernel(L0_attn_x, L0_attn_Wq, L0_attn_Wkv_down, L0_attn_Wk_up,
           L0_attn_Wv_up, L0_attn_Wo, L0_ffn_w_gate, L0_ffn_w_up,
           L0_ffn_w_down):
    x = np.ascontiguousarray(np.asarray(L0_attn_x, dtype=np.float32))
    B_all, T, D = x.shape
    per = B_all // NCORES
    nc = _get_nc()
    idf = np.eye(128, dtype=np.float32)
    idb = np.eye(128, dtype=ml_dtypes.bfloat16)
    common = dict(
        Wq=np.ascontiguousarray(L0_attn_Wq, dtype=np.float32),
        Wkv=np.ascontiguousarray(L0_attn_Wkv_down, dtype=np.float32),
        Wku=np.ascontiguousarray(L0_attn_Wk_up, dtype=np.float32),
        Wvu=np.ascontiguousarray(L0_attn_Wv_up, dtype=np.float32),
        Wo=np.ascontiguousarray(L0_attn_Wo, dtype=np.float32),
        Wg=np.ascontiguousarray(L0_ffn_w_gate, dtype=np.float32),
        Wu=np.ascontiguousarray(L0_ffn_w_up, dtype=np.float32),
        Wd=np.ascontiguousarray(L0_ffn_w_down, dtype=np.float32),
        idf=idf, idb=idb,
    )
    in_maps = []
    for ci in range(NCORES):
        m = dict(common)
        m["x"] = np.ascontiguousarray(
            x[ci * per:(ci + 1) * per].reshape(per * T, D))
        in_maps.append(m)
    res = run_bass_kernel_spmd(nc, in_maps, core_ids=list(range(NCORES)))
    out = np.concatenate(
        [np.asarray(r["out"]).reshape(per, T, D) for r in res.results], axis=0)
    return out


# version 26 (final)
# speedup vs baseline: 1.0380x; 1.0123x over previous
import numpy as np
import ml_dtypes
from contextlib import ExitStack

import concourse.bass as bass
import concourse.mybir as mybir
import concourse.bass_utils as _bu
from concourse.bass_utils import run_bass_kernel_spmd

_orig_run_command = _bu.run_command


def _run_command_ldwopt(cmd, *a, **kw):
    if isinstance(cmd, list):
        cmd = ["--enable-ldw-opt=true" if c == "--enable-ldw-opt=false" else c
               for c in cmd]
    return _orig_run_command(cmd, *a, **kw)


_bu.run_command = _run_command_ldwopt

F32 = mybir.dt.float32
BF16 = mybir.dt.bfloat16
I32 = mybir.dt.int32
ALU = mybir.AluOpType
AF = mybir.ActivationFunctionType
AX = mybir.AxisListType

NCORES = 8
CFG_FULL = dict(B=4, T=2048, D=1024, H=8, dh=256, L=64, I=3072)


class Ctr:
    def __init__(self):
        self.v = 0

    def inc(self, n=1):
        self.v += n
        return self.v


class Chan:

    def __init__(self, sem):
        self.sem = sem
        self.c = Ctr()

    def dma(self, q, out, in_):
        q.dma_start(out=out, in_=in_).then_inc(self.sem, 16)
        self.c.inc(16)
        return self.c.v

    @property
    def v(self):
        return self.c.v

    def wait(self, eng, v=None):
        eng.wait_ge(self.sem, self.c.v if v is None else v)


class MK:
    def __init__(self, cfg):
        self.cfg = dict(cfg)
        self.nc = bass.Bass("TRN2", target_bir_lowering=False, debug=False)
        self.bank = {}
        self.stg_store = {}

    def build(self):
        c = self.cfg
        nc = self.nc
        B, T, D, H, dh, L, I = (c["B"], c["T"], c["D"], c["H"], c["dh"],
                                c["L"], c["I"])
        TOK = B * T
        HD = H * dh
        self.TOK, self.HD = TOK, HD

        din = lambda n, s, dt=F32: nc.dram_tensor(n, s, dt, kind="ExternalInput").ap()
        self.x = din("x", [TOK, D])
        self.Wq = din("Wq", [HD, D])
        self.Wkv = din("Wkv", [L, D])
        self.Wku = din("Wku", [HD, L])
        self.Wvu = din("Wvu", [HD, L])
        self.Wo = din("Wo", [D, HD])
        self.Wg = din("Wg", [I, D])
        self.Wu = din("Wu", [I, D])
        self.Wd = din("Wd", [D, I])
        self.idf = din("idf", [128, 128])
        self.idb = din("idb", [128, 128], BF16)
        self.out = nc.dram_tensor("out", [TOK, D], F32, kind="ExternalOutput").ap()

        dbg = getattr(MK, "DEBUG_DUMP", ())

        def dsc(n, s, dt=F32):
            if n in dbg:
                return nc.dram_tensor(n, s, dt, kind="ExternalOutput").ap()
            return nc.dram_tensor(n, s, dt).ap()
        self.xnT_hi = dsc("xnT_hi", [D, TOK], BF16)
        self.xnT_lo = dsc("xnT_lo", [D, TOK], BF16)
        self.xqT_hi = dsc("xqT_hi", [D, TOK], BF16)
        self.xqT_lo = dsc("xqT_lo", [D, TOK], BF16)
        self.q_hi = dsc("q_hi", [B, HD, T], BF16)
        self.q_lo = dsc("q_lo", [B, HD, T], BF16)
        self.k_hi = dsc("k_hi", [B, HD, T], BF16)
        self.k_lo = dsc("k_lo", [B, HD, T], BF16)
        self.v_hi = dsc("v_hi", [B, T, HD], BF16)
        self.v_lo = dsc("v_lo", [B, T, HD], BF16)
        self.oT_hi = dsc("oT_hi", [B, HD, T], BF16)
        self.oT_lo = dsc("oT_lo", [B, HD, T], BF16)
        self.upT = dsc("upT", [I, TOK])
        self.gateT = dsc("gateT", [I, TOK])
        self.ffnT = dsc("ffnT", [D, TOK])
        self.attT = dsc("attT", [D, TOK])
        self.dbg_raw = {}
        for nm in ("up_raw", "gate_raw"):
            if nm in dbg:
                self.dbg_raw[nm] = dsc(nm, [I, TOK])
        self.cc_in = [dsc(f"cc_in{i}", [1, 2]) for i in range(4)]
        self.cc_out = [dsc(f"cc_out{i}", [NCORES, 2]) for i in range(4)]

        self._ctx = ExitStack()
        e = self._ctx.enter_context
        sb = lambda n, s, dt: e(nc.sbuf_tensor(n, s, dt))
        self.s_idf = sb("s_idf", [128, 128], F32)
        self.s_on16 = sb("s_on16", [128, 128], BF16)
        self.s_idb = sb("s_idb", [128, 128], BF16)
        self.s_ones = sb("s_ones", [1, 128], F32)
        self.s_sc = sb("s_sc", [128, 16], F32)
        self.s_st = sb("s_st", [128, 256], F32)
        self.s_cc = sb("s_cc", [1, 32], F32)
        self.s_sr = sb("s_sr", [1, 16], F32)
        self.s_vec = sb("s_vec", [128, 16], F32)
        self.a16 = sb("a16", [128, 78 * 1024], BF16)
        self.a32 = sb("a32", [128, 9216], F32)
        self.ai = sb("ai", [128, 3072], I32)
        self.ps_a = e(nc.psum_tensor("ps_a", [128, 2048], F32))
        self.ps_b = e(nc.psum_tensor("ps_b", [128, 1024], BF16))
        self.ps_c = e(nc.psum_tensor("ps_c", [128, 1024], BF16))
        self.ps_d = e(nc.psum_tensor("ps_d", [128, 512], F32))
        self.ps_e = e(nc.psum_tensor("ps_e", [128, 512], F32))

        self.sems = {}
        for name in ["cc", "ev", "es", "et"]:
            self.sems[name] = (e(nc.semaphore(name)), Ctr())
        self.chans = {}
        self._enter = e
        self.blk = e(nc.Block())

        self.emit_all()
        self._ctx.close()
        return nc

    def S(self, n):
        return self.sems[n][0]

    def C(self, n):
        return self.sems[n][1]

    def chan(self, name):
        if name not in self.chans:
            self.chans[name] = Chan(self._enter(self.nc.semaphore("ch_" + name)))
        return self.chans[name]

    def bank_wait(self, eng, *keys):
        for k in keys:
            if k in self.bank:
                sem_name, val = self.bank[k]
                eng.wait_ge(self.S(sem_name), val)

    def bank_free(self, key, sem_name):
        self.bank[key] = (sem_name, self.C(sem_name).v)

    def qbar(self, q):
        for n in ["ev", "es", "et"]:
            q.wait_ge(self.S(n), self.C(n).v)
        for name, ch in list(self.chans.items()):
            if name.startswith("st_"):
                ch.wait(q)

    SL = dict(s_x=0, r_x=1, s_up=2, r_up=3, s_g=4, r_g=5, b_g=6,
              s_gs=7, r_gs=8, s_o=9, r_o=10)
    MX = dict(tmp=0, xn=1, up=2, gate=3, gs=4, o=5)

    def emit_all(self):
        c = self.cfg
        B = c["B"]
        self.emit_preamble()
        self.emit_ln()
        self.emit_scale_round(0, [("xn", "s_x", "r_x", 32767.0)])
        self.emit_prep_lhsT(self.Wkv, 69632, dequant=False, tag="kvd")
        self.emit_prep_lhsT(self.Wku, 74752, dequant=False, tag="kup")
        self.emit_prep_lhsT(self.Wvu, 70656, dequant=False, tag="vup")
        self.emit_prep_lhsT(self.Wq, 0, dequant=False, tag="wq")
        for s in range(B):
            self.emit_proj_q(s)
        for s in range(B):
            self.emit_proj_kv(s)
        self.emit_xq()
        self.emit_prep_lhsT(self.Wu, 0, dequant=True, tag="wu")
        self.emit_ffn_mm(self.upT, "up", "wu")
        self.emit_prep_lhsT(self.Wg, 0, dequant=True, tag="wg")
        self.emit_ffn_mm(self.gateT, "gate", "wg")
        self.emit_scale_round(1, [("up", "s_up", "r_up", 32767.0),
                                  ("gate", "s_g", "r_g", 32767.0)])
        for s in range(B):
            for h in range(c["H"]):
                self.emit_flash(s, h)
        self.emit_prep_lhsT(self.Wo, 0, dequant=False, tag="wo")
        nu = B * (c["T"] // 512)
        nDc = c["D"] // 128
        nc_t = self._ffn_c_state()["nTile"]
        nd_t = self._ffn_d_state()["nTile"]
        ns1 = (nu // 2) * nDc
        for j in range(ns1):
            self.emit_combine_mm_range(1)
            self.emit_ffn_c_range(-(-nc_t // ns1))
        self.emit_ffn_c_range(10 ** 9)
        self.emit_scale_round(2, [("gs", "s_gs", "r_gs", 32767.0)])
        ns2 = (nu - nu // 2) * nDc
        for j in range(ns2):
            self.emit_combine_mm_range(1)
            self.emit_ffn_d_range(-(-nd_t // ns2))
        self.emit_ffn_d_range(10 ** 9)
        self.emit_scale_round(3, [("o", "s_o", "r_o", 32767.0)])
        self.emit_prep_lhsT(self.Wd, 0, dequant=True, tag="wd")
        self.emit_ffn_down()

        chans = list(self.chans.values())

        @self.blk.gpsimd
        def _(g):
            for ch in chans:
                ch.wait(g)

    def emit_preamble(self):
        blk = self.blk
        ev, c_ev = self.S("ev"), self.C("ev")
        ch = self.chan("ld_pre")

        @blk.sync
        def _(q):
            ch.dma(q, self.s_idf[:], self.idf)
            ch.dma(q, self.s_idb[:], self.idb)

        @blk.vector
        def _(v):
            ch.wait(v)
            v.memset(self.s_ones[:], 1.0)
            v.memset(self.s_st[:], -1e30)
            v.memset(self.s_cc[:], 0.0)
            v.memset(self.s_sr[:], 0.0)
            v.memset(self.s_vec[:], 0.0)
            v.drain()
            v.memset(self.s_sc[:, 11:12], 1e-5)
            v.memset(self.s_sc[:, 12:13], 0.0)
            v.memset(self.s_sc[:, 13:14], -3.0)
            v.memset(self.s_on16[:], 1.0)
            v.drain().then_inc(ev, 1)
        c_ev.inc()

    def emit_ln(self):
        blk = self.blk
        D = self.cfg["D"]
        TOK = self.TOK
        nDc = D // 128
        nTt = TOK // 128
        ST = 512
        nST = TOK // ST
        a16, a32 = self.a16, self.a32
        ev, c_ev = self.S("ev"), self.C("ev")
        es, c_es = self.S("es"), self.C("es")
        et, c_et = self.S("et"), self.C("et")
        x_t = self.x.rearrange("(n p) d -> n p d", p=128)
        MXC = self.MX
        chx = [self.chan("ld_x0"), self.chan("ld_x1")]
        chst = self.chan("st_p0")

        free_at = {}
        split_free = {}
        store_done = []
        pre_ev = c_ev.v

        @blk.sync
        def _(q):
            q.wait_ge(ev, pre_ev)
            for i in range(min(2, nTt)):
                chx[i % 2].dma(q, a32[:, (i % 2) * 1024:(i % 2) * 1024 + 1024],
                               x_t[i])

        for sti in range(nST):
            for j in range(ST // 128):
                ti = sti * (ST // 128) + j
                xbuf = a32[:, (ti % 2) * 1024:(ti % 2) * 1024 + 1024]
                d = a32[:, 2048:3072]
                xn = a32[:, 3072:4096]
                sp_h = a16[:, 16384 + (ti % 2) * 1024:16384 + (ti % 2) * 1024 + 1024]
                sp_l = a16[:, 18432 + (ti % 2) * 1024:18432 + (ti % 2) * 1024 + 1024]
                ldv = chx[ti % 2].v

                @blk.vector
                def _(v):
                    chx[ti % 2].wait(v, ldv)
                    v.reduce_sum(self.s_vec[:, 0:1], xbuf, axis=AX.X)
                    v.drain()
                    v.tensor_scalar(self.s_vec[:, 1:2], self.s_vec[:, 0:1],
                                    1.0 / D, None, ALU.mult)
                    v.drain()
                    v.tensor_scalar(d, xbuf, self.s_vec[:, 1:2], None, ALU.subtract)
                    v.drain()
                    v.tensor_tensor(xn, d, d, ALU.mult)
                    v.drain()
                    v.reduce_sum(self.s_vec[:, 2:3], xn, axis=AX.X)
                    v.drain().then_inc(ev, 1)
                c_ev.inc()
                free_at[ti] = c_ev.v

                if ti + 2 < nTt:
                    evv, nt = free_at[ti], ti + 2

                    @blk.sync
                    def _(q):
                        q.wait_ge(ev, evv)
                        chx[nt % 2].dma(q, a32[:, (nt % 2) * 1024:
                                               (nt % 2) * 1024 + 1024], x_t[nt])

                t_ev = c_ev.v

                @blk.scalar
                def _(s):
                    s.wait_ge(ev, t_ev)
                    s.activation(self.s_vec[:, 3:4], self.s_vec[:, 2:3], AF.Ln,
                                 bias=self.s_sc[:, 11:12], scale=1.0 / D)
                    s.drain()
                    s.activation(self.s_vec[:, 4:5], self.s_vec[:, 3:4], AF.Exp,
                                 bias=self.s_sc[:, 12:13], scale=-0.5)
                    s.drain().then_inc(es, 1)
                c_es.inc()

                t_es = c_es.v
                etv = split_free.get(ti - 2, 0)

                @blk.vector
                def _(v):
                    v.wait_ge(es, t_es)
                    v.wait_ge(et, etv)
                    v.tensor_scalar(xn, d, self.s_vec[:, 4:5], None, ALU.mult)
                    v.drain()
                    v.reduce_max(self.s_st[:, MXC["tmp"]:MXC["tmp"] + 1], xn,
                                 axis=AX.X, apply_absolute_value=True)
                    v.drain()
                    v.tensor_tensor(self.s_st[:, MXC["xn"]:MXC["xn"] + 1],
                                    self.s_st[:, MXC["xn"]:MXC["xn"] + 1],
                                    self.s_st[:, MXC["tmp"]:MXC["tmp"] + 1],
                                    ALU.max)
                    v.drain()
                    v.tensor_copy(sp_h, xn)
                    v.drain()
                    v.tensor_tensor(sp_l, xn, sp_h, ALU.subtract)
                    v.drain().then_inc(ev, 1)
                c_ev.inc()

                tv = c_ev.v

                @blk.tensor
                def _(t):
                    t.wait_ge(ev, tv)
                    self.bank_wait(t, "b", "c")
                    for cdx in range(nDc):
                        t.transpose(self.ps_b[:, cdx * 128:cdx * 128 + 128],
                                    sp_h[:, cdx * 128:cdx * 128 + 128],
                                    self.s_idb[:])
                    for cdx in range(nDc):
                        t.transpose(self.ps_c[:, cdx * 128:cdx * 128 + 128],
                                    sp_l[:, cdx * 128:cdx * 128 + 128],
                                    self.s_idb[:])
                    t.drain().then_inc(et, 1)
                c_et.inc()
                split_free[ti] = c_et.v

                t_et = c_et.v
                stv = store_done[sti - 1] if sti >= 1 else 0
                jj = j

                @blk.vector
                def _(v):
                    v.wait_ge(et, t_et)
                    if jj == 0:
                        chst.wait(v, stv)
                    dst_h = a16[:, 0:8 * ST].rearrange(
                        "p (c t) -> p c t", t=ST)[:, :, jj * 128:jj * 128 + 128]
                    dst_l = a16[:, 4096:4096 + 8 * ST].rearrange(
                        "p (c t) -> p c t", t=ST)[:, :, jj * 128:jj * 128 + 128]
                    v.tensor_copy(dst_h,
                                  self.ps_b[:].rearrange("p (c t) -> p c t", t=128))
                    v.drain()
                    v.tensor_copy(dst_l,
                                  self.ps_c[:].rearrange("p (c t) -> p c t", t=128))
                    v.drain().then_inc(ev, 1)
                c_ev.inc()
                self.bank_free("b", "ev")
                self.bank_free("c", "ev")

            t_ev2 = c_ev.v
            sti_ = sti

            @blk.gpsimd
            def _(g):
                g.wait_ge(ev, t_ev2)
                src_h = a16[:, 0:8 * ST].rearrange("p (c t) -> p c t", t=ST)
                src_l = a16[:, 4096:4096 + 8 * ST].rearrange("p (c t) -> p c t", t=ST)
                dst_h = self.xnT_hi.rearrange("(c p) n -> p c n", p=128)[
                    :, :, sti_ * ST:sti_ * ST + ST]
                dst_l = self.xnT_lo.rearrange("(c p) n -> p c n", p=128)[
                    :, :, sti_ * ST:sti_ * ST + ST]
                chst.dma(g, dst_h, src_h)
                chst.dma(g, dst_l, src_l)
            store_done.append(chst.v)

    def emit_scale_round(self, round_i, items):
        blk = self.blk
        cc, c_cc = self.S("cc"), self.C("cc")
        ev, c_ev = self.S("ev"), self.C("ev")
        et, c_et = self.S("et"), self.C("et")
        MXC, SL = self.MX, self.SL
        n = len(items)
        chcc = self.chan("st_cc")

        tv = c_ev.v

        sb = list(getattr(self, "_sbank", []))

        @blk.tensor
        def _(t):
            t.wait_ge(ev, tv)
            self.bank_wait(t, "a")
            for v_ in sb:
                t.wait_ge(self.S("es"), v_)
            for i, (mx, _, _, _) in enumerate(items):
                t.transpose(self.ps_a[0:1, i * 128:(i + 1) * 128],
                            self.s_st[:, MXC[mx]:MXC[mx] + 1], self.s_idf[:])
            t.drain().then_inc(et, 1)
        c_et.inc()

        tev = c_et.v

        @blk.vector
        def _(v):
            v.wait_ge(et, tev)
            for i in range(n):
                v.reduce_max(self.s_cc[0:1, i:i + 1],
                             self.ps_a[0:1, i * 128:(i + 1) * 128], axis=AX.X)
                v.drain()
            v.drain().then_inc(ev, 1)
        c_ev.inc()
        self.bank_free("a", "ev")
        self.psa_guard = c_ev.v
        ccv = c_ev.v

        @blk.gpsimd
        def _(g):
            g.wait_ge(ev, ccv)
            chcc.dma(g, self.cc_in[round_i][0:1, 0:2], self.s_cc[0:1, 0:2])
            chcc.wait(g)
            g.collective_compute(
                "AllGather", ALU.bypass,
                replica_groups=[list(range(NCORES))],
                ins=[self.cc_in[round_i].opt()],
                outs=[self.cc_out[round_i].opt()],
            ).then_inc(cc, 1)
            c_cc.inc()
            g.wait_ge(cc, c_cc.v)
            chcc.dma(g, self.s_cc[0:1, 8:8 + 2 * NCORES],
                     self.cc_out[round_i].flatten().unsqueeze(0))
        gatherv = chcc.v

        @blk.vector
        def _(v):
            chcc.wait(v, gatherv)
            for i, (mx, s_slot, r_slot, qmax) in enumerate(items):
                gathered = self.s_cc[0:1, 8:8 + 2 * NCORES].rearrange(
                    "p (c k) -> p c k", k=2)[:, :, i:i + 1].transpose([0, 2, 1])
                v.reduce_max(self.s_sr[0:1, 2 * i:2 * i + 1], gathered, axis=AX.X)
                v.drain()
                v.tensor_scalar(self.s_sr[0:1, 2 * i:2 * i + 1],
                                self.s_sr[0:1, 2 * i:2 * i + 1],
                                1.0 / qmax, 1e-12, ALU.mult, ALU.max)
                v.drain()
                v.reciprocal(self.s_sr[0:1, 2 * i + 1:2 * i + 2],
                             self.s_sr[0:1, 2 * i:2 * i + 1])
                v.drain()
            v.drain().then_inc(ev, 1)
        c_ev.inc()
        bv = c_ev.v

        @blk.tensor
        def _(t):
            t.wait_ge(ev, bv)
            self.bank_wait(t, "d")
            t.matmul(self.ps_d[:, 0:2 * n], self.s_ones[:],
                     self.s_sr[0:1, 0:2 * n], start=True, stop=True)
            t.drain().then_inc(et, 1)
        c_et.inc()
        mv = c_et.v

        @blk.vector
        def _(v):
            v.wait_ge(et, mv)
            for i, (mx, s_slot, r_slot, qmax) in enumerate(items):
                v.tensor_copy(self.s_sc[:, SL[s_slot]:SL[s_slot] + 1],
                              self.ps_d[:, 2 * i:2 * i + 1])
                v.drain()
                v.tensor_copy(self.s_sc[:, SL[r_slot]:SL[r_slot] + 1],
                              self.ps_d[:, 2 * i + 1:2 * i + 2])
                v.drain()
                if s_slot == "s_g":
                    v.tensor_scalar(self.s_sc[:, SL["b_g"]:SL["b_g"] + 1],
                                    self.s_sc[:, SL["s_g"]:SL["s_g"] + 1],
                                    -32768.0, None, ALU.mult)
                    v.drain()
            v.drain().then_inc(ev, 1)
        c_ev.inc()
        self.bank_free("d", "ev")
        self.scale_ready = c_ev.v

    def emit_prep_lhsT(self, W, off, dequant, tag):
        blk = self.blk
        rows, cols = W.shape
        rp = min(128, rows)
        cw = min(128, cols)
        nMt = (rows + 127) // 128
        nCc = (cols + cw - 1) // cw
        half = nMt * nCc * rp
        setattr(self, "OFF_" + tag, (off, half, rows, cols))
        ev, c_ev = self.S("ev"), self.C("ev")
        es, c_es = self.S("es"), self.C("es")
        et, c_et = self.S("et"), self.C("et")
        a16, a32, ai = self.a16, self.a32, self.ai
        SPL_HI, SPL_LO = 49152, 52224
        w_t = W.rearrange("(n p) d -> n p d", p=128) if rows >= 128 else None
        chw = [self.chan("ld_w0"), self.chan("ld_w1")]
        load_done = {}
        split_done = {}
        tr_done = {}

        @blk.sync
        def _(q):
            self.qbar(q)
            for i in range(min(2, nMt)):
                src = w_t[i] if rows >= 128 else W
                load_done[i] = chw[i % 2].dma(
                    q, a32[0:rp, (i % 2) * cols:(i % 2) * cols + cols], src)

        for mt in range(nMt):
            wb = a32[0:rp, (mt % 2) * cols:(mt % 2) * cols + cols]
            hi = a16[0:rp, SPL_HI:SPL_HI + cols]
            lo = a16[0:rp, SPL_LO:SPL_LO + cols]

            qv = load_done[mt]
            etv = tr_done.get(mt - 1, 0)
            mtx = mt

            @blk.vector
            def _(v):
                chw[mtx % 2].wait(v, qv)
                v.wait_ge(et, etv)
                if dequant:
                    nblk = cols // 32
                    bm = self.s_st[:, 64:64 + nblk]
                    rcp = self.s_st[:, 160:160 + nblk]
                    w3 = wb.rearrange("p (b k) -> p b k", k=32)
                    v.reduce_max(bm[0:rp, :], w3, axis=AX.X,
                                 apply_absolute_value=True)
                    v.drain()
                    v.tensor_scalar(bm[0:rp, :], bm[0:rp, :], 1.0 / 7.0, 1e-12,
                                    ALU.mult, ALU.max)
                    v.drain()
                    v.reciprocal(rcp[0:rp, :], bm[0:rp, :])
                    v.drain()
                    tmp = a32[0:rp, 6144:6144 + cols]
                    v.tensor_tensor(tmp.rearrange("p (b k) -> p b k", k=32), w3,
                                    rcp[0:rp, :].unsqueeze(2).broadcast_to(
                                        [rp, nblk, 32]), ALU.mult)
                    v.drain()
                    ki = ai[0:rp, 0:cols]
                    v.tensor_scalar(ki, tmp, 1.0, 8.0, ALU.mult, ALU.add)
                    v.drain()
                    v.tensor_scalar(tmp, ki, -8.0, None, ALU.add)
                    v.drain()
                    v.tensor_tensor(tmp.rearrange("p (b k) -> p b k", k=32),
                                    tmp.rearrange("p (b k) -> p b k", k=32),
                                    bm[0:rp, :].unsqueeze(2).broadcast_to(
                                        [rp, nblk, 32]), ALU.mult)
                    v.drain()
                    src = tmp
                else:
                    src = wb
                v.tensor_copy(hi, src)
                v.drain()
                v.tensor_tensor(lo, src, hi, ALU.subtract)
                v.drain().then_inc(ev, 1)
            c_ev.inc()
            split_done[mt] = c_ev.v

            if mt + 2 < nMt:
                evv, nt = split_done[mt], mt + 2

                @blk.sync
                def _(q):
                    q.wait_ge(ev, evv)
                    load_done[nt] = chw[nt % 2].dma(
                        q, a32[0:rp, (nt % 2) * cols:(nt % 2) * cols + cols],
                        w_t[nt])

            per_bank = 1024 // rp
            for hl in range(2):
                ps = self.ps_b if hl == 0 else self.ps_c
                bank_key = "b" if hl == 0 else "c"
                srcbuf = hi if hl == 0 else lo
                for bs in range(0, nCc, per_bank):
                    nb = min(per_bank, nCc - bs)
                    sv = split_done[mt]
                    bsc, nbc, psx, srcx = bs, nb, ps, srcbuf

                    @blk.tensor
                    def _(t):
                        t.wait_ge(ev, sv)
                        self.bank_wait(t, bank_key)
                        for k in range(nbc):
                            t.transpose(psx[0:cw, k * rp:k * rp + rp],
                                        srcx[:, (bsc + k) * cw:(bsc + k) * cw + cw],
                                        self.s_idb[0:rp, 0:rp])
                        t.drain().then_inc(et, 1)
                    c_et.inc()
                    tev = c_et.v
                    mtc, hlc = mt, hl

                    @blk.scalar
                    def _(s):
                        s.wait_ge(et, tev)
                        dst = a16[0:cw,
                                  off + hlc * half + (mtc * nCc + bsc) * rp:
                                  off + hlc * half + (mtc * nCc + bsc) * rp +
                                  nbc * rp]
                        s.copy(dst, psx[0:cw, 0:nbc * rp])
                        s.drain().then_inc(es, 1)
                    c_es.inc()
                    self.bank_free(bank_key, "es")
            tr_done[mt] = c_et.v
        setattr(self, "PREP_DONE_" + tag, c_es.v)

    def lhsT(self, tag, mt, cc_, hl):
        off, half, rows, cols = getattr(self, "OFF_" + tag)
        rp = min(128, rows)
        cw = min(128, cols)
        nCc = (cols + cw - 1) // cw
        base = off + hl * half + (mt * nCc + cc_) * rp
        return self.a16[0:cw, base:base + rp]

    def emit_proj_q(self, s):
        blk = self.blk
        c = self.cfg
        T, D, H, dh, L = c["T"], c["D"], c["H"], c["dh"], c["L"]
        nDc = D // 128
        nTg = T // 512
        a16, a32 = self.a16, self.a32
        ev, c_ev = self.S("ev"), self.C("ev")
        et, c_et = self.S("et"), self.C("et")

        XN_HI = 32768
        XN_LO = 32768 + nDc * T
        KV_HI = 65536
        KV_LO = 65536 + T
        STG_Q = (65536, 66560)
        STG_KV = (0, 1024)
        chxn = self.chan("ld_xn")

        @blk.sync
        def _(q):
            self.qbar(q)
            for src, dst in [(self.xnT_hi, XN_HI), (self.xnT_lo, XN_LO)]:
                chxn.dma(q, a16[:, dst:dst + nDc * T].rearrange(
                    "p (c t) -> p c t", t=T),
                    src.rearrange("(c p) n -> p c n", p=128)[
                        :, :, s * T:(s + 1) * T])
        xn_ready = chxn.v

        xnt = lambda hl, dc, t0, w: a16[:, (XN_HI if hl == 0 else XN_LO) +
                                        dc * T + t0:
                                        (XN_HI if hl == 0 else XN_LO) +
                                        dc * T + t0 + w]

        for h in range(H):
            for dc2 in range(dh // 128):
                for tg in range(nTg):
                    hh, dd, tt = h, dc2, tg

                    psx, bk = ((self.ps_d, "d") if self._ess_i % 2 == 0
                               else (self.ps_e, "e"))

                    @blk.tensor
                    def _(t):
                        chxn.wait(t, xn_ready)
                        self.bank_wait(t, bk)
                        mi = hh * (dh // 128) + dd
                        first = True
                        for dc in range(nDc):
                            for (sp, mp) in ((0, 0), (0, 1), (1, 0)):
                                t.matmul(psx[:, 0:512],
                                         self.lhsT("wq", mi, dc, sp),
                                         xnt(mp, dc, tt * 512, 512),
                                         start=first,
                                         stop=(dc == nDc - 1 and
                                               (sp, mp) == (1, 0)))
                                first = False
                        t.drain().then_inc(et, 1)
                    c_et.inc()
                    self._evac_split_store(
                        bank_key=bk, ps=psx, stg=STG_Q, n=512,
                        dst_hi=self.q_hi[s, h * dh + dc2 * 128:
                                         h * dh + dc2 * 128 + 128,
                                         tg * 512:tg * 512 + 512],
                        dst_lo=self.q_lo[s, h * dh + dc2 * 128:
                                         h * dh + dc2 * 128 + 128,
                                         tg * 512:tg * 512 + 512])


    def emit_proj_kv(self, s):
        blk = self.blk
        c = self.cfg
        T, D, H, dh, L = c["T"], c["D"], c["H"], c["dh"], c["L"]
        nDc = D // 128
        nTg = T // 512
        a16, a32 = self.a16, self.a32
        ev, c_ev = self.S("ev"), self.C("ev")
        et, c_et = self.S("et"), self.C("et")

        XN_HI = 32768
        XN_LO = 32768 + nDc * T
        KV_HI = 65536
        KV_LO = 65536 + T
        STG_Q = (65536, 66560)
        STG_KV = (0, 1024)
        chxn = self.chan("ld_xn")

        @blk.sync
        def _(q):
            self.qbar(q)
            for src, dst in [(self.xnT_hi, XN_HI), (self.xnT_lo, XN_LO)]:
                chxn.dma(q, a16[:, dst:dst + nDc * T].rearrange(
                    "p (c t) -> p c t", t=T),
                    src.rearrange("(c p) n -> p c n", p=128)[
                        :, :, s * T:(s + 1) * T])
        xn_ready = chxn.v

        xnt = lambda hl, dc, t0, w: a16[:, (XN_HI if hl == 0 else XN_LO) +
                                        dc * T + t0:
                                        (XN_HI if hl == 0 else XN_LO) +
                                        dc * T + t0 + w]

        for tg in range(nTg):
            tgc = tg

            @blk.tensor
            def _(t):
                chxn.wait(t, xn_ready)
                self.bank_wait(t, "d")
                first = True
                for dc in range(nDc):
                    for (sp, mp) in ((0, 0), (0, 1), (1, 0)):
                        t.matmul(self.ps_d[0:64, 0:512],
                                 self.lhsT("kvd", 0, dc, sp),
                                 xnt(mp, dc, tgc * 512, 512),
                                 start=first, stop=(dc == nDc - 1 and
                                                    (sp, mp) == (1, 0)))
                        first = False
                t.drain().then_inc(et, 1)
            c_et.inc()
            mmv = c_et.v

            qs0 = self.chan("st_ess0").v
            qs1 = self.chan("st_ess1").v

            @blk.vector
            def _(v):
                v.wait_ge(et, mmv)
                self.chan("st_ess0").wait(v, qs0)
                self.chan("st_ess1").wait(v, qs1)
                kvf = a32[0:64, 0:512]
                v.tensor_copy(kvf, self.ps_d[0:64, 0:512])
                v.drain()
                v.tensor_copy(a16[0:64, KV_HI + tgc * 512:KV_HI + tgc * 512 + 512],
                              kvf)
                v.drain()
                v.tensor_tensor(a16[0:64, KV_LO + tgc * 512:KV_LO + tgc * 512 + 512],
                                kvf,
                                a16[0:64, KV_HI + tgc * 512:KV_HI + tgc * 512 + 512],
                                ALU.subtract)
                v.drain().then_inc(ev, 1)
            c_ev.inc()
            self.bank_free("d", "ev")
        kv_ready = c_ev.v

        kvt = lambda hl, t0, w: a16[0:64, (KV_HI if hl == 0 else KV_LO) +
                                    t0:(KV_HI if hl == 0 else KV_LO) + t0 + w]

        for h in range(H):
            for dc2 in range(dh // 128):
                for tg in range(nTg):
                    hh, dd, tt = h, dc2, tg

                    psx, bk = ((self.ps_d, "d") if self._ess_i % 2 == 0
                               else (self.ps_e, "e"))

                    @blk.tensor
                    def _(t):
                        t.wait_ge(ev, kv_ready)
                        self.bank_wait(t, bk)
                        mi = hh * (dh // 128) + dd
                        first = True
                        for (sp, mp) in ((0, 0), (0, 1), (1, 0)):
                            t.matmul(psx[:, 0:512],
                                     self.lhsT("kup", mi, 0, sp),
                                     kvt(mp, tt * 512, 512),
                                     start=first, stop=((sp, mp) == (1, 0)))
                            first = False
                        t.drain().then_inc(et, 1)
                    c_et.inc()
                    self._evac_split_store(
                        bank_key=bk, ps=psx, stg=STG_KV, n=512,
                        dst_hi=self.k_hi[s, h * dh + dc2 * 128:
                                         h * dh + dc2 * 128 + 128,
                                         tg * 512:tg * 512 + 512],
                        dst_lo=self.k_lo[s, h * dh + dc2 * 128:
                                         h * dh + dc2 * 128 + 128,
                                         tg * 512:tg * 512 + 512])

        nTc = T // 128
        voff, vhalf, _, _ = getattr(self, "OFF_vup")
        for tc in range(nTc):
            for hg in range(H * dh // 512):
                tcc, hgc = tc, hg

                psx, bk = ((self.ps_d, "d") if self._ess_i % 2 == 0
                           else (self.ps_e, "e"))

                @blk.tensor
                def _(t):
                    t.wait_ge(ev, kv_ready)
                    self.bank_wait(t, bk)
                    first = True
                    for (sp, mp) in ((0, 0), (0, 1), (1, 0)):
                        rhs = a16[0:64, voff + mp * vhalf + hgc * 512:
                                  voff + mp * vhalf + hgc * 512 + 512]
                        t.matmul(psx[:, 0:512],
                                 kvt(sp, tcc * 128, 128), rhs,
                                 start=first, stop=((sp, mp) == (1, 0)))
                        first = False
                    t.drain().then_inc(et, 1)
                c_et.inc()
                self._evac_split_store(
                    bank_key=bk, ps=psx, stg=STG_KV, n=512,
                    dst_hi=self.v_hi[s, tc * 128:tc * 128 + 128,
                                     hg * 512:hg * 512 + 512],
                    dst_lo=self.v_lo[s, tc * 128:tc * 128 + 128,
                                     hg * 512:hg * 512 + 512])


    _ess_i = 0

    def _evac_split_store(self, bank_key, stg, n, dst_hi, dst_lo, ps=None):
        blk = self.blk
        a16 = self.a16
        ev, c_ev = self.S("ev"), self.C("ev")
        slot = self._ess_i % 2
        self._ess_i += 1
        ch = self.chan(f"st_ess{slot}")
        ps = self.ps_d if ps is None else ps
        base = stg[slot]
        hi = a16[:, base:base + n]
        lo = a16[:, base + 512:base + 512 + n]
        tv = self.C("et").v
        stv = ch.v

        @blk.vector
        def _(v):
            v.wait_ge(self.S("et"), tv)
            ch.wait(v, stv)
            v.tensor_copy(hi, ps[:, 0:n])
            v.drain()
            v.tensor_tensor(lo, ps[:, 0:n], hi, ALU.subtract)
            v.drain().then_inc(ev, 1)
        c_ev.inc()
        self.bank_free(bank_key, "ev")
        sv = c_ev.v

        @blk.gpsimd
        def _(g):
            g.wait_ge(ev, sv)
            ch.dma(g, dst_hi, hi)
            ch.dma(g, dst_lo, lo)

    def emit_xq(self):
        blk = self.blk
        D, TOK = self.cfg["D"], self.TOK
        a16, a32, ai = self.a16, self.a32, self.ai
        ev, c_ev = self.S("ev"), self.C("ev")
        SL = self.SL
        FW = min(1024, TOK)
        nTile = (D // 128) * (TOK // FW)
        nF = TOK // FW
        _t4 = lambda ap: ap.rearrange("(c p) (n f) -> c n p f", p=128, f=FW)
        hi_t4, lo_t4 = _t4(self.xnT_hi), _t4(self.xnT_lo)
        qhi_t4, qlo_t4 = _t4(self.xqT_hi), _t4(self.xqT_lo)
        hi_t = lambda i: hi_t4[i // nF, i % nF]
        lo_t = lambda i: lo_t4[i // nF, i % nF]
        qhi_t = lambda i: qhi_t4[i // nF, i % nF]
        qlo_t = lambda i: qlo_t4[i // nF, i % nF]
        IN_HI, IN_LO, ST_HI, ST_LO = 55296, 59392, 63488, 67584
        chi = [self.chan("ld_xq0"), self.chan("ld_xq1")]
        cho = [self.chan("st_xq0"), self.chan("st_xq1")]
        load_done = {}
        proc_done = {}
        scl = self.scale_ready

        @blk.sync
        def _(q):
            self.qbar(q)
            q.wait_ge(ev, scl)
            for i in range(min(2, nTile)):
                chi[i % 2].dma(q, a16[:, IN_HI + (i % 2) * FW:
                                      IN_HI + (i % 2) * FW + FW], hi_t(i))
                load_done[i] = chi[i % 2].dma(
                    q, a16[:, IN_LO + (i % 2) * FW:IN_LO + (i % 2) * FW + FW],
                    lo_t(i))

        for i in range(nTile):
            sl = i % 2
            qv = load_done[i]
            stv = cho[sl].v
            slx = sl

            @blk.vector
            def _(v):
                chi[slx].wait(v, qv)
                cho[slx].wait(v, stv)
                xh = a16[:, IN_HI + slx * FW:IN_HI + slx * FW + FW]
                xl = a16[:, IN_LO + slx * FW:IN_LO + slx * FW + FW]
                xf = a32[:, 0:FW]
                qf = a32[:, 1024:1024 + FW]
                ki = ai[:, 0:FW]
                sh = a16[:, ST_HI + slx * FW:ST_HI + slx * FW + FW]
                slo = a16[:, ST_LO + slx * FW:ST_LO + slx * FW + FW]
                v.tensor_tensor(xf, xh, xl, ALU.add)
                v.drain()
                v.tensor_scalar(ki, xf, self.s_sc[:, SL["r_x"]:SL["r_x"] + 1],
                                32768.0, ALU.mult, ALU.add)
                v.drain()
                v.tensor_scalar(sh, ki, -32768.0,
                                self.s_sc[:, SL["s_x"]:SL["s_x"] + 1],
                                ALU.add, ALU.mult)
                v.drain()
                v.tensor_scalar(qf, ki, -32768.0,
                                self.s_sc[:, SL["s_x"]:SL["s_x"] + 1],
                                ALU.add, ALU.mult)
                v.drain()
                v.tensor_tensor(slo, qf, sh, ALU.subtract)
                v.drain().then_inc(ev, 1)
            c_ev.inc()
            proc_done[i] = c_ev.v

            if i + 2 < nTile:
                evv, nt = proc_done[i], i + 2

                @blk.sync
                def _(q):
                    q.wait_ge(ev, evv)
                    chi[nt % 2].dma(q, a16[:, IN_HI + (nt % 2) * FW:
                                           IN_HI + (nt % 2) * FW + FW],
                                    hi_t(nt))
                    load_done[nt] = chi[nt % 2].dma(
                        q, a16[:, IN_LO + (nt % 2) * FW:IN_LO + (nt % 2) * FW + FW],
                        lo_t(nt))

            pv = proc_done[i]
            ii = i

            @blk.gpsimd
            def _(g):
                g.wait_ge(ev, pv)
                cho[slx].dma(g, qhi_t(ii),
                             a16[:, ST_HI + slx * FW:ST_HI + slx * FW + FW])
                cho[slx].dma(g, qlo_t(ii),
                             a16[:, ST_LO + slx * FW:ST_LO + slx * FW + FW])

    def emit_ffn_mm(self, dstT, mx_name, tag):
        blk = self.blk
        c = self.cfg
        D, I, TOK = c["D"], c["I"], self.TOK
        nDc, nIc, nTg = D // 128, I // 128, TOK // 512
        a16, a32 = self.a16, self.a32
        ev, c_ev = self.S("ev"), self.C("ev")
        es, c_es = self.S("es"), self.C("es")
        et, c_et = self.S("et"), self.C("et")
        MXC = self.MX
        XQ = 55296
        qhi = self.xqT_hi.rearrange("(c p) n -> p c n", p=128)
        qlo = self.xqT_lo.rearrange("(c p) n -> p c n", p=128)
        chq = [self.chan("ld_fq0"), self.chan("ld_fq1")]
        chs = [self.chan("st_fs0"), self.chan("st_fs1")]
        load_done = {}
        tg_mm = {}
        prep_es = getattr(self, "PREP_DONE_" + tag)

        @blk.sync
        def _(q):
            self.qbar(q)
            q.wait_ge(es, prep_es)
            for i in range(min(2, nTg)):
                sl = i % 2
                chq[sl].dma(q, a16[:, XQ + sl * 8192:XQ + sl * 8192 + 4096]
                            .rearrange("p (c t) -> p c t", t=512),
                            qhi[:, :, i * 512:i * 512 + 512])
                load_done[i] = chq[sl].dma(
                    q, a16[:, XQ + sl * 8192 + 4096:XQ + sl * 8192 + 8192]
                    .rearrange("p (c t) -> p c t", t=512),
                    qlo[:, :, i * 512:i * 512 + 512])

        for tg in range(nTg):
            sl = tg % 2
            for ic in range(nIc):
                bank_key = "d" if ic % 2 == 0 else "e"
                ps = self.ps_d if ic % 2 == 0 else self.ps_e
                qv = load_done[tg]
                slx, icx, tgx, psx = sl, ic, tg, ps

                @blk.tensor
                def _(t):
                    chq[slx].wait(t, qv)
                    self.bank_wait(t, bank_key)
                    first = True
                    for dc in range(nDc):
                        for (sp, mp) in ((0, 0), (0, 1), (1, 0)):
                            rhs = a16[:, XQ + slx * 8192 + mp * 4096 + dc * 512:
                                      XQ + slx * 8192 + mp * 4096 + dc * 512 + 512]
                            t.matmul(psx[:, 0:512],
                                     self.lhsT(tag, icx, dc, sp), rhs,
                                     start=first,
                                     stop=(dc == nDc - 1 and (sp, mp) == (1, 0)))
                            first = False
                    t.drain().then_inc(et, 1)
                c_et.inc()
                mmv = c_et.v

                stg = a32[:, (ic % 2) * 512:(ic % 2) * 512 + 512]
                stv = chs[ic % 2].v
                stgx = stg

                @blk.vector
                def _(v):
                    v.wait_ge(et, mmv)
                    chs[icx % 2].wait(v, stv)
                    v.tensor_copy(stgx, psx[:, 0:512])
                    v.drain()
                    v.reduce_max(self.s_st[:, MXC["tmp"]:MXC["tmp"] + 1], stgx,
                                 axis=AX.X, apply_absolute_value=True)
                    v.drain()
                    v.tensor_tensor(self.s_st[:, MXC[mx_name]:MXC[mx_name] + 1],
                                    self.s_st[:, MXC[mx_name]:MXC[mx_name] + 1],
                                    self.s_st[:, MXC["tmp"]:MXC["tmp"] + 1],
                                    ALU.max)
                    v.drain().then_inc(ev, 1)
                c_ev.inc()
                self.bank_free(bank_key, "ev")
                dv = c_ev.v

                dbg_t = self.dbg_raw.get(mx_name + "_raw")

                @blk.gpsimd
                def _(g):
                    g.wait_ge(ev, dv)
                    chs[icx % 2].dma(g, dstT[icx * 128:icx * 128 + 128,
                                             tgx * 512:tgx * 512 + 512], stgx)
                    if dbg_t is not None:
                        chs[icx % 2].dma(g, dbg_t[icx * 128:icx * 128 + 128,
                                                  tgx * 512:tgx * 512 + 512],
                                         stgx)
            tg_mm[tg] = c_et.v

            if tg + 2 < nTg:
                etv, nt = tg_mm[tg], tg + 2

                @blk.sync
                def _(q):
                    q.wait_ge(et, etv)
                    sl2 = nt % 2
                    chq[sl2].dma(q, a16[:, XQ + sl2 * 8192:XQ + sl2 * 8192 + 4096]
                                 .rearrange("p (c t) -> p c t", t=512),
                                 qhi[:, :, nt * 512:nt * 512 + 512])
                    load_done[nt] = chq[sl2].dma(
                        q, a16[:, XQ + sl2 * 8192 + 4096:XQ + sl2 * 8192 + 8192]
                        .rearrange("p (c t) -> p c t", t=512),
                        qlo[:, :, nt * 512:nt * 512 + 512])

    def emit_flash(self, s, h):
        blk = self.blk
        c = self.cfg
        T, dh, H = c["T"], c["dh"], c["H"]
        nKc = T // 128
        nQg = T // 512
        ndc = dh // 128
        a16, a32 = self.a16, self.a32
        ev, c_ev = self.S("ev"), self.C("ev")
        es, c_es = self.S("es"), self.C("es")
        et, c_et = self.S("et"), self.C("et")
        ET_HI, ET_LO, OTS = 24576, 32768, 40960
        head_idx = s * H + h
        par = head_idx % 2
        BASE = [0, 45056][par]
        K_, V_, Q_ = BASE, BASE + 8192, BASE + 16384
        chf = [self.chan("ld_fl0"), self.chan("ld_fl1")]
        cho = [self.chan("st_fl0"), self.chan("st_fl1")]
        guard = getattr(self, "psa_guard", 0)
        if not hasattr(self, "_fl"):
            ev0 = self.C("et").v
            self._fl = {0: dict(loads=0, lastpv=ev0, issued=False),
                        1: dict(loads=0, lastpv=ev0, issued=False)}
            self._et_free = 0
            self._sbank = [0, 0, 0, 0]

        def issue_loads(si, hi_, parity):
            base = [0, 45056][parity]
            ch = chf[parity]
            lastpv = self._fl[parity]["lastpv"]

            @blk.sync
            def _(q):
                if si == 0 and hi_ == 0:
                    self.qbar(q)
                else:
                    q.wait_ge(et, lastpv)
                for hl, src in ((0, self.k_hi), (1, self.k_lo)):
                    ch.dma(q, a16[:, base + hl * 4096:base + hl * 4096 + ndc * T]
                           .rearrange("p (c t) -> p c t", t=T),
                           src.rearrange("b (c p) t -> b p c t", p=128)[
                               si, :, ndc * hi_:ndc * hi_ + ndc, :])
                for hl, src in ((0, self.v_hi), (1, self.v_lo)):
                    ch.dma(q, a16[:, base + 8192 + hl * 4096:
                                  base + 8192 + hl * 4096 + nKc * dh]
                           .rearrange("p (c t) -> p c t", t=dh),
                           src.rearrange("b (c p) m -> b p c m", p=128)[
                               si, :, :, hi_ * dh:(hi_ + 1) * dh])
                for hl, src in ((0, self.q_hi), (1, self.q_lo)):
                    ch.dma(q, a16[:, base + 16384 + hl * 4096:
                                  base + 16384 + hl * 4096 + ndc * T]
                           .rearrange("p (c t) -> p c t", t=T),
                           src.rearrange("b (c p) t -> b p c t", p=128)[
                               si, :, ndc * hi_:ndc * hi_ + ndc, :])
            self._fl[parity]["loads"] = ch.v
            self._fl[parity]["issued"] = True

        if not self._fl[par]["issued"]:
            issue_loads(s, h, par)
        loads = self._fl[par]["loads"]
        self._fl[par]["issued"] = False
        if head_idx + 1 < c["B"] * H:
            nh = head_idx + 1
            issue_loads(nh // H, nh % H, nh % 2)

        kT = lambda hl, dc, t0, w: a16[:, K_ + hl * 4096 + dc * T + t0:
                                       K_ + hl * 4096 + dc * T + t0 + w]
        vT = lambda hl, kc, d0, w: a16[:, V_ + hl * 4096 + kc * dh + d0:
                                       V_ + hl * 4096 + kc * dh + d0 + w]
        qT = lambda hl, dc, t0, w: a16[:, Q_ + hl * 4096 + dc * T + t0:
                                       Q_ + hl * 4096 + dc * T + t0 + w]
        eT = lambda hl, kc: a16[:, (ET_HI if hl == 0 else ET_LO) + kc * 512:
                                (ET_HI if hl == 0 else ET_LO) + kc * 512 + 512]

        first_of_head = True
        for qg in range(nQg):
            split_done = {}
            sum_emitted = 0
            et_free = self._et_free

            def emit_sum(kc):
                svv = split_done[kc]
                kcx = kc

                @blk.tensor
                def _(t):
                    t.wait_ge(ev, svv)
                    if kcx == 0:
                        self.bank_wait(t, "d")
                    t.matmul(self.ps_d[:, 0:512], self.s_on16[:], eT(0, kcx),
                             start=(kcx == 0), stop=False)
                    mm = t.matmul(self.ps_d[:, 0:512], self.s_on16[:],
                                  eT(1, kcx), start=False,
                                  stop=(kcx == nKc - 1))
                    if kcx == nKc - 1:
                        mm.then_inc(et, 1)
                if kcx == nKc - 1:
                    c_et.inc()

            for kc in range(nKc):
                bank = kc % 4
                qgx, kcx, bk = qg, kc, bank
                freev = self._sbank[bank]
                foh = first_of_head
                first_of_head = False

                @blk.tensor
                def _(t):
                    if foh:
                        chf[par].wait(t, loads)
                        t.wait_ge(ev, guard)
                        self.bank_wait(t, "a")
                    t.wait_ge(es, freev)
                    first = True
                    for dc in range(ndc):
                        for (sp, mp) in ((0, 0), (0, 1), (1, 0)):
                            mm = t.matmul(
                                self.ps_a[:, bk * 512:bk * 512 + 512],
                                kT(sp, dc, kcx * 128, 128),
                                qT(mp, dc, qgx * 512, 512),
                                start=first,
                                stop=(dc == ndc - 1 and (sp, mp) == (1, 0)))
                            first = False
                    mm.then_inc(et, 1)
                c_et.inc()
                qkv = c_et.v

                @blk.scalar
                def _(s_):
                    s_.wait_ge(et, qkv)
                    s_.activation(a32[:, bk * 512:bk * 512 + 512],
                                  self.ps_a[:, bk * 512:bk * 512 + 512],
                                  AF.Exp, bias=self.s_sc[:, 13:14],
                                  scale=1.0 / 16.0)
                    s_.drain().then_inc(es, 1)
                c_es.inc()
                self._sbank[bank] = c_es.v
                expv = c_es.v

                @blk.vector
                def _(v):
                    v.wait_ge(es, expv)
                    v.wait_ge(et, et_free)
                    v.tensor_copy(eT(0, kcx), a32[:, bk * 512:bk * 512 + 512])
                    v.drain()
                    v.tensor_tensor(eT(1, kcx),
                                    a32[:, bk * 512:bk * 512 + 512],
                                    eT(0, kcx), ALU.subtract)
                    v.drain().then_inc(ev, 1)
                c_ev.inc()
                split_done[kc] = c_ev.v

                if kc >= 1:
                    emit_sum(kc - 1)
            emit_sum(nKc - 1)
            sum_et = c_et.v

            @blk.vector
            def _(v):
                v.wait_ge(et, sum_et)
                v.reciprocal(a32[:, 2048:2560], self.ps_d[:, 0:512])
                v.drain().then_inc(ev, 1)
            c_ev.inc()
            self.bank_free("d", "ev")
            rcv = c_ev.v

            for dc in range(ndc):
                ps = self.ps_e if dc == 0 else self.ps_d
                bk2 = "e" if dc == 0 else "d"
                psx, dcx = ps, dc
                svv = split_done[nKc - 1]

                @blk.tensor
                def _(t):
                    t.wait_ge(ev, svv)
                    self.bank_wait(t, bk2)
                    first = True
                    for kc in range(nKc):
                        for (sp, mp) in ((0, 0), (0, 1), (1, 0)):
                            mm = t.matmul(psx[:, 0:512],
                                          vT(sp, kc, dcx * 128, 128),
                                          eT(mp, kc),
                                          start=first,
                                          stop=(kc == nKc - 1 and
                                                (sp, mp) == (1, 0)))
                            first = False
                    mm.then_inc(et, 1)
                c_et.inc()
                pvv = c_et.v
                if dc == ndc - 1:
                    self._fl[par]["lastpv"] = c_et.v
                    self._et_free = c_et.v
                slot = qg % 2
                stv = cho[slot].v
                OTB = OTS + slot * 2048

                @blk.vector
                def _(v):
                    v.wait_ge(et, pvv)
                    v.wait_ge(ev, rcv)
                    cho[slot].wait(v, stv)
                    rm = a32[:, 2048:2560]
                    t1 = a32[:, 2560:3072]
                    hi = a16[:, OTB + dcx * 512:OTB + dcx * 512 + 512]
                    lo = a16[:, OTB + 1024 + dcx * 512:
                             OTB + 1024 + dcx * 512 + 512]
                    v.tensor_tensor(hi, psx[:, 0:512], rm, ALU.mult)
                    v.drain()
                    v.tensor_tensor(t1, psx[:, 0:512], rm, ALU.mult)
                    v.drain()
                    v.tensor_tensor(lo, t1, hi, ALU.subtract)
                    v.drain().then_inc(ev, 1)
                c_ev.inc()
                self.bank_free(bk2, "ev")

            ov = c_ev.v
            sx, hx, qgx2, slot2 = s, h, qg, qg % 2
            OTB2 = OTS + slot2 * 2048

            @blk.gpsimd
            def _(g):
                g.wait_ge(ev, ov)
                for hl, dst in ((0, self.oT_hi), (1, self.oT_lo)):
                    cho[slot2].dma(
                        g, dst.rearrange("b (c p) t -> b p c t", p=128)[
                            sx, :, ndc * hx:ndc * hx + ndc,
                            qgx2 * 512:qgx2 * 512 + 512],
                        a16[:, OTB2 + hl * 1024:OTB2 + hl * 1024 + 1024]
                        .rearrange("p (c t) -> p c t", t=512))

    def _ffn_c_state(self):
        if not hasattr(self, "_p4"):
            TOK = self.TOK
            FW = min(1024, TOK)
            self._p4 = dict(i=0, a=0, FW=FW,
                            nTile=(self.cfg["I"] // 128) * (TOK // FW),
                            load_done={}, proc={}, store={}, kv={}, sig={},
                            started=False)
        return self._p4

    def emit_ffn_c_range(self, n):
        st = self._ffn_c_state()
        blk = self.blk
        a32, ai = self.a32, self.ai
        ev, c_ev = self.S("ev"), self.C("ev")
        es, c_es = self.S("es"), self.C("es")
        SL, MXC = self.SL, self.MX
        FW = st["FW"]
        nTile = st["nTile"]
        g_t4 = self.gateT.rearrange("(c p) (n f) -> c n p f", p=128, f=FW)
        nF = self.TOK // FW
        g_t = lambda i: g_t4[i // nF, i % nF]
        chi = [self.chan(f"ld_c{i}") for i in range(3)]
        cho = [self.chan("st_c0"), self.chan("st_c1")]
        load_done, proc_done, store_done = st["load_done"], st["proc"], st["store"]
        kv_done, sig_done = st["kv"], st["sig"]

        if not st["started"]:
            st["started"] = True
            scl = self.scale_ready
            wo_done = getattr(self, "PREP_DONE_wo", 0)

            @blk.gpsimd
            def _(q):
                q.wait_ge(ev, scl)
                q.wait_ge(self.S("es"), wo_done)
                for i in range(min(3, nTile)):
                    load_done[i] = chi[i % 3].dma(
                        q, a32[:, 1024 + (i % 3) * 1024:1024 + (i % 3) * 1024 + FW],
                        g_t(i))

        def emit_A(i):
            gbuf = a32[:, 1024 + (i % 3) * 1024:1024 + (i % 3) * 1024 + FW]
            ki = ai[:, (i % 3) * 1024:(i % 3) * 1024 + FW]
            qv = load_done[i]
            gbx, kix, ix = gbuf, ki, i

            @blk.vector
            def _(v):
                chi[ix % 3].wait(v, qv)
                v.tensor_scalar(kix, gbx, self.s_sc[:, SL["r_g"]:SL["r_g"] + 1],
                                32768.0, ALU.mult, ALU.add)
                v.drain().then_inc(ev, 1)
            c_ev.inc()
            kv_done[i] = c_ev.v
            sig = a32[:, 5120 + (i % 2) * 1024:5120 + (i % 2) * 1024 + FW]
            sfree = proc_done.get(i - 2, 0)
            stfree = store_done.get(i - 2, 0)
            kvv, sgx, slx = kv_done[i], sig, i % 2

            @blk.scalar
            def _(s_):
                s_.wait_ge(ev, kvv)
                s_.wait_ge(ev, sfree)
                cho[slx].wait(s_, stfree)
                s_.activation(sgx, kix, AF.Sigmoid,
                              bias=self.s_sc[:, SL["b_g"]:SL["b_g"] + 1],
                              scale=self.s_sc[:, SL["s_g"]:SL["s_g"] + 1])
                s_.drain().then_inc(es, 1)
            c_es.inc()
            sig_done[i] = c_es.v

            if i + 3 < nTile:
                evv, nt = kv_done[i], i + 3

                @blk.gpsimd
                def _(q):
                    q.wait_ge(ev, evv)
                    load_done[nt] = chi[nt % 3].dma(
                        q, a32[:, 1024 + (nt % 3) * 1024:
                               1024 + (nt % 3) * 1024 + FW], g_t(nt))

        def emit_B(i):
            sl = i % 2
            sig = a32[:, 5120 + sl * 1024:5120 + sl * 1024 + FW]
            sigq = a32[:, 7168:7168 + FW]
            gq = a32[:, 8192:8192 + FW]
            ki = ai[:, (i % 3) * 1024:(i % 3) * 1024 + FW]
            sv = sig_done[i]
            stv = store_done.get(i - 2, 0)
            gsx = sig
            sgx, sqx, gqx, kix, slx, ix = sig, sigq, gq, ki, sl, i

            @blk.vector
            def _(v):
                v.wait_ge(es, sv)
                cho[slx].wait(v, stv)
                v.tensor_scalar(gqx, kix, -32768.0,
                                self.s_sc[:, SL["s_g"]:SL["s_g"] + 1],
                                ALU.add, ALU.mult)
                v.drain()
                v.tensor_scalar(kix, sgx, 65536.0, 0.0, ALU.mult, ALU.add)
                v.drain()
                v.tensor_scalar(sqx, kix, 65535.0, 2.0 ** -16, ALU.min, ALU.mult)
                v.drain()
                v.tensor_tensor(gsx, gqx, sqx, ALU.mult)
                v.drain()
                v.reduce_max(self.s_st[:, MXC["tmp"]:MXC["tmp"] + 1], gsx,
                             axis=AX.X, apply_absolute_value=True)
                v.drain()
                v.tensor_tensor(self.s_st[:, MXC["gs"]:MXC["gs"] + 1],
                                self.s_st[:, MXC["gs"]:MXC["gs"] + 1],
                                self.s_st[:, MXC["tmp"]:MXC["tmp"] + 1], ALU.max)
                v.drain().then_inc(ev, 1)
            c_ev.inc()
            proc_done[i] = c_ev.v
            pv = proc_done[i]

            @blk.gpsimd
            def _(g):
                g.wait_ge(ev, pv)
                cho[slx].dma(g, g_t(ix), gsx)
            store_done[i] = cho[slx].v

        i0 = st["i"]
        i1 = min(i0 + n, nTile)
        st["i"] = i1
        for b in range(i0, i1):
            while st["a"] < min(b + 2, nTile):
                emit_A(st["a"])
                st["a"] += 1
            emit_B(b)

    def _ffn_d_state(self):
        if not hasattr(self, "_p5"):
            TOK = self.TOK
            FW = min(512, TOK)
            self._p5 = dict(i=0, FW=FW,
                            nTile=(self.cfg["I"] // 128) * (TOK // FW),
                            load_done={}, proc={}, store={}, started=False)
        return self._p5

    def emit_ffn_d_range(self, n):
        st = self._ffn_d_state()
        blk = self.blk
        a32, ai = self.a32, self.ai
        ev, c_ev = self.S("ev"), self.C("ev")
        SL, MXC = self.SL, self.MX
        FW = st["FW"]
        nTile = st["nTile"]
        nF = self.TOK // FW
        gs_t4 = self.gateT.rearrange("(c p) (n f) -> c n p f", p=128, f=FW)
        up_t4 = self.upT.rearrange("(c p) (n f) -> c n p f", p=128, f=FW)
        gs_t = lambda i: gs_t4[i // nF, i % nF]
        up_t = lambda i: up_t4[i // nF, i % nF]
        chi = [self.chan(f"ld_d{i}") for i in range(4)]
        cho = [self.chan("st_d0"), self.chan("st_d1")]
        load_done, proc_done, store_done = st["load_done"], st["proc"], st["store"]

        if not st["started"]:
            st["started"] = True
            scl = self.scale_ready
            pc0 = self.chan("st_c0").v
            pc1 = self.chan("st_c1").v

            @blk.gpsimd
            def _(q):
                q.wait_ge(ev, scl)
                self.chan("st_c0").wait(q, pc0)
                self.chan("st_c1").wait(q, pc1)
                for i in range(min(4, nTile)):
                    s4 = i % 4
                    chi[s4].dma(q, a32[:, 1024 + s4 * 512:1024 + s4 * 512 + FW],
                                gs_t(i))
                    load_done[i] = chi[s4].dma(
                        q, a32[:, 3072 + s4 * 512:3072 + s4 * 512 + FW],
                        up_t(i))

        i0 = st["i"]
        i1 = min(i0 + n, nTile)
        st["i"] = i1
        for i in range(i0, i1):
            sl = i % 2
            s4 = i % 4
            gsb = a32[:, 1024 + s4 * 512:1024 + s4 * 512 + FW]
            upb = a32[:, 3072 + s4 * 512:3072 + s4 * 512 + FW]
            g2 = a32[:, 5120:5120 + FW]
            uq = a32[:, 5632:5632 + FW]
            ob = a32[:, 6144 + sl * 512:6144 + sl * 512 + FW]
            ki = ai[:, 0:FW]
            ki2 = ai[:, 1024:1024 + FW]
            qv = load_done[i]
            stv = store_done.get(i - 2, 0)
            gx, ux, g2x, uqx, obx, slx, s4x = gsb, upb, g2, uq, ob, sl, i % 4

            @blk.vector
            def _(v):
                chi[s4x].wait(v, qv)
                cho[slx].wait(v, stv)
                v.tensor_scalar(ki, gx,
                                self.s_sc[:, SL["r_gs"]:SL["r_gs"] + 1],
                                32768.0, ALU.mult, ALU.add)
                v.drain()
                v.tensor_scalar(ki2, ux,
                                self.s_sc[:, SL["r_up"]:SL["r_up"] + 1],
                                32768.0, ALU.mult, ALU.add)
                v.drain()
                v.tensor_scalar(g2x, ki, -32768.0,
                                self.s_sc[:, SL["s_gs"]:SL["s_gs"] + 1],
                                ALU.add, ALU.mult)
                v.drain()
                v.tensor_scalar(uqx, ki2, -32768.0,
                                self.s_sc[:, SL["s_up"]:SL["s_up"] + 1],
                                ALU.add, ALU.mult)
                v.drain()
                v.tensor_tensor(obx, g2x, uqx, ALU.mult)
                v.drain()
                v.reduce_max(self.s_st[:, MXC["tmp"]:MXC["tmp"] + 1], obx,
                             axis=AX.X, apply_absolute_value=True)
                v.drain()
                v.tensor_tensor(self.s_st[:, MXC["o"]:MXC["o"] + 1],
                                self.s_st[:, MXC["o"]:MXC["o"] + 1],
                                self.s_st[:, MXC["tmp"]:MXC["tmp"] + 1], ALU.max)
                v.drain().then_inc(ev, 1)
            c_ev.inc()
            proc_done[i] = c_ev.v

            if i + 4 < nTile:
                evv, nt = proc_done[i], i + 4

                @blk.gpsimd
                def _(q):
                    q.wait_ge(ev, evv)
                    sl4 = nt % 4
                    chi[sl4].dma(q, a32[:, 1024 + sl4 * 512:
                                        1024 + sl4 * 512 + FW], gs_t(nt))
                    load_done[nt] = chi[sl4].dma(
                        q, a32[:, 3072 + sl4 * 512:3072 + sl4 * 512 + FW],
                        up_t(nt))

            pv = proc_done[i]
            ii = i

            @blk.gpsimd
            def _(g):
                g.wait_ge(ev, pv)
                cho[slx].dma(g, up_t(ii), obx)
            store_done[i] = cho[slx].v

    def emit_ffn_down(self):
        blk = self.blk
        c = self.cfg
        D, I, TOK = c["D"], c["I"], self.TOK
        nIc, nDc = I // 128, D // 128
        TW = 256 if TOK >= 512 else TOK
        nG = TOK // TW
        a16, a32, ai = self.a16, self.a32, self.ai
        ev, c_ev = self.S("ev"), self.C("ev")
        es, c_es = self.S("es"), self.C("es")
        et, c_et = self.S("et"), self.C("et")
        SL = self.SL
        o_t = self.upT.rearrange("(c p) n -> c p n", p=128)
        chi = [self.chan("ld_o0"), self.chan("ld_o1")]
        chs = [self.chan("st_dn0"), self.chan("st_dn1")]
        load_done = {}
        qdq_done = {}
        mm_done = {}
        prep_es = self.PREP_DONE_wd
        ch7 = self.chan("ld_fin")
        chx7 = self.chan("ld_px0")
        chout = [self.chan("st_out0"), self.chan("st_out1")]
        x_t7 = self.x.rearrange("(n p) d -> n p d", p=128)
        out_t7 = self.out.rearrange("(n p) d -> n p d", p=128)
        nJ = TW // 128
        f7 = dict(first=True, at0=self.chan("st_at0").v,
                  at1=self.chan("st_at1").v, tr={}, fin={}, ov={},
                  st={}, ld={}, xv={}, chsv={})
        self._fd_first = True

        def emit_lq(g, ics):
            gsl = g % 2
            OQ = 49152 + gsl * 12288
            mmv = mm_done.get(g - 2, 0)
            for ic in ics:
                sl = ic % 2
                ob = a32[:, sl * 512:sl * 512 + TW]
                evv = qdq_done.get((g, ic - 2),
                                   qdq_done.get((g - 1, nIc - 1), 0))
                obx, gx, icx, slx, OQx = ob, g, ic, sl, OQ
                first_l = self._fd_first
                self._fd_first = False

                @blk.sync
                def _(q):
                    if first_l:
                        self.qbar(q)
                        q.wait_ge(es, prep_es)
                    q.wait_ge(ev, evv)
                    load_done[(gx, icx)] = chi[slx].dma(
                        q, obx, o_t[icx, :, gx * TW:gx * TW + TW])

                qv = load_done[(g, ic)]
                mmvx = mmv

                @blk.vector
                def _(v):
                    chi[slx].wait(v, qv)
                    v.wait_ge(et, mmvx)
                    ki = ai[:, 0:TW]
                    qf = a32[:, 1024:1024 + TW]
                    hi = a16[:, OQx + icx * TW:OQx + icx * TW + TW]
                    lo = a16[:, OQx + nIc * TW + icx * TW:
                             OQx + nIc * TW + icx * TW + TW]
                    v.tensor_scalar(ki, obx,
                                    self.s_sc[:, SL["r_o"]:SL["r_o"] + 1],
                                    32768.0, ALU.mult, ALU.add)
                    v.drain()
                    v.tensor_scalar(hi, ki, -32768.0,
                                    self.s_sc[:, SL["s_o"]:SL["s_o"] + 1],
                                    ALU.add, ALU.mult)
                    v.drain()
                    v.tensor_scalar(qf, ki, -32768.0,
                                    self.s_sc[:, SL["s_o"]:SL["s_o"] + 1],
                                    ALU.add, ALU.mult)
                    v.drain()
                    v.tensor_tensor(lo, qf, hi, ALU.subtract)
                    v.drain().then_inc(ev, 1)
                c_ev.inc()
                qdq_done[(g, ic)] = c_ev.v

        def f7_loads(gc):
            off = gc * TW
            v0, v1 = f7["chsv"][gc]
            first_f = f7["first"]
            f7["first"] = False
            at0v, at1v = f7["at0"], f7["at1"]
            trl = f7["tr"].get(gc - 1, 0)
            stp = [f7["st"].get((gc - 1, j), 0) for j in range(2)]
            ovp = f7["ov"].get((gc - 1, nJ - 1), 0)

            @blk.sync
            def _(q):
                if first_f:
                    self.chan("st_at0").wait(q, at0v)
                    self.chan("st_at1").wait(q, at1v)
                chs[0].wait(q, v0)
                chs[1].wait(q, v1)
                q.wait_ge(et, trl)
                chout[0].wait(q, stp[0])
                chout[1].wait(q, stp[1])
                ch7.dma(q, a32[:, 3072:5120].rearrange("p (c t) -> p c t",
                                                       t=TW),
                        self.attT.rearrange("(c p) n -> p c n", p=128)[
                            :, :, off:off + TW])
                ch7.dma(q, a32[:, 5120:7168].rearrange("p (c t) -> p c t",
                                                       t=TW),
                        self.ffnT.rearrange("(c p) n -> p c n", p=128)[
                            :, :, off:off + TW])
            f7["ld"][gc] = ch7.v
            ovpx = ovp

            @blk.sync
            def _(q):
                q.wait_ge(ev, ovpx)
                for j in range(nJ):
                    chx7.dma(q, a32[:, 7168 + j * 1024:7168 + j * 1024 + 1024],
                             x_t7[off // 128 + j])
            f7["xv"][gc] = chx7.v

        def f7_fin(gc):
            ldv = f7["ld"][gc]

            @blk.vector
            def _(v):
                ch7.wait(v, ldv)
                v.tensor_tensor(a32[:, 3072:5120], a32[:, 3072:5120],
                                a32[:, 5120:7168], ALU.add)
                v.drain().then_inc(ev, 1)
            c_ev.inc()
            f7["fin"][gc] = c_ev.v

        def f7_tr(gc):
            fdx = f7["fin"][gc]
            pav = f7["ov"].get((gc - 1, nJ - 1), 0)

            @blk.tensor
            def _(t):
                t.wait_ge(ev, fdx)
                t.wait_ge(ev, pav)
                for j in range(nJ):
                    for dc in range(nDc):
                        t.transpose(
                            self.ps_a[:, j * 1024 + dc * 128:
                                      j * 1024 + dc * 128 + 128],
                            a32[:, 3072 + dc * TW + j * 128:
                                3072 + dc * TW + j * 128 + 128],
                            self.s_idf[:])
                t.drain().then_inc(et, 1)
            c_et.inc()
            f7["tr"][gc] = c_et.v

        def f7_ostg(gc, j):
            off = gc * TW
            trv = f7["tr"][gc]
            xv = f7["xv"][gc]
            stvp = f7["st"].get((gc - 1, j), 0)
            jx = j

            @blk.vector
            def _(v):
                v.wait_ge(et, trv)
                chx7.wait(v, xv)
                chout[jx].wait(v, stvp)
                v.tensor_tensor(a32[:, 5120 + jx * 1024:5120 + jx * 1024 + 1024],
                                self.ps_a[:, jx * 1024:jx * 1024 + 1024],
                                a32[:, 7168 + jx * 1024:7168 + jx * 1024 + 1024],
                                ALU.add)
                v.drain().then_inc(ev, 1)
            c_ev.inc()
            f7["ov"][(gc, j)] = c_ev.v
            ovv = c_ev.v
            tix = off // 128 + j

            @blk.gpsimd
            def _(q):
                q.wait_ge(ev, ovv)
                chout[jx].dma(q, out_t7[tix],
                              a32[:, 5120 + jx * 1024:5120 + jx * 1024 + 1024])
            f7["st"][(gc, j)] = chout[j].v

        emit_lq(0, range(nIc))
        for g in range(nG):
            gsl = g % 2
            OQ = 49152 + gsl * 12288
            if g >= 1:
                f7_loads(g - 1)
            ics_per_dc = nIc // nDc
            for dc in range(nDc):
                bank_key = "d" if dc % 2 == 0 else "e"
                ps = self.ps_d if dc % 2 == 0 else self.ps_e
                evv = qdq_done[(g, nIc - 1)]
                psx, dcx, gx, OQx = ps, dc, g, OQ

                @blk.tensor
                def _(t):
                    t.wait_ge(ev, evv)
                    self.bank_wait(t, bank_key)
                    first_mm = True
                    for ic in range(nIc):
                        for (sp, mp) in ((0, 0), (0, 1), (1, 0)):
                            rhs = a16[:, OQx + mp * nIc * TW + ic * TW:
                                      OQx + mp * nIc * TW + ic * TW + TW]
                            t.matmul(psx[:, 0:TW],
                                     self.lhsT("wd", dcx, ic, sp), rhs,
                                     start=first_mm,
                                     stop=(ic == nIc - 1 and (sp, mp) == (1, 0)))
                            first_mm = False
                    t.drain().then_inc(et, 1)
                c_et.inc()
                mm_done[g] = c_et.v
                mmv2 = c_et.v

                stg = a32[:, 2048 + (dc % 2) * 512:2048 + (dc % 2) * 512 + TW]
                stv = chs[dc % 2].v
                stgx = stg

                @blk.vector
                def _(v):
                    v.wait_ge(et, mmv2)
                    chs[dcx % 2].wait(v, stv)
                    v.tensor_copy(stgx, psx[:, 0:TW])
                    v.drain().then_inc(ev, 1)
                c_ev.inc()
                self.bank_free(bank_key, "ev")
                dv = c_ev.v

                @blk.gpsimd
                def _(g_):
                    g_.wait_ge(ev, dv)
                    chs[dcx % 2].dma(g_, self.ffnT[dcx * 128:dcx * 128 + 128,
                                                   gx * TW:gx * TW + TW],
                                     stgx)

                if g + 1 < nG:
                    emit_lq(g + 1, range(dc * ics_per_dc,
                                         (dc + 1) * ics_per_dc))
                if g >= 1:
                    if dc == 3:
                        f7_fin(g - 1)
                    elif dc == 4:
                        f7_tr(g - 1)
                    elif dc == 5:
                        f7_ostg(g - 1, 0)
                    elif dc == 6 and nJ > 1:
                        f7_ostg(g - 1, 1)
            f7["chsv"][g] = (chs[0].v, chs[1].v)

        gl = nG - 1
        f7_loads(gl)
        f7_fin(gl)
        f7_tr(gl)
        for j in range(nJ):
            f7_ostg(gl, j)

    def _p7a_state(self):
        if not hasattr(self, "_p7a"):
            self._p7a = dict(u=0, dc=0, started=False)
        return self._p7a

    def emit_combine_mm_range(self, n_dc):
        st = self._p7a_state()
        blk = self.blk
        c = self.cfg
        B, T = c["B"], c["T"]
        HD = self.HD
        nHc = HD // 128
        nDc = c["D"] // 128
        a16, a32 = self.a16, self.a32
        ev, c_ev = self.S("ev"), self.C("ev")
        et, c_et = self.S("et"), self.C("et")
        OG_HI, OG_LO = 32768, 40960
        chg = self.chan("ld_og")
        chs = [self.chan("st_at0"), self.chan("st_at1")]
        units = [(s, qgi) for s in range(B) for qgi in range(T // 512)]

        for _ in range(n_dc):
            u, dc = st["u"], st["dc"]
            if u >= len(units):
                return
            s, qgi = units[u]
            if dc == 0:
                first_l = not st["started"]
                st["started"] = True
                mm_prev = st.get("mm_done", 0)
                sx, qx = s, qgi

                @blk.sync
                def _(q):
                    if first_l:
                        self.qbar(q)
                        q.wait_ge(self.S("es"), self.PREP_DONE_wo)
                    else:
                        q.wait_ge(et, mm_prev)
                    for hl, src_, dsto in ((0, self.oT_hi, OG_HI),
                                           (1, self.oT_lo, OG_LO)):
                        chg.dma(q, a16[:, dsto:dsto + nHc * 512].rearrange(
                            "p (c t) -> p c t", t=512),
                            src_.rearrange("b (c p) t -> b p c t", p=128)[
                                sx, :, :, qx * 512:qx * 512 + 512])
                st["loads"] = chg.v

            loads = st["loads"]
            bank_key = "d" if dc % 2 == 0 else "e"
            ps = self.ps_d if dc % 2 == 0 else self.ps_e
            psx, dcx = ps, dc

            @blk.tensor
            def _(t):
                chg.wait(t, loads)
                self.bank_wait(t, bank_key)
                first_mm = True
                for hc in range(nHc):
                    for (sp, mp) in ((0, 0), (0, 1), (1, 0)):
                        rhs = a16[:, (OG_HI if mp == 0 else OG_LO) +
                                  hc * 512:
                                  (OG_HI if mp == 0 else OG_LO) +
                                  hc * 512 + 512]
                        mm = t.matmul(psx[:, 0:512],
                                      self.lhsT("wo", dcx, hc, sp), rhs,
                                      start=first_mm,
                                      stop=(hc == nHc - 1 and
                                            (sp, mp) == (1, 0)))
                        first_mm = False
                mm.then_inc(et, 1)
            c_et.inc()
            st["mm_done"] = c_et.v
            mmv = c_et.v
            stg = a32[:, (dc % 2) * 512:(dc % 2) * 512 + 512]
            stv = chs[dc % 2].v
            stgx = stg

            @blk.vector
            def _(v):
                v.wait_ge(et, mmv)
                chs[dcx % 2].wait(v, stv)
                v.tensor_copy(stgx, psx[:, 0:512])
                v.drain().then_inc(ev, 1)
            c_ev.inc()
            self.bank_free(bank_key, "ev")
            dv = c_ev.v
            off = (s * T + qgi * 512)

            @blk.sync
            def _(q):
                q.wait_ge(ev, dv)
                chs[dcx % 2].dma(q, self.attT[dcx * 128:dcx * 128 + 128,
                                              off:off + 512], stgx)

            if dc + 1 == nDc:
                st["u"], st["dc"] = u + 1, 0
            else:
                st["dc"] = dc + 1

    def emit_combine_final(self):
        blk = self.blk
        c = self.cfg
        B, T, D = c["B"], c["T"], c["D"]
        nDc = D // 128
        a32 = self.a32
        ev, c_ev = self.S("ev"), self.C("ev")
        es, c_es = self.S("es"), self.C("es")
        et, c_et = self.S("et"), self.C("et")
        x_t = self.x.rearrange("(n p) d -> n p d", p=128)
        out_t = self.out.rearrange("(n p) d -> n p d", p=128)
        chg = self.chan("ld_fin")
        chxx = [self.chan("ld_px0"), self.chan("ld_px1")]
        chout = [self.chan("st_out0"), self.chan("st_out1")]
        first = True
        grp_done = 0
        xf_free = {}
        pa_free = {}
        stv_prev = {}

        for s in range(B):
            for qgi in range(T // 512):
                sx, qx = s, qgi
                first_l = first
                first = False
                gv = grp_done

                @blk.sync
                def _(q):
                    if first_l:
                        self.qbar(q)
                    else:
                        q.wait_ge(ev, gv)
                    chg.dma(q, a32[:, 0:4096].rearrange("p (c t) -> p c t", t=512),
                            self.attT.rearrange("(c p) n -> p c n", p=128)[
                                :, :, sx * T + qx * 512:sx * T + qx * 512 + 512])
                    chg.dma(q, a32[:, 4096:8192].rearrange("p (c t) -> p c t", t=512),
                            self.ffnT.rearrange("(c p) n -> p c n", p=128)[
                                :, :, sx * T + qx * 512:sx * T + qx * 512 + 512])
                loads = chg.v

                @blk.vector
                def _(v):
                    chg.wait(v, loads)
                    v.tensor_tensor(a32[:, 0:4096], a32[:, 0:4096],
                                    a32[:, 4096:8192], ALU.add)
                    v.drain().then_inc(ev, 1)
                c_ev.inc()
                fin_done = c_ev.v

                for j in range(4):
                    ti = (s * T + qgi * 512) // 128 + j
                    jx, tix = j, ti
                    xev = xf_free.get(j % 2, 0)

                    @blk.sync
                    def _(q):
                        q.wait_ge(ev, max(xev, fin_done))
                        chxx[jx % 2].dma(q, a32[:, 4096 + (jx % 2) * 1024:
                                                4096 + (jx % 2) * 1024 + 1024],
                                         x_t[tix])
                    xload = chxx[j % 2].v

                    pav = pa_free.get(j % 2, 0)

                    @blk.tensor
                    def _(t):
                        t.wait_ge(ev, fin_done)
                        t.wait_ge(ev, pav)
                        psl = (jx % 2) * 1024
                        for dc in range(nDc):
                            t.transpose(
                                self.ps_a[:, psl + dc * 128:psl + dc * 128 + 128],
                                a32[:, dc * 512 + jx * 128:
                                    dc * 512 + jx * 128 + 128],
                                self.s_idf[:])
                        t.drain().then_inc(et, 1)
                    c_et.inc()
                    trv = c_et.v
                    stv = chout[j % 2].v

                    @blk.vector
                    def _(v):
                        v.wait_ge(et, trv)
                        chxx[jx % 2].wait(v, xload)
                        chout[jx % 2].wait(v, stv)
                        psl = (jx % 2) * 1024
                        ostg = a32[:, 6144 + (jx % 2) * 1024:
                                   6144 + (jx % 2) * 1024 + 1024]
                        v.tensor_tensor(ostg, self.ps_a[:, psl:psl + 1024],
                                        a32[:, 4096 + (jx % 2) * 1024:
                                            4096 + (jx % 2) * 1024 + 1024],
                                        ALU.add)
                        v.drain().then_inc(ev, 1)
                    c_ev.inc()
                    xf_free[j % 2] = c_ev.v
                    pa_free[j % 2] = c_ev.v
                    osv = c_ev.v

                    @blk.gpsimd
                    def _(g):
                        g.wait_ge(ev, osv)
                        chout[jx % 2].dma(
                            g, out_t[tix],
                            a32[:, 6144 + (jx % 2) * 1024:
                                6144 + (jx % 2) * 1024 + 1024])
                grp_done = c_ev.v


_CACHE = {}


def _get_nc():
    if "full" not in _CACHE:
        _CACHE["full"] = MK(CFG_FULL).build()
    return _CACHE["full"]


def kernel(L0_attn_x, L0_attn_Wq, L0_attn_Wkv_down, L0_attn_Wk_up,
           L0_attn_Wv_up, L0_attn_Wo, L0_ffn_w_gate, L0_ffn_w_up,
           L0_ffn_w_down):
    x = np.ascontiguousarray(np.asarray(L0_attn_x, dtype=np.float32))
    B_all, T, D = x.shape
    per = B_all // NCORES
    nc = _get_nc()
    idf = np.eye(128, dtype=np.float32)
    idb = np.eye(128, dtype=ml_dtypes.bfloat16)
    common = dict(
        Wq=np.ascontiguousarray(L0_attn_Wq, dtype=np.float32),
        Wkv=np.ascontiguousarray(L0_attn_Wkv_down, dtype=np.float32),
        Wku=np.ascontiguousarray(L0_attn_Wk_up, dtype=np.float32),
        Wvu=np.ascontiguousarray(L0_attn_Wv_up, dtype=np.float32),
        Wo=np.ascontiguousarray(L0_attn_Wo, dtype=np.float32),
        Wg=np.ascontiguousarray(L0_ffn_w_gate, dtype=np.float32),
        Wu=np.ascontiguousarray(L0_ffn_w_up, dtype=np.float32),
        Wd=np.ascontiguousarray(L0_ffn_w_down, dtype=np.float32),
        idf=idf, idb=idb,
    )
    in_maps = []
    for ci in range(NCORES):
        m = dict(common)
        m["x"] = np.ascontiguousarray(
            x[ci * per:(ci + 1) * per].reshape(per * T, D))
        in_maps.append(m)
    res = run_bass_kernel_spmd(nc, in_maps, core_ids=list(range(NCORES)))
    out = np.concatenate(
        [np.asarray(r["out"]).reshape(per, T, D) for r in res.results], axis=0)
    return out
